# version 01
# baseline (speedup 1.0000x reference)
import sys
import numpy as np
import ml_dtypes
import concourse.bass as bass
import concourse.mybir as mybir
from concourse.bass_utils import run_bass_kernel_spmd
from contextlib import ExitStack

F32 = mybir.dt.float32
BF16 = mybir.dt.bfloat16
AF = mybir.ActivationFunctionType
ALU = mybir.AluOpType
AX = mybir.AxisListType
BF = ml_dtypes.bfloat16

T = 4096
D = 1024
NT = T // 128
EPS_LN = 1e-5
EPS_RMS = 1e-6

ENGS = ("sync", "scalar", "vector", "tensor", "gpsimd")
PHASES = ("A1", "A2", "B", "C1", "C2", "D")


def ts(i, n=128):
    return slice(i * n, (i + 1) * n)


class Prog:

    def __init__(self, nc, stack):
        self.nc = nc
        self.ops = {e: [] for e in ENGS}
        self.sems = {}
        self.cnt = {}
        self.trace = {e: [] for e in ENGS}
        self.done = {}
        self.blocks = 0
        self.gtrace = []
        self.psums = {}
        self.bankof_cache = {}
        for s in ("dS", "dG", "dA", "pe", "act", "dve", "st"):
            self.sems[s] = stack.enter_context(nc.semaphore(s))
            self.cnt[s] = 0

    def reg_psum(self, handle, nbanks):
        base = sum(nb for _, nb in self.psums.values())
        self.psums[handle.name] = (base, nbanks)

    def banks(self, ap):
        try:
            nm = ap.tensor.name
        except Exception:
            return ()
        if nm not in self.psums:
            return ()
        base, nbanks = self.psums[nm]
        esz = 2 if "bfloat" in str(ap.dtype) else 4
        free = int(ap.tensor.shape[-1])
        off = int(ap.offset)
        col0 = off % free
        ext = 1
        for step, count in list(ap.ap):
            if int(step) < free and int(count) > 1:
                ext += int(step) * (int(count) - 1)
        b0 = (col0 * esz) // 2048
        b1 = ((col0 + ext - 1) * esz) // 2048
        return tuple(range(base + b0, base + min(b1, nbanks - 1) + 1))

    def raw(self, eng, fn):
        self.ops[eng].append(fn)

    _TL = {"sync": 0, "scalar": 1, "vector": 2, "tensor": 3, "gpsimd": 4}
    _QTL = {"sync": 5, "gpsimd": 6, "scalar": 7}

    def _vc_init(self):
        if not hasattr(self, "_vc"):
            self._vc = [ [0]*8 for _ in range(5) ]
            self._qv = [ [0]*8 for _ in range(3) ]
            self._tlpos = [0]*8
            self._semvc = {s: {} for s in self.cnt}
            self._eff = []
            self._nhaz = 0

    def note(self, eng, kind, sem=None, val=None, rd=(), wr=(), tag=""):
        self._vc_init()
        import traceback
        E = self._TL[eng]
        if kind == "wait":
            w = self._semvc[sem].get(val)
            if w is None:
                raise AssertionError(f"wait {sem}>={val} on {eng}: count never produced")
            c = self._vc[E]
            self._vc[E] = [max(a, b) for a, b in zip(c, w)]
            return
        vec = list(self._vc[E])
        self._tlpos[E] += 1
        vec[E] = self._tlpos[E]
        self._vc[E] = vec
        if kind in ("dma", "dma0"):
            qi = self._QTL[eng]
            q = qi - 5
            cvec = [max(a, b) for a, b in zip(vec, self._qv[q])]
            self._tlpos[qi] += 1
            cvec[qi] = self._tlpos[qi]
            self._qv[q] = cvec
            if sem is not None:
                self._semvc[sem][val] = cvec
            end_tl, end_idx, begin = qi, self._tlpos[qi], vec
        else:
            if sem is not None:
                self._semvc[sem][val] = vec
            end_tl, end_idx, begin = E, vec[E], vec
        if rd or wr:
            fr = sys._getframe(2)
            line = fr.f_lineno
            me = (end_tl, end_idx, begin, frozenset(rd) | frozenset(wr),
                  frozenset(wr), eng, tag, line)
            W = 800
            for prev in self._eff[-W:]:
                ptl, pidx, pbegin, pbanks, pwr, peng, ptag, pline = prev
                if not (me[3] & pbanks):
                    continue
                if not (pwr or me[4]):
                    if not ({peng, eng} == {"scalar", "vector"}):
                        continue
                if peng == eng and kind not in ("dma", "dma0"):
                    continue
                if ptl == end_tl:
                    continue
                if begin[ptl] >= pidx:
                    continue
                self._nhaz += 1
                if self._nhaz < 30:
                    print(f"PSUM-HAZARD: {ptag}@line{pline} ({peng}) banks{sorted(pbanks)} "
                          f"not ordered before {tag}@line{line} ({eng}) banks{sorted(me[3])}")
            self._eff.append(me)

    def audit_done(self):
        if getattr(self, "_nhaz", 0):
            raise AssertionError(f"{self._nhaz} PSUM hazards detected")

    def _inc(self, eng, sem, build, n):
        self.cnt[sem] += 1
        s = self.sems[sem]
        self.ops[eng].append(lambda e: build(e).then_inc(s, n))
        self.trace[eng].append(("op", sem, self.cnt[sem]))
        self.note(eng, "dma" if n == 16 else "op", sem, self.cnt[sem],
                  self._rd, self._wr, self._tag)
        return self.cnt[sem]

    def wait(self, eng, sem, val):
        if val <= 0:
            return
        s = self.sems[sem]
        mult = 16 if sem in ("dS", "dG", "dA", "st") else 1
        self.ops[eng].append(lambda e: e.wait_ge(s, mult * val))
        self.trace[eng].append(("wait", sem, val))
        self.note(eng, "wait", sem, val)

    def dma(self, queue, sem, out_ap, in_ap, inc=True):
        self._rd, self._wr, self._tag = self.banks(in_ap), self.banks(out_ap), "dma"
        if inc:
            return self._inc(queue, sem, lambda e: e.dma_start(out_ap, in_ap), 16)
        self.ops[queue].append(lambda e: e.dma_start(out_ap, in_ap))
        self.note(queue, "dma0", None, None, self._rd, self._wr, "dma0")
        return None

    def mm(self, out_ap, l_ap, r_ap, start, stop, inc=False, tp=None):
        if tp is None:
            b = lambda e: e.matmul(out_ap, l_ap, r_ap, start=start, stop=stop)
        else:
            b = lambda e: e.matmul(out_ap, l_ap, r_ap, start=start, stop=stop,
                                   tile_position=tp)
        self._rd, self._wr, self._tag = (), self.banks(out_ap), "mm"
        if inc:
            return self._inc("tensor", "pe", b, 1)
        self.ops["tensor"].append(b)
        self.note("tensor", "op", None, None, self._rd, self._wr, "mm")

    def tr(self, out_ap, in_ap, ident_ap, inc=False):
        b = lambda e: e.transpose(out_ap, in_ap, ident_ap)
        self._rd, self._wr, self._tag = (), self.banks(out_ap), "tr"
        if inc:
            return self._inc("tensor", "pe", b, 1)
        self.ops["tensor"].append(b)
        self.note("tensor", "op", None, None, self._rd, self._wr, "tr")

    def act(self, out_ap, in_ap, func, bias=0.0, scale=1.0, accum=None, inc=True):
        def b(e):
            kw = {}
            if accum is not None:
                kw["accum_out"] = accum
            return e.activation(out_ap, in_ap, func, bias=bias, scale=scale, **kw)
        self._rd, self._wr, self._tag = self.banks(in_ap), self.banks(out_ap), "act"
        if inc:
            return self._inc("scalar", "act", b, 1)
        self.ops["scalar"].append(b)
        self.note("scalar", "op", None, None, self._rd, self._wr, "act")

    def dve(self, name, *aps, inc=False, **kw):
        b = lambda e: getattr(e, name)(*aps, **kw)
        rd = []
        wr = list(self.banks(aps[0])) if aps and hasattr(aps[0], "tensor") else []
        for a in aps[1:]:
            if hasattr(a, "tensor"):
                rd += list(self.banks(a))
        self._rd, self._wr, self._tag = tuple(rd), tuple(wr), "dve:" + name
        if name.startswith("reduce") or name == "tensor_reduce":
            self.ops["vector"].append(b)
            self.note("vector", "op", None, None, self._rd, self._wr, "dve:" + name)
            so, si = self._spacer
            sb_ = lambda e: e.tensor_copy(so, si)
            self._rd, self._wr = (), ()
            if inc:
                return self._inc("vector", "dve", sb_, 1)
            self.ops["vector"].append(sb_)
            self.note("vector", "op", None, None, (), (), "dve:spacer")
            return None
        if inc:
            return self._inc("vector", "dve", b, 1)
        self.ops["vector"].append(b)
        self.note("vector", "op", None, None, self._rd, self._wr, "dve:" + name)

    def gpc(self, name, *aps, inc=False, **kw):
        b = lambda e: getattr(e, name)(*aps, **kw)
        if inc:
            return self._inc("gpsimd", "dG", b, 16)
        self.ops["gpsimd"].append(b)

    def check(self):
        tr = {e: list(self.trace[e]) for e in ENGS}
        pos = {e: 0 for e in ENGS}
        val = {s: self.done.get(s, 0) for s in self.cnt}
        tot = dict(val)
        for e in ENGS:
            for it in tr[e]:
                if it[0] == "op" and it[1] is not None:
                    tot[it[1]] = max(tot[it[1]], it[2])
        for e in ENGS:
            for it in tr[e]:
                if it[0] == "wait" and it[2] > tot[it[1]]:
                    raise AssertionError(f"wait {it} on {e} exceeds total {tot[it[1]]}")
        progressed = True
        while progressed:
            progressed = False
            for e in ENGS:
                while pos[e] < len(tr[e]):
                    it = tr[e][pos[e]]
                    if it[0] == "wait" and val[it[1]] < it[2]:
                        break
                    if it[0] == "op" and it[1] is not None:
                        val[it[1]] = max(val[it[1]], it[2])
                    pos[e] += 1
                    progressed = True
        stuck = {e: (pos[e], len(tr[e]), tr[e][pos[e]] if pos[e] < len(tr[e]) else None)
                 for e in ENGS if pos[e] < len(tr[e])}
        if stuck:
            raise AssertionError(f"DEADLOCK: {stuck} sem_vals={val}")
        self.done = val

    def flush(self):
        self.check()
        self._vc_init()
        bar = [max(c[i] for c in self._vc) for i in range(8)]
        for e in range(5):
            self._vc[e] = [max(a, b) for a, b in zip(self._vc[e], bar)]
        self.trace = {e: [] for e in ENGS}
        ops = self.ops
        self.ops = {e: [] for e in ENGS}
        with self.nc.Block() as blk:
            for eng in ENGS:
                lst = ops[eng]
                if lst:
                    def runner(e, lst=lst):
                        for f in lst:
                            f(e)
                    getattr(blk, eng)(runner)


def build_program():
    import os as _os
    nc = bass.Bass("TRN2", target_bir_lowering=False, debug=False)

    dI = {}
    def din(name, shape, dtype=F32):
        dI[name] = nc.dram_tensor(name, list(shape), dtype, kind="ExternalInput").ap()
        return dI[name]

    x_in = din("x", (T, D))
    y_out = nc.dram_tensor("y", [T, D], F32, kind="ExternalOutput").ap()
    bf_shapes = {"w1": (32, D, 128), "w2": (32, D, 128), "h1": (32, 4096, 128),
                 "h2": (32, 4096, 128), "w3": (8, 4096, 128)}
    bf_dram = {nm: nc.dram_tensor("bf_" + nm, list(sh), BF16, kind="Internal").ap()
               for nm, sh in bf_shapes.items()}
    dI_bf = {nm: [bf_dram[nm][oc].rearrange("(ic p) j -> p ic j", p=128)
                  for oc in range(bf_shapes[nm][0])] for nm in bf_shapes}
    dbg_out = nc.dram_tensor("dbg", [128, 4096], F32, kind="ExternalOutput").ap()
    _dbg = _os.environ.get("DEV_DBG", "")
    R = nc.dram_tensor("Rres", [T, D], F32, kind="Internal").ap()
    S2 = nc.dram_tensor("S2res", [T, D], F32, kind="Internal").ap()

    din("ident", (128, 128), BF16)
    din("mask_c", (128, 128), BF16)
    din("mask_a", (128, 128), BF16)
    din("qgc", (128, 512))
    din("kgc", (128, 512))
    din("ropeCg", (T, 64), BF16)
    din("ropeSg", (T, 64), BF16)
    din("ropeCl", (T, 64), BF16)
    din("ropeSl", (T, 64), BF16)
    din("l0_wq", (D, D))
    din("l0_wkv", (D, 128))
    din("l0_wk", (128, 512))
    din("l0_wv", (128, 512))
    din("l0_wo", (512, D))
    din("l0f_wT", (D, D))
    din("l0f_b", (1, D))
    din("l0f_ldT", (100, 4))
    din("l0f_luT", (4, 50))
    din("l0f_da", (100, D))
    din("l0f_uaT", (50, D))
    din("l1_n1", (128, 8))
    din("l1_n2", (128, 8))
    din("l1_gwq", (D, D))
    din("l1_gkvd", (D, 256))
    din("l1_gkup", (256, 256))
    din("l1_gvup", (256, 256))
    din("l1_gwo", (D, D))
    din("l1_lwq", (D, D))
    din("l1_lwk", (D, 256))
    din("l1_lwv", (D, 256))
    din("l1_lwo", (D, D))
    import os as _os
    if not _os.environ.get("DEV_NO_BIGW"):
        din("l1f_w1", (32, D, 128))
        din("l1f_w2", (32, D, 128))
        din("l1f_h1", (32, 4096, 128))
        din("l1f_h2", (32, 4096, 128))
        din("l1f_w3", (8, 4096, 128))
    din("l1f_s1", (128, 32))
    din("l1f_s2", (128, 32))
    din("l1f_sh1", (128, 32))
    din("l1f_sh2", (128, 32))
    din("l1f_s3", (128, 8))

    _ = _os
    st = ExitStack()
    with st:
        P = Prog(nc, st)

        def sb(name, shape, dtype=F32):
            return st.enter_context(nc.sbuf_tensor(name, list(shape), dtype))

        ident = sb("ident_s", (128, 128), BF16)
        mask_c = sb("mask_c_s", (128, 128), BF16)
        mask_a = sb("mask_a_s", (128, 128), BF16)
        ones_r = sb("ones_r", (1, 64))
        xt = [sb(f"xt{i}", (128, D)) for i in range(2)]
        x2t = [sb(f"x2t{i}", (128, D)) for i in range(2)]
        lnt = [sb(f"lnt{i}", (128, D), BF16) for i in range(2)]
        lnT = [sb(f"lnT{i}", (128, D), BF16) for i in range(2)]
        trash = sb("trash", (128, D))
        cols = sb("cols", (128, 64))
        colh = sb("colh", (128, 64))
        colh_ln = sb("colh_ln", (128, 16))
        colh_rs = sb("colh_rs", (128, 16))
        spacer = sb("spacer", (128, 16))
        P._spacer = (spacer[:, 0:8], spacer[:, 8:16])
        negC = sb("negC", (128, 1))
        zcol = sb("zcol", (128, 1))
        rsum = sb("rsum", (1, 512))
        bcast = sb("bcast", (64, 512))
        pbuf = [sb(f"pbuf{i}", (128, 1024), BF16) for i in range(2)]

        cvals = [EPS_LN, EPS_RMS, -0.5, 0.125]
        cstt = sb("cstt", (128, len(cvals)))
        if not _os.environ.get("DEV_NO_CST"):
            for i, v in enumerate(cvals):
                P.dve("memset", cstt[:, i:i + 1], float(v))
                nc.const_aps.register(F32, float(v), cstt[:, i:i + 1])

        PS_A = st.enter_context(nc.psum_tensor("PS_A", (128, 1024), F32))
        PS_B = st.enter_context(nc.psum_tensor("PS_B", (128, 1024), F32))
        PS_C = st.enter_context(nc.psum_tensor("PS_C", (128, 512), F32))
        PS_D = st.enter_context(nc.psum_tensor("PS_D", (128, 512), F32))
        PS_TR = st.enter_context(nc.psum_tensor("PS_TR", (128, 1024), BF16))
        PS_TR2 = st.enter_context(nc.psum_tensor("PS_TR2", (128, 1024), BF16))
        P.reg_psum(PS_A, 2); P.reg_psum(PS_B, 2); P.reg_psum(PS_C, 1)
        P.reg_psum(PS_D, 1); P.reg_psum(PS_TR, 1); P.reg_psum(PS_TR2, 1)

        dbgbuf = sb("dbgbuf", (128, 1024)) if _os.environ.get("DEV_DBG") else None

        def dbg(src_ap, width, part=128):
            for o in range(0, width, 1024):
                w = min(1024, width - o)
                cpy = P.dve("tensor_copy", dbgbuf[0:part, 0:w], src_ap[:, o:o + w], inc=True)
                P.wait("sync", "dve", cpy)
                cs_ = P.dma("sync", "st", dbg_out[0:part, o:o + w], dbgbuf[0:part, 0:w])
                P.wait("vector", "st", cs_)

        def ln_stats(src_ap, with_mean, eps, F, psum=False):
            sq = cols[:, 4:5]
            ca = P.act(trash[:, :F], src_ap, AF.Square, accum=sq)
            if with_mean:
                if psum:
                    ca = P.act(trash[:, :F], src_ap, AF.Copy, accum=cols[:, 0:1])
                else:
                    P.dve("reduce_sum", cols[:, 0:1], src_ap, axis=AX.X)
                P.wait("vector", "act", ca)
                P.dve("tensor_scalar_mul", cols[:, 0:1], cols[:, 0:1], 1.0 / F)
                P.dve("tensor_scalar_mul", cols[:, 1:2], sq, 1.0 / F)
                P.dve("tensor_mul", cols[:, 5:6], cols[:, 0:1], cols[:, 0:1])
                P.dve("tensor_sub", cols[:, 1:2], cols[:, 1:2], cols[:, 5:6])
                cv = P.dve("tensor_scalar_max", cols[:, 1:2], cols[:, 1:2], 0.0, inc=True)
            else:
                P.wait("vector", "act", ca)
                cv = P.dve("tensor_scalar_mul", cols[:, 1:2], sq, 1.0 / F, inc=True)
            P.wait("scalar", "dve", cv)
            P.act(cols[:, 6:7], cols[:, 1:2], AF.Ln, bias=float(eps), inc=False)
            return P.act(cols[:, 2:3], cols[:, 6:7], AF.Exp, scale=-0.5)

        def headnorm(src_ap, nh, hd, eps=EPS_RMS, tap=False):
            ca = P.act(trash[:, 0:nh * hd], src_ap, AF.Square)
            P.wait("vector", "act", ca)
            if tap:
                dbg(trash[:, 0:nh * hd], nh * hd)
            P.dve("reduce_sum", colh[:, 0:nh],
                  trash[:, 0:nh * hd].rearrange("p (h d) -> p h d", h=nh), axis=AX.X)
            cv = P.dve("tensor_scalar_mul", colh[:, 0:nh], colh[:, 0:nh], 1.0 / hd, inc=True)
            P.wait("scalar", "dve", cv)
            P.act(colh_ln[:, 0:nh], colh[:, 0:nh], AF.Ln, bias=float(eps), inc=False)
            return P.act(colh_rs[:, 0:nh], colh_ln[:, 0:nh], AF.Exp, scale=-0.5)

        with ExitStack() as stA:
            def sbA(name, shape, dtype=BF16):
                return stA.enter_context(nc.sbuf_tensor(name, list(shape), dtype))

            wq = sbA("wq", (128, 8 * 1024))
            wkv = sbA("wkv", (128, 8 * 128))
            wk = sbA("wk", (128, 512))
            wv = sbA("wv", (128, 512))
            wo8 = [sbA(f"wo8_{p}", (64, 1024)) for p in range(8)]
            qgc_s = sbA("qgc_s", (128, 512), F32)
            kgc_s = sbA("kgc_s", (128, 512), F32)
            qn = sbA("qn", (128, 512))
            kn = sbA("kn", (128, 512))
            qT = [sbA(f"qT{p}", (128, T)) for p in range(4)]
            kT = [sbA(f"kT{p}", (128, T)) for p in range(4)]
            VA = sbA("VA", (128, NT * 528))
            OT = [sbA(f"OT{p}", (64, 512)) for p in range(8)]
            kvb = sbA("kvb", (128, 128))
            kvT = sbA("kvT", (128, 128))

            _pre = int(_os.environ.get("DEV_PRE", "63"))
            cdc = 0
            if _pre & 1:
                P.dma("sync", "dS", ident[:], dI["ident"][:])
                P.dma("sync", "dS", mask_c[:], dI["mask_c"][:])
                P.dma("sync", "dS", mask_a[:], dI["mask_a"][:])
                P.dma("sync", "dS", qgc_s[:], dI["qgc"][:])
                cdc = P.dma("sync", "dS", kgc_s[:], dI["kgc"][:])
            if _pre & 16:
                P.dve("memset", ones_r[:], 1.0)
            if _pre & 2:
                for ic in range(8):
                    P.dma("gpsimd", "dG", wq[:, ic * 1024:(ic + 1) * 1024], dI["l0_wq"][ts(ic), :])
                    P.dma("gpsimd", "dG", wkv[:, ts(ic)], dI["l0_wkv"][ts(ic), :])
                P.dma("gpsimd", "dG", wk[:], dI["l0_wk"][:])
                P.dma("gpsimd", "dG", wv[:], dI["l0_wv"][:])
                for p in range(8):
                    P.dma("gpsimd", "dG", wo8[p][:], dI["l0_wo"][p * 64:(p + 1) * 64, :])
            dG_w = P.cnt["dG"]
            if "D" in PHASES:
                for nm in ("w1", "w2"):
                    for c in range(8):
                        P.dma("gpsimd", "dG", bf_dram[nm][c * 4:(c + 1) * 4],
                              dI["l1f_" + nm][c * 4:(c + 1) * 4])
                for c in range(8):
                    P.dma("gpsimd", "dG", bf_dram["w3"][c:c + 1],
                          dI["l1f_w3"][c:c + 1])
                for nm in ("h1", "h2"):
                    for c in range(16):
                        P.dma("gpsimd", "dG", bf_dram[nm][c * 2:(c + 1) * 2],
                              dI["l1f_" + nm][c * 2:(c + 1) * 2])

            if _pre & 4:
                P.wait("vector", "dS", cdc)
                P.dve("reduce_max", cols[:, 8:9], qgc_s[:], axis=AX.X, apply_absolute_value=True)
                P.dve("reduce_max", cols[:, 9:10], kgc_s[:], axis=AX.X, apply_absolute_value=True)
                P.dve("tensor_mul", cols[:, 8:9], cols[:, 8:9], cols[:, 9:10])
                P.dve("tensor_scalar_mul", negC[:], cols[:, 8:9], -5.6569)
            if _pre & 8:
                P.dve("memset", bass.AP(VA, 32, [[NT * 528, 128], [528, NT], [33, 16], [1, 1]]), 1.0)
            if _pre & 32:
                P.dve("memset", zcol[:], 0.0, inc=True)
            P.flush()

            prev = {"cln": 0, "dve_end": 0, "act_end": 0}
            cln_hist = []
            for t in (range(NT) if "A1" in PHASES else ()):
                pb = t % 2
                if t >= 2:
                    P.wait("sync", "dve", cln_hist[t - 2])
                cD = P.dma("sync", "dS", xt[pb][:], x_in[ts(t), :])
                P.wait("scalar", "dve", prev["dve_end"])
                P.wait("scalar", "dS", cD)
                P.wait("vector", "dS", cD)
                ca = ln_stats(xt[pb][:], True, EPS_LN, D)
                P.wait("vector", "act", ca)
                cln = P.dve("tensor_scalar", lnt[pb][:], xt[pb][:], cols[:, 0:1],
                            cols[:, 2:3], ALU.subtract, ALU.mult, inc=True)
                if t == 0 and _dbg == "ln1":
                    P.wait("vector", "dve", cln)
                    dbg(lnt[pb][:], D)
                P.wait("tensor", "dve", cln)
                P.wait("tensor", "dG", dG_w)
                for c in range(8):
                    P.tr(PS_TR[:, ts(c)], lnt[pb][:, ts(c)], ident[:], inc=(c == 7))
                cpe = P.cnt["pe"]
                P.wait("vector", "pe", cpe)
                clnT = P.dve("tensor_copy", lnT[pb][:], PS_TR[:], inc=True)
                P.wait("tensor", "dve", clnT)
                for nb in range(2):
                    for ic in range(8):
                        P.mm(PS_A[:, nb * 512:(nb + 1) * 512], lnT[pb][:, ts(ic)],
                             wq[:, ic * 1024 + nb * 512: ic * 1024 + (nb + 1) * 512],
                             start=(ic == 0), stop=(ic == 7),
                             inc=(nb == 1 and ic == 7))
                cQ = P.cnt["pe"]
                for ic in range(8):
                    P.mm(PS_B[:, 0:128], lnT[pb][:, ts(ic)], wkv[:, ts(ic)],
                         start=(ic == 0), stop=(ic == 7), inc=(ic == 7))
                cKV = P.cnt["pe"]
                if t == 0 and _dbg == "q0":
                    P.wait("vector", "pe", cQ)
                    dbg(PS_A[:], D)
                    P.wait("scalar", "dve", P.cnt["dve"])
                P.wait("scalar", "pe", cQ)
                cqs = headnorm(PS_A[:], 16, 64, tap=(t == 0 and _dbg == "qsq2"))
                if t == 0 and _dbg in ("qsq", "qrs"):
                    P.wait("vector", "act", cqs)
                    if _dbg == "qsq":
                        dbg(trash[:], D)
                    else:
                        dbg(colh[:, 0:16], 16)
                        P.dve("tensor_copy", dbgbuf[:, 16:32], colh[:, 16:32])
                        P.dve("tensor_copy", dbgbuf[:, 32:48], colh_rs[:, 0:16], inc=True)
                        P.dma("sync", "st", dbg_out[:, 16:48], dbgbuf[:, 16:48])
                    P.wait("scalar", "dve", P.cnt["dve"])
                P.wait("vector", "act", cqs)
                P.dve("tensor_tensor", trash[:, 0:512].rearrange("p (h d) -> p h d", h=16),
                      bass.AP(PS_A, 0, [[1024, 128], [64, 16], [1, 32]]),
                      colh_rs[:, 0:16].unsqueeze(2).broadcast_to([128, 16, 32]), ALU.mult)
                cqn = P.dve("tensor_mul", qn[:], trash[:, 0:512], qgc_s[:], inc=True)
                if t == 0 and _dbg == "qn":
                    dbg(qn[:], 512)
                P.wait("tensor", "dve", cqn)
                for c in range(4):
                    P.tr(PS_TR2[:, ts(c)], qn[:, ts(c)], ident[:], inc=(c == 3))
                cpq = P.cnt["pe"]
                P.wait("vector", "pe", cpq)
                for c in range(4):
                    P.dve("tensor_copy", qT[c][:, ts(t)], PS_TR2[:, ts(c)], inc=(c == 3))
                cqT = P.cnt["dve"]
                P.wait("vector", "pe", cKV)
                ckvb = P.dve("tensor_copy", kvb[:], PS_B[:, 0:128], inc=True)
                P.wait("tensor", "dve", ckvb)
                ckvTp = P.tr(PS_TR[:, 0:128], kvb[:], ident[:], inc=True)
                P.wait("vector", "pe", ckvTp)
                ckvT = P.dve("tensor_copy", kvT[:], PS_TR[:, 0:128], inc=True)
                P.wait("tensor", "dve", ckvT)
                cK = P.mm(PS_B[:, 512:1024], kvT[:], wk[:], start=True, stop=True, inc=True)
                P.wait("scalar", "pe", cK)
                cks = headnorm(PS_B[:, 512:1024], 16, 32)
                P.wait("vector", "act", cks)
                P.dve("tensor_tensor", trash[:, 0:512].rearrange("p (h d) -> p h d", h=16),
                      PS_B[:, 512:1024].rearrange("p (h d) -> p h d", h=16),
                      colh_rs[:, 0:16].unsqueeze(2).broadcast_to([128, 16, 32]), ALU.mult)
                ckn = P.dve("tensor_mul", kn[:], trash[:, 0:512], kgc_s[:], inc=True)
                if t == 0 and _dbg == "kn":
                    dbg(kn[:], 512)
                P.wait("tensor", "dve", ckn)
                for c in range(4):
                    P.tr(PS_TR2[:, ts(c)], kn[:, ts(c)], ident[:], inc=(c == 3))
                cpk = P.cnt["pe"]
                P.wait("vector", "pe", cpk)
                for c in range(4):
                    P.dve("tensor_copy", kT[c][:, ts(t)], PS_TR2[:, ts(c)], inc=(c == 3))
                P.wait("tensor", "dve", cqn)
                cV = P.mm(PS_A[:, 0:512], kvT[:], wv[:], start=True, stop=True, inc=True)
                P.wait("vector", "pe", cV)
                cVe = P.dve("tensor_copy",
                            bass.AP(VA, t * 528, [[NT * 528, 128], [33, 16], [1, 32]]),
                            PS_A[:, 0:512].rearrange("p (h d) -> p h d", h=16), inc=True)
                if t == 0 and _dbg == "va":
                    dbg(VA[:, 0:528], 528)
                if t == 0 and _dbg == "qT0":
                    dbg(qT[0][:, 0:512], 512)
                if t == 0 and _dbg == "kT0":
                    dbg(kT[0][:, 0:512], 512)
                cln_hist.append(cln)
                prev = {"cln": cln, "dve_end": P.cnt["dve"], "act_end": P.cnt["act"]}
            P.flush()

            A2L = int(_os.environ.get("DEV_A2", "99"))
            st_hist = []
            for s in (range(8) if "A2" in PHASES else ()):
                q0 = s * 512
                for h in range(16):
                    pT = h // 4
                    off = 32 * (h % 4)
                    tp = (96, 0) if off == 96 else None
                    kts = list(range(0, 4 * s + 4))
                    klast = kts[-1]
                    ce_hist = {}
                    for gi in range(0, len(kts), 2):
                        pair = kts[gi:gi + 2]
                        psb = PS_A if (gi // 2) % 2 == 0 else PS_B
                        pbv = pbuf[(gi // 2) % 2]
                        if gi - 4 >= 0:
                            P.wait("tensor", "act", ce_hist[gi - 4])
                        for j, kt in enumerate(pair):
                            qlo = max(0, kt * 128 - q0)
                            P.mm(psb[:, j * 512 + qlo:(j + 1) * 512],
                                 kT[pT][off:off + 32, ts(kt)],
                                 qT[pT][off:off + 32, q0 + qlo:q0 + 512],
                                 start=True, stop=True, inc=(j == len(pair) - 1), tp=tp)
                        cs = P.cnt["pe"]
                        P.wait("scalar", "pe", cs)
                        ce = P.act(pbv[:, 0:1024], psb[:, 0:1024], AF.Exp,
                                   bias=negC[:], scale=0.125)
                        ce_hist[gi] = ce
                        if s == 0 and h == 0 and gi == 0 and _dbg == "sc00":
                            P.wait("vector", "act", ce)
                            dbg(psb[:, 0:1024], 1024)
                        need_mask = [kt for kt in pair if q0 <= kt * 128 < q0 + 512] \
                            if A2L >= 2 else []
                        cm = 0
                        if need_mask:
                            P.wait("vector", "act", ce)
                            for kt in need_mask:
                                j = pair.index(kt)
                                qrel = kt * 128 - q0
                                sl = slice(j * 512 + qrel, j * 512 + qrel + 128)
                                cm = P.dve("tensor_mul", pbv[:, sl], pbv[:, sl],
                                           mask_c[:], inc=True)
                            P.wait("tensor", "dve", cm)
                        else:
                            P.wait("tensor", "act", ce)
                        if A2L < 3:
                            continue
                        if s == 0 and h == 0 and gi == 0 and _dbg == "pb00":
                            P.wait("vector", "act", ce)
                            if cm:
                                pass
                            dbg(pbv[:, 0:1024], 1024)
                        for j, kt in enumerate(pair):
                            qlo = max(0, kt * 128 - q0)
                            P.mm(PS_C[0:33, qlo:512],
                                 bass.AP(VA, kt * 528 + h * 33, [[NT * 528, 128], [1, 33]]),
                                 pbv[:, j * 512 + qlo:(j + 1) * 512],
                                 start=(kt == 0), stop=(kt == klast),
                                 inc=(j == len(pair) - 1))
                        cpv = P.cnt["pe"]
                    if s == 0 and h == 0 and _dbg == "psc":
                        P.wait("vector", "pe", cpv)
                        dbg(PS_C[0:33, 0:512], 512, part=33)
                    if A2L < 4:
                        continue
                    oOT, oOff = OT[h // 2], 32 * (h % 2)
                    P.wait("vector", "pe", cpv)
                    crs = P.dve("reciprocal", rsum[:, 0:512], PS_C[32:33, 0:512], inc=True)
                    P.wait("tensor", "dve", crs)
                    cbc = P.mm(PS_D[0:32, 0:512], ones_r[:, 0:32], rsum[:, 0:512],
                               start=True, stop=True, inc=True)
                    P.wait("vector", "pe", cbc)
                    P.dve("tensor_copy", bcast[0:32, 0:512], PS_D[0:32, 0:512])
                    cot = P.dve("tensor_mul", oOT[oOff:oOff + 32, 0:512],
                                PS_C[0:32, 0:512], bcast[0:32, 0:512], inc=True)
                    P.wait("tensor", "dve", cot)
                for qt in range(4 if A2L >= 5 else 0):
                    for nb in range(2):
                        for p in range(8):
                            P.mm(PS_A[:, nb * 512:(nb + 1) * 512], OT[p][:, ts(qt)],
                                 wo8[p][:, nb * 512:(nb + 1) * 512],
                                 start=(p == 0), stop=(p == 7),
                                 inc=(p == 7 and nb == 1))
                    if s == 0 and qt == 0 and _dbg == "wo":
                        P.wait("vector", "pe", P.cnt["pe"])
                        dbg(PS_A[:], D)
                    cwo = P.cnt["pe"]
                    t_glob = s * 4 + qt
                    pb = t_glob % 2
                    if len(st_hist) >= 2:
                        P.wait("sync", "st", st_hist[-2])
                    cD = P.dma("sync", "dS", xt[pb][:], x_in[ts(t_glob), :])
                    if A2L >= 6:
                        P.wait("scalar", "pe", cwo)
                        P.wait("vector", "pe", cwo)
                        ca = ln_stats(PS_A[:], True, EPS_LN, D, psum=True)
                        P.wait("vector", "act", ca)
                        P.wait("vector", "dS", cD)
                        if len(st_hist) >= 2:
                            P.wait("vector", "st", st_hist[-2])
                        P.dve("tensor_scalar", trash[:], PS_A[:], cols[:, 0:1],
                              cols[:, 2:3], ALU.subtract, ALU.mult)
                        cx1 = P.dve("tensor_add", xt[pb][:], xt[pb][:], trash[:], inc=True)
                    else:
                        P.wait("vector", "pe", cwo)
                        P.wait("vector", "dS", cD)
                        if len(st_hist) >= 2:
                            P.wait("vector", "st", st_hist[-2])
                        P.dve("tensor_copy", trash[:], PS_A[:])
                        cx1 = P.dve("tensor_copy", xt[pb][:], xt[pb][:], inc=True)
                    P.wait("tensor", "dve", cx1)
                    P.wait("sync", "dve", cx1)
                    cst = P.dma("sync", "st", R[ts(t_glob), :], xt[pb][:])
                    st_hist.append(cst)
            if "A2" in PHASES and A2L < 5:
                for t_glob in range(NT):
                    pb = t_glob % 2
                    if len(st_hist) >= 2:
                        P.wait("sync", "st", st_hist[-2])
                    cD = P.dma("sync", "dS", xt[pb][:], x_in[ts(t_glob), :])
                    P.wait("sync", "dS", cD)
                    cst = P.dma("sync", "st", R[ts(t_glob), :], xt[pb][:])
                    st_hist.append(cst)
            P.flush()


        if "B" in PHASES:
          with ExitStack() as stB:
            def sbB(name, shape, dtype=BF16):
                return stB.enter_context(nc.sbuf_tensor(name, list(shape), dtype))
            wc = sbB("wc", (128, 8 * 1024))
            ldT_b = sbB("ldT_b", (100, 4))
            luT_b = sbB("luT_b", (4, 50))
            da_b = sbB("da_b", (100, 1024))
            uaT_b = sbB("uaT_b", (50, 1024))
            A_sb = sbB("A_sb", (4, 1024))
            B_sb = sbB("B_sb", (50, 1024))
            b_sb = sbB("b_sb", (1, 1024), F32)
            ones1 = sbB("ones1", (1, 128), F32)
            wstg = [sbB(f"bstg{i}", (128, 1024), F32) for i in range(2)]

            P.dve("memset", ones1[:], 1.0)
            small = [(ldT_b, "l0f_ldT"), (luT_b, "l0f_luT"),
                     (da_b, "l0f_da"), (uaT_b, "l0f_uaT")]
            for dst, nm in small:
                cL = P.dma("sync", "dS", wstg[0][0:dst.shape[0], 0:dst.shape[1]], dI[nm][:])
                P.wait("vector", "dS", cL)
                P.dve("tensor_copy", dst[:], wstg[0][0:dst.shape[0], 0:dst.shape[1]], inc=True)
                P.wait("sync", "dve", P.cnt["dve"])
            cB1 = P.dma("sync", "dS", b_sb[:], dI["l0f_b"][:])
            lastc = [0, 0]
            for ic in range(8):
                if lastc[ic % 2]:
                    P.wait("sync", "dve", lastc[ic % 2])
                cL = P.dma("sync", "dS", wstg[ic % 2][:], dI["l0f_wT"][ts(ic), :])
                P.wait("vector", "dS", cL)
                lastc[ic % 2] = P.dve("tensor_copy", wc[:, ic * 1024:(ic + 1) * 1024],
                                      wstg[ic % 2][:], inc=True)
            cwcast = P.cnt["dve"]
            P.wait("tensor", "dve", cwcast)
            P.mm(PS_A[0:4, 0:512], ldT_b[:], da_b[:, 0:512], start=True, stop=True)
            cA = P.mm(PS_A[0:4, 512:1024], ldT_b[:], da_b[:, 512:1024], start=True,
                      stop=True, inc=True)
            P.wait("vector", "pe", cA)
            cAe = P.dve("tensor_copy", A_sb[:], PS_A[0:4, 0:1024], inc=True)
            P.wait("tensor", "dve", cAe)
            P.mm(PS_B[0:50, 0:512], luT_b[:], A_sb[:, 0:512], start=True, stop=True)
            cBm = P.mm(PS_B[0:50, 512:1024], luT_b[:], A_sb[:, 512:1024], start=True,
                       stop=True, inc=True)
            P.wait("vector", "pe", cBm)
            cBe = P.dve("tensor_copy", B_sb[:], PS_B[0:50, 0:1024], inc=True)
            P.wait("tensor", "dve", cBe)
            for rc in range(8):
                P.mm(PS_A[:, 0:512], B_sb[:, ts(rc)], uaT_b[:, 0:512],
                     start=True, stop=True)
                cL2 = P.mm(PS_A[:, 512:1024], B_sb[:, ts(rc)], uaT_b[:, 512:1024],
                           start=True, stop=True, inc=True)
                P.wait("vector", "pe", cL2)
                cadd = P.dve("tensor_add", wc[:, rc * 1024:(rc + 1) * 1024],
                             PS_A[:, 0:1024], wc[:, rc * 1024:(rc + 1) * 1024], inc=True)
                P.wait("tensor", "dve", cadd)
            P.flush()

            prevB = {"dve_end": 0}
            st_hist_b = []
            cln_hist_b = []
            for t in range(NT):
                pb = t % 2
                if t >= 2:
                    P.wait("sync", "dve", cln_hist_b[t - 2])
                    P.wait("sync", "st", st_hist_b[t - 2])
                cD = P.dma("sync", "dS", xt[pb][:], R[ts(t), :])
                P.wait("scalar", "dve", prevB["dve_end"])
                P.wait("scalar", "dS", cD)
                P.wait("vector", "dS", cD)
                ca = ln_stats(xt[pb][:], True, EPS_LN, D)
                P.wait("vector", "act", ca)
                cln = P.dve("tensor_scalar", lnt[pb][:], xt[pb][:], cols[:, 0:1],
                            cols[:, 2:3], ALU.subtract, ALU.mult, inc=True)
                P.wait("tensor", "dve", cln)
                for c in range(8):
                    P.tr(PS_TR[:, ts(c)], lnt[pb][:, ts(c)], ident[:], inc=(c == 7))
                cpe = P.cnt["pe"]
                P.wait("vector", "pe", cpe)
                clnT = P.dve("tensor_copy", lnT[pb][:], PS_TR[:], inc=True)
                P.wait("tensor", "dve", clnT)
                P.wait("tensor", "dS", cB1)
                for nb in range(2):
                    for ic in range(8):
                        P.mm(PS_A[:, nb * 512:(nb + 1) * 512], lnT[pb][:, ts(ic)],
                             wc[:, ic * 1024 + nb * 512: ic * 1024 + (nb + 1) * 512],
                             start=(ic == 0), stop=False)
                    P.mm(PS_A[:, nb * 512:(nb + 1) * 512], ones1[:],
                         b_sb[:, nb * 512:(nb + 1) * 512],
                         start=False, stop=True, inc=(nb == 1))
                cff = P.cnt["pe"]
                P.wait("scalar", "pe", cff)
                P.wait("vector", "pe", cff)
                ca2 = ln_stats(PS_A[:], True, EPS_LN, D, psum=True)
                P.wait("vector", "act", ca2)
                P.dve("tensor_scalar", trash[:], PS_A[:], cols[:, 0:1],
                      cols[:, 2:3], ALU.subtract, ALU.mult)
                cx2 = P.dve("tensor_add", xt[pb][:], xt[pb][:], trash[:], inc=True)
                P.wait("tensor", "dve", cx2)
                P.wait("sync", "dve", cx2)
                cst = P.dma("sync", "st", R[ts(t), :], xt[pb][:])
                st_hist_b.append(cst)
                cln_hist_b.append(cln)
                prevB = {"dve_end": P.cnt["dve"]}
            P.flush()

        def phase_C(glob):
          with ExitStack() as stC:
            pre = "g" if glob else "l"
            def sbC(name, shape, dtype=BF16):
                return stC.enter_context(nc.sbuf_tensor(pre + name, list(shape), dtype))
            wq_s = sbC("wq", (128, 8 * 1024))
            wo_s = sbC("wo", (128, 8 * 1024))
            if glob:
                kvd_s = sbC("kvd", (128, 8 * 256))
                kup_s = sbC("kup", (128, 2 * 256))
                vup_s = sbC("vup", (128, 2 * 256))
            else:
                wk_s = sbC("wk", (128, 8 * 256))
                wv_s = sbC("wv", (128, 8 * 256))
            rCt = sbC("rC", (128, NT * 64))
            rSt = sbC("rS", (128, NT * 64))
            qrbuf = sbC("qrbuf", (128, 1024), F32)
            qn1 = sbC("qn1", (128, 1024))
            kn1 = sbC("kn1", (128, 256))
            latb = sbC("latb", (128, 256))
            latT = sbC("latT", (128, 256))
            qTl = [sbC(f"qT{p}", (128, T)) for p in range(8)]
            kTl = [sbC(f"kT{p}", (128, T)) for p in range(2)]
            VAl = sbC("VA", (128, NT * 260))
            OTl = [sbC(f"OT{p}", (128, 512)) for p in range(8)]
            ncol = sbC("ncol", (128, 8), F32)
            stW = ExitStack()
            wstg = [stW.enter_context(nc.sbuf_tensor(pre + "stg0", [128, 1024], F32))] * 2

            P.dve("memset", bass.AP(VAl, 64, [[NT * 260, 128], [260, NT], [65, 4], [1, 1]]), 1.0)
            cnc = P.dma("sync", "dS", ncol[:], dI["l1_n1" if glob else "l1_n2"][:])
            tCd = dI["ropeCg" if glob else "ropeCl"]
            tSd = dI["ropeSg" if glob else "ropeSl"]
            P.dma("sync", "dS", rCt[:].rearrange("p (t j) -> p t j", j=64),
                  tCd.rearrange("(t p) j -> p t j", p=128))
            crt = P.dma("sync", "dS", rSt[:].rearrange("p (t j) -> p t j", j=64),
                        tSd.rearrange("(t p) j -> p t j", p=128))
            wqsrc = dI["l1_gwq" if glob else "l1_lwq"]
            wosrc = dI["l1_gwo" if glob else "l1_lwo"]
            loads = []
            for ic in range(8):
                loads.append((wq_s[:, ic * 1024:(ic + 1) * 1024], wqsrc[ts(ic), :], ic, 1024))
            if glob:
                for ic in range(8):
                    loads.append((kvd_s[:, ic * 256:(ic + 1) * 256],
                                  dI["l1_gkvd"][ts(ic), :], ic, 256))
                for c in range(2):
                    loads.append((kup_s[:, c * 256:(c + 1) * 256],
                                  dI["l1_gkup"][ts(c), :], None, 256))
                    loads.append((vup_s[:, c * 256:(c + 1) * 256],
                                  dI["l1_gvup"][ts(c), :], None, 256))
            else:
                for ic in range(8):
                    loads.append((wk_s[:, ic * 256:(ic + 1) * 256],
                                  dI["l1_lwk"][ts(ic), :], ic, 256))
                    loads.append((wv_s[:, ic * 256:(ic + 1) * 256],
                                  dI["l1_lwv"][ts(ic), :], ic, 256))
            for ic in range(8):
                loads.append((wo_s[:, ic * 1024:(ic + 1) * 1024], wosrc[ts(ic), :], None, 1024))
            P.wait("vector", "dS", cnc)
            lastc = 0
            for i, (dst, srcap, sc, w) in enumerate(loads):
                stg = wstg[0]
                if lastc:
                    P.wait("sync", "dve", lastc)
                cL = P.dma("sync", "dS", stg[:, 0:w], srcap)
                P.wait("vector", "dS", cL)
                if sc is not None:
                    lastc = P.dve("tensor_scalar_mul", dst, stg[:, 0:w],
                                  ncol[:, sc:sc + 1], inc=True)
                else:
                    lastc = P.dve("tensor_copy", dst, stg[:, 0:w], inc=True)
            P.flush()
            stW.close()

            resid_src = R if glob else S2
            qt_of = lambda h: (h % 4) + 4 * (h // 8)
            off_of = lambda h: 64 * ((h // 4) % 2)
            prevC = {"dve_end": 0}
            ld_hist = []
            s2_hist = []
            for t in range(NT):
                pb = t % 2
                if t >= 2:
                    P.wait("sync", "dve", ld_hist[t - 2])
                    P.wait("sync", "st", s2_hist[t - 2])
                    P.wait("vector", "st", s2_hist[t - 2])
                cD = P.dma("sync", "dS", xt[pb][:], resid_src[ts(t), :])
                P.wait("scalar", "dve", prevC["dve_end"])
                P.wait("scalar", "dS", cD)
                P.wait("vector", "dS", cD)
                if glob:
                    ca = ln_stats(xt[pb][:], True, EPS_LN, D)
                    P.wait("vector", "act", ca)
                    cT = P.dve("tensor_scalar", x2t[pb][:], xt[pb][:], cols[:, 0:1],
                               cols[:, 2:3], ALU.subtract, ALU.mult, inc=True)
                    P.wait("sync", "dve", cT)
                    cs2 = P.dma("sync", "st", S2[ts(t), :], x2t[pb][:])
                    base_ap = x2t[pb][:]
                else:
                    cs2 = P.cnt["st"]
                    base_ap = xt[pb][:]
                P.wait("scalar", "dve", P.cnt["dve"])
                ca2 = ln_stats(base_ap, False, EPS_RMS, D)
                P.wait("vector", "act", ca2)
                cln = P.dve("tensor_scalar_mul", lnt[pb][:], base_ap, cols[:, 2:3], inc=True)
                P.wait("tensor", "dve", cln)
                for c in range(8):
                    P.tr(PS_TR[:, ts(c)], lnt[pb][:, ts(c)], ident[:], inc=(c == 7))
                cpe = P.cnt["pe"]
                P.wait("vector", "pe", cpe)
                clnT = P.dve("tensor_copy", lnT[pb][:], PS_TR[:], inc=True)
                P.wait("tensor", "dve", clnT)
                for nb in range(2):
                    for ic in range(8):
                        P.mm(PS_A[:, nb * 512:(nb + 1) * 512], lnT[pb][:, ts(ic)],
                             wq_s[:, ic * 1024 + nb * 512: ic * 1024 + (nb + 1) * 512],
                             start=(ic == 0), stop=(ic == 7),
                             inc=(nb == 1 and ic == 7))
                cQ = P.cnt["pe"]
                if t == 0 and _dbg == "cq0" + pre:
                    P.wait("vector", "pe", cQ)
                    dbg(PS_A[:], 1024)
                    P.wait("tensor", "dve", P.cnt["dve"])
                if t == 0 and _dbg == "ctb" + pre:
                    P.wait("vector", "dS", crt)
                    P.dve("tensor_copy", dbgbuf[:, 0:64], rCt[:, 0:64])
                    cdd = P.dve("tensor_copy", dbgbuf[:, 64:128], rSt[:, 0:64], inc=True)
                    P.wait("sync", "dve", cdd)
                    P.dma("sync", "st", dbg_out[:, 0:128], dbgbuf[:, 0:128])
                Cb = rCt[:, t * 64:(t + 1) * 64].rearrange("p (k two) -> p k two", two=2)                     .unsqueeze(1).broadcast_to([128, 16, 32, 2])
                Sb = rSt[:, t * 64:(t + 1) * 64].rearrange("p (k two) -> p k two", two=2)                     .unsqueeze(1).broadcast_to([128, 16, 32, 2])
                P.wait("vector", "pe", cQ)
                P.wait("vector", "dS", crt)
                qv4 = bass.AP(PS_A, 0, [[1024, 128], [64, 16], [2, 32], [1, 2]])
                qsw = bass.AP(PS_A, 1, [[1024, 128], [64, 16], [2, 32], [-1, 2]])
                P.dve("tensor_tensor",
                      qrbuf[:].rearrange("p (h k two) -> p h k two", h=16, two=2),
                      qv4, Cb, ALU.mult)
                P.dve("tensor_tensor",
                      trash[:].rearrange("p (h k two) -> p h k two", h=16, two=2),
                      qsw, Sb, ALU.mult)
                cqr = P.dve("tensor_add", qrbuf[:], qrbuf[:], trash[:], inc=True)
                P.wait("scalar", "dve", cqr)
                cqs = headnorm(qrbuf[:], 16, 64)
                P.wait("vector", "act", cqs)
                capp = P.dve("tensor_tensor",
                             qn1[:].rearrange("p (h d) -> p h d", h=16),
                             qrbuf[:].rearrange("p (h d) -> p h d", h=16),
                             colh_rs[:, 0:16].unsqueeze(2).broadcast_to([128, 16, 64]),
                             ALU.mult, inc=True)
                if t == 0 and _dbg == "cqn" + pre:
                    dbg(qn1[:], 1024)
                P.wait("tensor", "dve", capp)
                for c in range(8):
                    P.tr(PS_TR2[:, ts(c)], qn1[:, ts(c)], ident[:], inc=(c == 7))
                cpq = P.cnt["pe"]
                P.wait("vector", "pe", cpq)
                for c in range(8):
                    h0, h1 = 2 * c, 2 * c + 1
                    P.dve("tensor_copy",
                          qTl[qt_of(h0)][off_of(h0):off_of(h0) + 64, ts(t)],
                          PS_TR2[0:64, ts(c)])
                    P.dve("tensor_copy",
                          qTl[qt_of(h1)][off_of(h1):off_of(h1) + 64, ts(t)],
                          PS_TR2[64:128, ts(c)], inc=(c == 7))
                cqT = P.cnt["dve"]
                if glob:
                    for ic in range(8):
                        P.mm(PS_B[:, 0:256], lnT[pb][:, ts(ic)],
                             kvd_s[:, ic * 256:(ic + 1) * 256],
                             start=(ic == 0), stop=(ic == 7), inc=(ic == 7))
                    cKV = P.cnt["pe"]
                    P.wait("vector", "pe", cKV)
                    clat = P.dve("tensor_copy", latb[:], PS_B[:, 0:256], inc=True)
                    P.wait("tensor", "dve", clat)
                    for c in range(2):
                        P.tr(PS_TR[:, c * 128:(c + 1) * 128], latb[:, ts(c)], ident[:],
                             inc=(c == 1))
                    cplat = P.cnt["pe"]
                    P.wait("vector", "pe", cplat)
                    clatT = P.dve("tensor_copy", latT[:], PS_TR[:, 0:256], inc=True)
                    P.wait("tensor", "dve", clatT)
                    for c in range(2):
                        P.mm(PS_B[:, 512:768], latT[:, ts(c)],
                             kup_s[:, c * 256:(c + 1) * 256],
                             start=(c == 0), stop=(c == 1), inc=(c == 1))
                    cK = P.cnt["pe"]
                else:
                    for ic in range(8):
                        P.mm(PS_B[:, 512:768], lnT[pb][:, ts(ic)],
                             wk_s[:, ic * 256:(ic + 1) * 256],
                             start=(ic == 0), stop=(ic == 7), inc=(ic == 7))
                    cK = P.cnt["pe"]
                Cb4 = rCt[:, t * 64:(t + 1) * 64].rearrange("p (k two) -> p k two", two=2)                     .unsqueeze(1).broadcast_to([128, 4, 32, 2])
                Sb4 = rSt[:, t * 64:(t + 1) * 64].rearrange("p (k two) -> p k two", two=2)                     .unsqueeze(1).broadcast_to([128, 4, 32, 2])
                P.wait("vector", "pe", cK)
                kv4 = bass.AP(PS_B, 512, [[1024, 128], [64, 4], [2, 32], [1, 2]])
                ksw = bass.AP(PS_B, 513, [[1024, 128], [64, 4], [2, 32], [-1, 2]])
                P.dve("tensor_tensor",
                      qrbuf[:, 0:256].rearrange("p (h k two) -> p h k two", h=4, two=2),
                      kv4, Cb4, ALU.mult)
                P.dve("tensor_tensor",
                      trash[:, 0:256].rearrange("p (h k two) -> p h k two", h=4, two=2),
                      ksw, Sb4, ALU.mult)
                ckr = P.dve("tensor_add", qrbuf[:, 0:256], qrbuf[:, 0:256],
                            trash[:, 0:256], inc=True)
                P.wait("scalar", "dve", ckr)
                cks = headnorm(qrbuf[:, 0:256], 4, 64)
                P.wait("vector", "act", cks)
                ckn = P.dve("tensor_tensor",
                            kn1[:].rearrange("p (h d) -> p h d", h=4),
                            qrbuf[:, 0:256].rearrange("p (h d) -> p h d", h=4),
                            colh_rs[:, 0:4].unsqueeze(2).broadcast_to([128, 4, 64]),
                            ALU.mult, inc=True)
                if t == 0 and _dbg == "ckn" + pre:
                    dbg(kn1[:], 256)
                P.wait("tensor", "dve", ckn)
                for c in range(2):
                    P.tr(PS_TR2[:, c * 128:(c + 1) * 128], kn1[:, ts(c)], ident[:],
                         inc=(c == 1))
                cpk = P.cnt["pe"]
                P.wait("vector", "pe", cpk)
                for c in range(2):
                    P.dve("tensor_copy", kTl[c][:, ts(t)], PS_TR2[:, c * 128:(c + 1) * 128],
                          inc=(c == 1))
                P.wait("tensor", "dve", cqr)
                if glob:
                    for c in range(2):
                        P.mm(PS_A[:, 0:256], latT[:, ts(c)],
                             vup_s[:, c * 256:(c + 1) * 256],
                             start=(c == 0), stop=(c == 1), inc=(c == 1))
                else:
                    for ic in range(8):
                        P.mm(PS_A[:, 0:256], lnT[pb][:, ts(ic)],
                             wv_s[:, ic * 256:(ic + 1) * 256],
                             start=(ic == 0), stop=(ic == 7), inc=(ic == 7))
                cV = P.cnt["pe"]
                P.wait("vector", "pe", cV)
                cVe = P.dve("tensor_copy",
                            bass.AP(VAl, t * 260, [[NT * 260, 128], [65, 4], [1, 64]]),
                            PS_A[:, 0:256].rearrange("p (h d) -> p h d", h=4), inc=True)
                if t == 0 and _dbg == "cva" + pre:
                    dbg(VAl[:, 0:260], 260)
                ld_hist.append(cln)
                s2_hist.append(cs2)
                prevC = {"dve_end": P.cnt["dve"]}
            P.flush()

            st_hist_c = []
            P.wait("sync", "st", P.cnt["st"])
            for s in range(8):
                q0 = s * 512
                for h in range(16):
                    kv = h // 4
                    kb = 64 * (kv % 2)
                    kten = kTl[kv // 2]
                    qten = qTl[qt_of(h)]
                    qb = off_of(h)
                    if glob:
                        kts = list(range(0, 4 * s + 4))
                    else:
                        kts = list(range(max(0, 4 * s - 2), 4 * s + 4))
                    klast = kts[-1]
                    kfirst = kts[0]
                    ce_hist = {}
                    for gi in range(0, len(kts), 2):
                        pair = kts[gi:gi + 2]
                        psb = PS_A if (gi // 2) % 2 == 0 else PS_B
                        pbv = pbuf[(gi // 2) % 2]
                        for j, kt in enumerate(pair):
                            qlo = max(0, kt * 128 - q0)
                            qhi = 512 if glob else min(512, kt * 128 + 384 - q0)
                            P.mm(psb[:, j * 512 + qlo: j * 512 + qhi],
                                 kten[kb:kb + 64, ts(kt)],
                                 qten[qb:qb + 64, q0 + qlo:q0 + qhi],
                                 start=True, stop=True, inc=(j == len(pair) - 1))
                        cs_ = P.cnt["pe"]
                        P.wait("scalar", "pe", cs_)
                        ce = P.act(pbv[:, 0:1024], psb[:, 0:1024], AF.Exp, scale=0.125)
                        ce_hist[gi] = ce
                        cm = 0
                        masks = []
                        for kt in pair:
                            j = pair.index(kt)
                            dqrel = kt * 128 - q0
                            if 0 <= dqrel < 512:
                                masks.append((j, dqrel, mask_c))
                            if not glob:
                                aqrel = kt * 128 + 256 - q0
                                if 0 <= aqrel < 512 and aqrel < (kt * 128 + 384 - q0):
                                    masks.append((j, aqrel, mask_a))
                        if masks:
                            P.wait("vector", "act", ce)
                            for j, qrel, mk in masks:
                                sl = slice(j * 512 + qrel, j * 512 + qrel + 128)
                                cm = P.dve("tensor_mul", pbv[:, sl], pbv[:, sl],
                                           mk[:], inc=True)
                            P.wait("tensor", "dve", cm)
                        else:
                            P.wait("tensor", "act", ce)
                        for j, kt in enumerate(pair):
                            qlo = max(0, kt * 128 - q0)
                            qhi = 512 if glob else min(512, kt * 128 + 384 - q0)
                            P.mm(PS_C[0:65, qlo:qhi],
                                 bass.AP(VAl, kt * 260 + kv * 65, [[NT * 260, 128], [1, 65]]),
                                 pbv[:, j * 512 + qlo: j * 512 + qhi],
                                 start=(kt == kfirst), stop=(kt == klast),
                                 inc=(j == len(pair) - 1))
                        cpv = P.cnt["pe"]
                    P.wait("vector", "pe", cpv)
                    crs = P.dve("reciprocal", rsum[:, 0:512], PS_C[64:65, 0:512], inc=True)
                    P.wait("tensor", "dve", crs)
                    cbc = P.mm(PS_D[0:64, 0:512], ones_r[:, 0:64], rsum[:, 0:512],
                               start=True, stop=True, inc=True)
                    P.wait("vector", "pe", cbc)
                    P.dve("tensor_copy", bcast[0:64, 0:512], PS_D[0:64, 0:512])
                    cot = P.dve("tensor_mul", OTl[h // 2][64 * (h % 2):64 * (h % 2) + 64, 0:512],
                                PS_C[0:64, 0:512], bcast[0:64, 0:512], inc=True)
                    if s == 0 and h == 0 and _dbg == "cot" + pre:
                        dbg(OTl[0][0:64, 0:512], 512, part=64)
                    P.wait("tensor", "dve", cot)
                for qt in range(4):
                    for nb in range(2):
                        for p in range(8):
                            P.mm(PS_A[:, nb * 512:(nb + 1) * 512],
                                 OTl[p][:, ts(qt)],
                                 wo_s[:, p * 1024 + nb * 512: p * 1024 + (nb + 1) * 512],
                                 start=(p == 0), stop=(p == 7),
                                 inc=(p == 7 and nb == 1))
                    cwo = P.cnt["pe"]
                    t_glob = s * 4 + qt
                    pb = t_glob % 2
                    if len(st_hist_c) >= 2:
                        P.wait("sync", "st", st_hist_c[-2])
                    cDu = P.dma("sync", "dS", x2t[pb][:], S2[ts(t_glob), :])
                    if glob:
                        P.wait("vector", "pe", cwo)
                        P.wait("vector", "dS", cDu)
                        if len(st_hist_c) >= 2:
                            P.wait("vector", "st", st_hist_c[-2])
                        cu1 = P.dve("tensor_add", x2t[pb][:], x2t[pb][:], PS_A[:], inc=True)
                        P.wait("tensor", "dve", cu1)
                        P.wait("sync", "dve", cu1)
                        cst = P.dma("sync", "st", S2[ts(t_glob), :], x2t[pb][:])
                    else:
                        cDx = P.dma("sync", "dS", xt[pb][:], R[ts(t_glob), :])
                        P.wait("vector", "pe", cwo)
                        P.wait("vector", "dS", cDu)
                        if len(st_hist_c) >= 2:
                            P.wait("vector", "st", st_hist_c[-2])
                        cu2 = P.dve("tensor_add", x2t[pb][:], x2t[pb][:], PS_A[:], inc=True)
                        P.wait("scalar", "dve", cu2)
                        ca3 = ln_stats(x2t[pb][:], True, EPS_LN, D)
                        P.wait("vector", "act", ca3)
                        P.wait("vector", "dS", cDx)
                        P.dve("tensor_scalar", trash[:], x2t[pb][:], cols[:, 0:1],
                              cols[:, 2:3], ALU.subtract, ALU.mult)
                        cx3 = P.dve("tensor_add", xt[pb][:], xt[pb][:], trash[:], inc=True)
                        P.wait("tensor", "dve", cu2)
                        P.wait("sync", "dve", cx3)
                        cst = P.dma("sync", "st", R[ts(t_glob), :], xt[pb][:])
                    st_hist_c.append(cst)
            P.flush()

        if "C1" in PHASES:
            phase_C(True)
        if "C2" in PHASES:
            phase_C(False)

        if "D" in PHASES:
          with ExitStack() as stD:
            def sbD(name, shape, dtype=BF16):
                return stD.enter_context(nc.sbuf_tensor(name, list(shape), dtype))
            xnT = sbD("xnT", (128, 8 * 512))
            uT = sbD("uT", (128, 32 * 512))
            vT = sbD("vT", (128, 32 * 512))
            fT = sbD("fT", (128, 8 * 512))
            silb = sbD("silb", (128, 512), F32)
            w1b = [sbD(f"w1b{i}", (128, 8 * 128)) for i in range(2)]
            w2b = [sbD(f"w2b{i}", (128, 8 * 128)) for i in range(2)]
            hb = [sbD(f"hb{i}", (128, 32 * 128)) for i in range(2)]
            w3b = [sbD(f"w3b{i}", (128, 32 * 128)) for i in range(2)]
            scl = sbD("scl", (128, 32 * 4 + 8), F32)
            s1c = scl[:, 0:32]; s2c = scl[:, 32:64]
            sh1c = scl[:, 64:96]; sh2c = scl[:, 96:128]; s3c = scl[:, 128:136]
            P.dma("sync", "dS", s1c, dI["l1f_s1"][:])
            P.dma("sync", "dS", s2c, dI["l1f_s2"][:])
            P.dma("sync", "dS", sh1c, dI["l1f_sh1"][:])
            P.dma("sync", "dS", sh2c, dI["l1f_sh2"][:])
            cscl = P.dma("sync", "dS", s3c, dI["l1f_s3"][:])
            P.wait("scalar", "dS", cscl)
            P.wait("scalar", "dG", P.cnt["dG"])
            NG = 8
            prevD = {"dve_end": 0, "uT": 0, "vT": 0}
            st_hist_d = []
            xtg = []
            for g in range(NG):
                for ti in range(4):
                    t = g * 4 + ti
                    pb = t % 2
                    if len(xtg) >= 2:
                        P.wait("sync", "dve", xtg[-2])
                    cD = P.dma("sync", "dS", xt[pb][:], R[ts(t), :])
                    P.wait("scalar", "dve", prevD["dve_end"])
                    P.wait("scalar", "dS", cD)
                    P.wait("vector", "dS", cD)
                    ca = ln_stats(xt[pb][:], True, EPS_LN, D)
                    P.wait("vector", "act", ca)
                    cln = P.dve("tensor_scalar", lnt[pb][:], xt[pb][:], cols[:, 0:1],
                                cols[:, 2:3], ALU.subtract, ALU.mult, inc=True)
                    xtg.append(cln)
                    P.wait("tensor", "dve", cln)
                    for c in range(8):
                        P.tr(PS_TR[:, ts(c)], lnt[pb][:, ts(c)], ident[:], inc=(c == 7))
                    cpe = P.cnt["pe"]
                    P.wait("vector", "pe", cpe)
                    P.wait("vector", "pe", prevD.get("xnT_pe", 0))
                    cxT = P.dve("tensor_copy",
                                bass.AP(xnT, ti * 128, [[4096, 128], [512, 8], [1, 128]]),
                                PS_TR[:].rearrange("p (c k) -> p c k", k=128), inc=True)
                    prevD["dve_end"] = P.cnt["dve"]
                cxnT = P.cnt["dve"]
                if g == 0 and _dbg == "dxnT":
                    dbg(xnT[:], 4096)
                last_mm = {0: 0, 1: 0}
                last_u = 0
                for oc in range(32):
                    bi = oc % 2
                    if oc == 0:
                        P.wait("vector", "pe", prevD.get("w3_pe", 0))
                    if last_mm[bi]:
                        P.wait("scalar", "pe", last_mm[bi])
                    cl1 = P.dma("scalar", "dA",
                                w1b[bi][:].rearrange("p (ic j) -> p ic j", j=128),
                                dI_bf["w1"][oc])
                    cl2 = P.dma("scalar", "dA",
                                w2b[bi][:].rearrange("p (ic j) -> p ic j", j=128),
                                dI_bf["w2"][oc])
                    P.wait("tensor", "dA", cl2)
                    if oc == 0:
                        P.wait("tensor", "dve", cxnT)
                    if oc >= 1:
                        P.wait("tensor", "dve", last_u)
                    for ic in range(8):
                        P.mm(PS_A[:, 0:512], w1b[bi][:, ts(ic)],
                             xnT[:, ic * 512:(ic + 1) * 512],
                             start=(ic == 0), stop=(ic == 7))
                    for ic in range(8):
                        P.mm(PS_A[:, 512:1024], w2b[bi][:, ts(ic)],
                             xnT[:, ic * 512:(ic + 1) * 512],
                             start=(ic == 0), stop=(ic == 7), inc=(ic == 7))
                    cz = P.cnt["pe"]
                    last_mm[bi] = cz
                    if g == 0 and oc == 0 and _dbg == "dz1":
                        P.wait("vector", "pe", cz)
                        dbg(PS_A[:, 0:512], 512)
                        P.wait("scalar", "dve", P.cnt["dve"])
                    if g == 0 and oc == 0 and _dbg == "dw1b":
                        P.wait("vector", "dA", cl1)
                        dbg(w1b[bi][:], 1024)
                        P.wait("tensor", "dve", P.cnt["dve"])
                    P.wait("scalar", "pe", cz)
                    csil = P.act(silb[:], PS_A[:, 0:512], AF.Silu, scale=s1c[:, oc:oc + 1])
                    P.wait("vector", "act", csil)
                    last_u = P.dve("scalar_tensor_tensor",
                                   uT[:, oc * 512:(oc + 1) * 512],
                                   PS_A[:, 512:1024], s2c[:, oc:oc + 1], silb[:],
                                   ALU.mult, ALU.mult, inc=True)
                cuT = last_u
                if g == 0 and _dbg == "duT":
                    dbg(uT[:, 0:2048], 2048)
                prevD["xnT_pe"] = last_mm[0] if last_mm[0] > last_mm[1] else last_mm[1]
                last_mm = {0: 0, 1: 0}
                lact = {0: 0, 1: 0}
                last_act = 0
                for oc in range(32):
                    bi = oc % 2
                    if last_mm[bi]:
                        P.wait("scalar", "pe", last_mm[bi])
                    cl = P.dma("scalar", "dA",
                               hb[bi][:].rearrange("p (ic j) -> p ic j", j=128),
                               dI_bf["h1"][oc])
                    P.wait("tensor", "dA", cl)
                    if oc == 0:
                        P.wait("tensor", "dve", cuT)
                    if lact[bi]:
                        P.wait("tensor", "act", lact[bi])
                    bank = PS_B[:, (oc % 2) * 512:(oc % 2) * 512 + 512]
                    for ic in range(32):
                        P.mm(bank, hb[bi][:, ts(ic)], uT[:, ic * 512:(ic + 1) * 512],
                             start=(ic == 0), stop=(ic == 31), inc=(ic == 31))
                    ch = P.cnt["pe"]
                    last_mm[bi] = ch
                    P.wait("scalar", "pe", ch)
                    last_act = P.act(vT[:, oc * 512:(oc + 1) * 512], bank, AF.Gelu,
                                     scale=sh1c[:, oc:oc + 1])
                    lact[bi] = last_act
                cvT = last_act
                if g == 0 and _dbg == "dvT":
                    P.wait("vector", "act", cvT)
                    dbg(vT[:, 0:2048], 2048)
                last_mm = {0: 0, 1: 0}
                lact = {0: 0, 1: 0}
                for oc in range(32):
                    bi = oc % 2
                    if last_mm[bi]:
                        P.wait("scalar", "pe", last_mm[bi])
                    cl = P.dma("scalar", "dA",
                               hb[bi][:].rearrange("p (ic j) -> p ic j", j=128),
                               dI_bf["h2"][oc])
                    P.wait("tensor", "dA", cl)
                    if oc == 0:
                        P.wait("tensor", "act", cvT)
                    if lact[bi]:
                        P.wait("tensor", "act", lact[bi])
                    bank = PS_B[:, (oc % 2) * 512:(oc % 2) * 512 + 512]
                    for ic in range(32):
                        P.mm(bank, hb[bi][:, ts(ic)], vT[:, ic * 512:(ic + 1) * 512],
                             start=(ic == 0), stop=(ic == 31), inc=(ic == 31))
                    ch = P.cnt["pe"]
                    last_mm[bi] = ch
                    P.wait("scalar", "pe", ch)
                    last_act = P.act(uT[:, oc * 512:(oc + 1) * 512], bank, AF.Gelu,
                                     scale=sh2c[:, oc:oc + 1])
                    lact[bi] = last_act
                cuT2 = last_act
                last_mm = {0: 0, 1: 0}
                lact = {0: 0, 1: 0}
                for oc in range(8):
                    bi = oc % 2
                    if last_mm[bi]:
                        P.wait("scalar", "pe", last_mm[bi])
                    cl = P.dma("scalar", "dA",
                               w3b[bi][:].rearrange("p (ic j) -> p ic j", j=128),
                               dI_bf["w3"][oc])
                    P.wait("tensor", "dA", cl)
                    if oc == 0:
                        P.wait("tensor", "act", cuT2)
                    if lact[bi]:
                        P.wait("tensor", "act", lact[bi])
                    bank = PS_B[:, (oc % 2) * 512:(oc % 2) * 512 + 512]
                    for ic in range(32):
                        P.mm(bank, w3b[bi][:, ts(ic)], uT[:, ic * 512:(ic + 1) * 512],
                             start=(ic == 0), stop=(ic == 31), inc=(ic == 31))
                    cf = P.cnt["pe"]
                    last_mm[bi] = cf
                    P.wait("scalar", "pe", cf)
                    last_act = P.act(fT[:, oc * 512:(oc + 1) * 512], bank, AF.Copy,
                                     scale=s3c[:, oc:oc + 1])
                    lact[bi] = last_act
                cfT = last_act
                prevD["w3_pe"] = cf
                if g == 0 and _dbg == "dfT":
                    P.wait("vector", "act", cfT)
                    dbg(fT[:], 4096)
                for ti in range(4):
                    t = g * 4 + ti
                    pb = t % 2
                    if len(st_hist_d) >= 2:
                        P.wait("sync", "st", st_hist_d[-2])
                    if len(xtg) >= 2:
                        P.wait("sync", "dve", xtg[-2])
                    cD = P.dma("sync", "dS", xt[pb][:], R[ts(t), :])
                    P.wait("tensor", "act", cfT)
                    P.wait("tensor", "dve", prevD["dve_end"])
                    for c in range(8):
                        P.tr(PS_TR[:, ts(c)], fT[:, c * 512 + ti * 128: c * 512 + ti * 128 + 128],
                             ident[:], inc=(c == 7))
                    cpf = P.cnt["pe"]
                    P.wait("vector", "pe", cpf)
                    P.wait("scalar", "dve", prevD["dve_end"])
                    cfm = P.dve("tensor_copy", lnt[pb][:], PS_TR[:], inc=True)
                    P.wait("scalar", "dve", cfm)
                    ca = ln_stats(lnt[pb][:], True, EPS_LN, D)
                    P.wait("vector", "act", ca)
                    P.wait("vector", "dS", cD)
                    if len(st_hist_d) >= 2:
                        P.wait("vector", "st", st_hist_d[-2])
                    P.dve("tensor_scalar", trash[:], lnt[pb][:], cols[:, 0:1],
                          cols[:, 2:3], ALU.subtract, ALU.mult)
                    cx4 = P.dve("tensor_add", xt[pb][:], xt[pb][:], trash[:], inc=True)
                    xtg.append(cx4)
                    P.wait("sync", "dve", cx4)
                    cst = P.dma("sync", "st", y_out[ts(t), :], xt[pb][:])
                    st_hist_d.append(cst)
                    prevD["dve_end"] = P.cnt["dve"]
            P.flush()
        if "D" not in PHASES:
            P.wait("sync", "st", P.cnt["st"])
            zsrc = R if ("A2" in PHASES or "B" in PHASES) else x_in
            if "C1" in PHASES and "C2" not in PHASES:
                zsrc = S2
            for t in range(NT):
                pb = t % 2
                cD = P.dma("sync", "dS", xt[pb][:], zsrc[ts(t), :])
                P.wait("sync", "dS", cD)
                P.dma("sync", "st", y_out[ts(t), :], xt[pb][:])
        P.wait("sync", "st", P.cnt["st"])
        P.flush()
        P.audit_done()

    return nc



def _rope_tables(base):
    inv = 1.0 / (float(base) ** (np.arange(0, 64, 2, dtype=np.float64) / 64.0))
    pos = np.arange(T, dtype=np.float64)
    ang = pos[:, None] * inv[None, :]
    cos = np.cos(ang); sin = np.sin(ang)
    C = np.empty((T, 64), np.float64); S = np.empty((T, 64), np.float64)
    C[:, 0::2] = cos; C[:, 1::2] = cos
    S[:, 0::2] = -sin; S[:, 1::2] = sin
    return C.astype(np.float32), S.astype(np.float32)


def kernel(**inp):
    i32 = np.ascontiguousarray
    base = {}
    base["ident"] = np.eye(128, dtype=BF)
    idx = np.arange(128)
    base["mask_c"] = (idx[None, :] >= idx[:, None]).astype(BF)
    base["mask_a"] = (idx[None, :] < idx[:, None]).astype(BF)
    base["qgc"] = i32(np.broadcast_to(np.tile(np.asarray(inp["L0_attn_q_g"], np.float32)[:32], 16), (128, 512)))
    base["kgc"] = i32(np.broadcast_to(np.tile(np.asarray(inp["L0_attn_k_g"], np.float32), 16), (128, 512)))
    Cg_, Sg_ = _rope_tables(1e6)
    Cl_, Sl_ = _rope_tables(1e4)
    base["ropeCg"], base["ropeSg"] = Cg_.astype(BF), Sg_.astype(BF)
    base["ropeCl"], base["ropeSl"] = Cl_.astype(BF), Sl_.astype(BF)
    base["l0_wq"] = i32(np.asarray(inp["L0_attn_Wq"], np.float32))
    base["l0_wkv"] = i32(np.asarray(inp["L0_attn_Wkv"], np.float32))
    base["l0_wk"] = i32(np.asarray(inp["L0_attn_Wk"], np.float32))
    base["l0_wv"] = i32(np.asarray(inp["L0_attn_Wv"], np.float32))
    wo = np.asarray(inp["L0_attn_Wo"], np.float32).reshape(16, 64, 1024)
    base["l0_wo"] = i32(wo[:, :32, :].reshape(512, 1024))
    base["l0f_wT"] = i32(np.asarray(inp["L0_ffn_W"], np.float32).T)
    base["l0f_b"] = i32(np.asarray(inp["L0_ffn_b"], np.float32).reshape(1, D))
    base["l0f_ldT"] = i32(np.asarray(inp["L0_ffn_lora_down"], np.float32).T)
    base["l0f_luT"] = i32(np.asarray(inp["L0_ffn_lora_up"], np.float32).T)
    base["l0f_da"] = i32(np.asarray(inp["L0_ffn_down_aux"], np.float32))
    base["l0f_uaT"] = i32(np.asarray(inp["L0_ffn_up_aux"], np.float32).T)
    base["l1_n1"] = i32(np.asarray(inp["L1_attn_norm1_w"], np.float32).reshape(8, 128).T)
    base["l1_n2"] = i32(np.asarray(inp["L1_attn_norm2_w"], np.float32).reshape(8, 128).T)
    for k, nm in (("l1_gwq", "L1_attn_g_wq"), ("l1_gkvd", "L1_attn_g_kv_down"),
                  ("l1_gkup", "L1_attn_g_k_up"), ("l1_gvup", "L1_attn_g_v_up"),
                  ("l1_gwo", "L1_attn_g_wo"), ("l1_lwq", "L1_attn_l_wq"),
                  ("l1_lwk", "L1_attn_l_wk"), ("l1_lwv", "L1_attn_l_wv"),
                  ("l1_lwo", "L1_attn_l_wo")):
        base[k] = i32(np.asarray(inp[nm], np.float32))

    def tile_oc(wT, ocs):
        K = wT.shape[0]
        return i32(wT.reshape(K, ocs, 128).transpose(1, 0, 2))

    base["l1f_w1"] = tile_oc(np.asarray(inp["L1_ffn_w1_w"], np.float32).T, 32)
    base["l1f_w2"] = tile_oc(np.asarray(inp["L1_ffn_w2_w"], np.float32).T, 32)
    base["l1f_h1"] = tile_oc(np.asarray(inp["L1_ffn_h1_w"], np.float32).T, 32)
    base["l1f_h2"] = tile_oc(np.asarray(inp["L1_ffn_h2_w"], np.float32).T, 32)
    base["l1f_w3"] = tile_oc(np.asarray(inp["L1_ffn_w3_w"], np.float32).T, 8)
    base["l1f_s1"] = i32(np.asarray(inp["L1_ffn_w1_s"], np.float32).reshape(32, 128).T)
    base["l1f_s2"] = i32(np.asarray(inp["L1_ffn_w2_s"], np.float32).reshape(32, 128).T)
    base["l1f_sh1"] = i32(np.asarray(inp["L1_ffn_h1_s"], np.float32).reshape(32, 128).T)
    base["l1f_sh2"] = i32(np.asarray(inp["L1_ffn_h2_s"], np.float32).reshape(32, 128).T)
    base["l1f_s3"] = i32(np.asarray(inp["L1_ffn_w3_s"], np.float32).reshape(8, 128).T)

    import os as _os
    if _os.environ.get("DEV_NO_BIGW"):
        for k in ("l1f_w1", "l1f_w2", "l1f_h1", "l1f_h2", "l1f_w3"):
            base.pop(k)
    x = np.asarray(inp["L0_attn_x"], np.float32)
    in_maps = []
    for c in range(8):
        m = dict(base)
        m["x"] = i32(x[c])
        in_maps.append(m)

    nc = build_program()
    ncores = int(_os.environ.get("DEV_CORES", "8"))
    res = run_bass_kernel_spmd(nc, in_maps[:ncores], core_ids=list(range(ncores)))
    out = [np.asarray(res.results[c]["y"], np.float32) for c in range(ncores)]
    if _os.environ.get("DEV_DBG"):
        np.save("/tmp/dbg.npy", np.asarray(res.results[0]["dbg"], np.float32))
    while len(out) < 8:
        out.append(out[-1])
    return np.stack(out)


# version 02
# speedup vs baseline: 1.5076x; 1.5076x over previous
import sys
import numpy as np
import ml_dtypes
import concourse.bass as bass
import concourse.mybir as mybir
from concourse.bass_utils import run_bass_kernel_spmd
from contextlib import ExitStack

F32 = mybir.dt.float32
BF16 = mybir.dt.bfloat16
AF = mybir.ActivationFunctionType
ALU = mybir.AluOpType
AX = mybir.AxisListType
BF = ml_dtypes.bfloat16

T = 4096
D = 1024
NT = T // 128
EPS_LN = 1e-5
EPS_RMS = 1e-6

ENGS = ("sync", "scalar", "vector", "tensor", "gpsimd")
PHASES = ("A1", "A2", "B", "C1", "C2", "D")


def ts(i, n=128):
    return slice(i * n, (i + 1) * n)


class Prog:

    def __init__(self, nc, stack):
        self.nc = nc
        self.ops = {e: [] for e in ENGS}
        self.sems = {}
        self.cnt = {}
        self.trace = {e: [] for e in ENGS}
        self.done = {}
        self.blocks = 0
        self.gtrace = []
        self.psums = {}
        self.bankof_cache = {}
        for s in ("dS", "dG", "dA", "pe", "act", "dve", "st"):
            self.sems[s] = stack.enter_context(nc.semaphore(s))
            self.cnt[s] = 0

    def reg_psum(self, handle, nbanks):
        base = sum(nb for _, nb in self.psums.values())
        self.psums[handle.name] = (base, nbanks)

    def banks(self, ap):
        try:
            nm = ap.tensor.name
        except Exception:
            return ()
        if nm not in self.psums:
            return ()
        base, nbanks = self.psums[nm]
        esz = 2 if "bfloat" in str(ap.dtype) else 4
        free = int(ap.tensor.shape[-1])
        off = int(ap.offset)
        col0 = off % free
        ext = 1
        for step, count in list(ap.ap):
            if int(step) < free and int(count) > 1:
                ext += int(step) * (int(count) - 1)
        b0 = (col0 * esz) // 2048
        b1 = ((col0 + ext - 1) * esz) // 2048
        return tuple(range(base + b0, base + min(b1, nbanks - 1) + 1))

    def raw(self, eng, fn):
        self.ops[eng].append(fn)

    _TL = {"sync": 0, "scalar": 1, "vector": 2, "tensor": 3, "gpsimd": 4}
    _QTL = {"sync": 5, "gpsimd": 6, "scalar": 7}

    def _vc_init(self):
        if not hasattr(self, "_vc"):
            self._vc = [ [0]*8 for _ in range(5) ]
            self._qv = [ [0]*8 for _ in range(3) ]
            self._tlpos = [0]*8
            self._semvc = {s: {} for s in self.cnt}
            self._eff = []
            self._nhaz = 0

    def note(self, eng, kind, sem=None, val=None, rd=(), wr=(), tag=""):
        self._vc_init()
        import traceback
        E = self._TL[eng]
        if kind == "wait":
            w = self._semvc[sem].get(val)
            if w is None:
                raise AssertionError(f"wait {sem}>={val} on {eng}: count never produced")
            c = self._vc[E]
            self._vc[E] = [max(a, b) for a, b in zip(c, w)]
            return
        vec = list(self._vc[E])
        self._tlpos[E] += 1
        vec[E] = self._tlpos[E]
        self._vc[E] = vec
        if kind in ("dma", "dma0"):
            qi = self._QTL[eng]
            q = qi - 5
            cvec = [max(a, b) for a, b in zip(vec, self._qv[q])]
            self._tlpos[qi] += 1
            cvec[qi] = self._tlpos[qi]
            self._qv[q] = cvec
            if sem is not None:
                self._semvc[sem][val] = cvec
            end_tl, end_idx, begin = qi, self._tlpos[qi], vec
        else:
            if sem is not None:
                self._semvc[sem][val] = vec
            end_tl, end_idx, begin = E, vec[E], vec
        if rd or wr:
            fr = sys._getframe(2)
            line = fr.f_lineno
            me = (end_tl, end_idx, begin, frozenset(rd) | frozenset(wr),
                  frozenset(wr), eng, tag, line)
            W = 800
            for prev in self._eff[-W:]:
                ptl, pidx, pbegin, pbanks, pwr, peng, ptag, pline = prev
                if not (me[3] & pbanks):
                    continue
                if not (pwr or me[4]):
                    if not ({peng, eng} == {"scalar", "vector"}):
                        continue
                if peng == eng and kind not in ("dma", "dma0"):
                    continue
                if ptl == end_tl:
                    continue
                if begin[ptl] >= pidx:
                    continue
                self._nhaz += 1
                if self._nhaz < 30:
                    print(f"PSUM-HAZARD: {ptag}@line{pline} ({peng}) banks{sorted(pbanks)} "
                          f"not ordered before {tag}@line{line} ({eng}) banks{sorted(me[3])}")
            self._eff.append(me)

    def audit_done(self):
        if getattr(self, "_nhaz", 0):
            raise AssertionError(f"{self._nhaz} PSUM hazards detected")

    def _inc(self, eng, sem, build, n):
        self.cnt[sem] += 1
        s = self.sems[sem]
        self.ops[eng].append(lambda e: build(e).then_inc(s, n))
        self.trace[eng].append(("op", sem, self.cnt[sem]))
        self.note(eng, "dma" if n == 16 else "op", sem, self.cnt[sem],
                  self._rd, self._wr, self._tag)
        return self.cnt[sem]

    def wait(self, eng, sem, val):
        if val <= 0:
            return
        s = self.sems[sem]
        mult = 16 if sem in ("dS", "dG", "dA", "st") else 1
        self.ops[eng].append(lambda e: e.wait_ge(s, mult * val))
        self.trace[eng].append(("wait", sem, val))
        self.note(eng, "wait", sem, val)

    def dma(self, queue, sem, out_ap, in_ap, inc=True):
        self._rd, self._wr, self._tag = self.banks(in_ap), self.banks(out_ap), "dma"
        if inc:
            return self._inc(queue, sem, lambda e: e.dma_start(out_ap, in_ap), 16)
        self.ops[queue].append(lambda e: e.dma_start(out_ap, in_ap))
        self.note(queue, "dma0", None, None, self._rd, self._wr, "dma0")
        return None

    def mm(self, out_ap, l_ap, r_ap, start, stop, inc=False, tp=None):
        if tp is None:
            b = lambda e: e.matmul(out_ap, l_ap, r_ap, start=start, stop=stop)
        else:
            b = lambda e: e.matmul(out_ap, l_ap, r_ap, start=start, stop=stop,
                                   tile_position=tp)
        self._rd, self._wr, self._tag = (), self.banks(out_ap), "mm"
        if inc:
            return self._inc("tensor", "pe", b, 1)
        self.ops["tensor"].append(b)
        self.note("tensor", "op", None, None, self._rd, self._wr, "mm")

    def tr(self, out_ap, in_ap, ident_ap, inc=False):
        b = lambda e: e.transpose(out_ap, in_ap, ident_ap)
        self._rd, self._wr, self._tag = (), self.banks(out_ap), "tr"
        if inc:
            return self._inc("tensor", "pe", b, 1)
        self.ops["tensor"].append(b)
        self.note("tensor", "op", None, None, self._rd, self._wr, "tr")

    def act(self, out_ap, in_ap, func, bias=0.0, scale=1.0, accum=None, inc=True):
        def b(e):
            kw = {}
            if accum is not None:
                kw["accum_out"] = accum
            return e.activation(out_ap, in_ap, func, bias=bias, scale=scale, **kw)
        self._rd, self._wr, self._tag = self.banks(in_ap), self.banks(out_ap), "act"
        if inc:
            return self._inc("scalar", "act", b, 1)
        self.ops["scalar"].append(b)
        self.note("scalar", "op", None, None, self._rd, self._wr, "act")

    def dve(self, name, *aps, inc=False, **kw):
        b = lambda e: getattr(e, name)(*aps, **kw)
        rd = []
        wr = list(self.banks(aps[0])) if aps and hasattr(aps[0], "tensor") else []
        for a in aps[1:]:
            if hasattr(a, "tensor"):
                rd += list(self.banks(a))
        self._rd, self._wr, self._tag = tuple(rd), tuple(wr), "dve:" + name
        if name.startswith("reduce") or name == "tensor_reduce":
            self.ops["vector"].append(b)
            self.note("vector", "op", None, None, self._rd, self._wr, "dve:" + name)
            so, si = self._spacer
            sb_ = lambda e: e.tensor_copy(so, si)
            self._rd, self._wr = (), ()
            if inc:
                return self._inc("vector", "dve", sb_, 1)
            self.ops["vector"].append(sb_)
            self.note("vector", "op", None, None, (), (), "dve:spacer")
            return None
        if inc:
            return self._inc("vector", "dve", b, 1)
        self.ops["vector"].append(b)
        self.note("vector", "op", None, None, self._rd, self._wr, "dve:" + name)

    def dve_sp(self, name, *aps, **kw):
        self.dve(name, *aps, inc=False, **kw)
        so, si = self._spacer
        return self.dve("tensor_copy", so, si, inc=True)

    def gpc(self, name, *aps, inc=False, **kw):
        b = lambda e: getattr(e, name)(*aps, **kw)
        if inc:
            return self._inc("gpsimd", "dG", b, 16)
        self.ops["gpsimd"].append(b)

    def check(self):
        tr = {e: list(self.trace[e]) for e in ENGS}
        pos = {e: 0 for e in ENGS}
        val = {s: self.done.get(s, 0) for s in self.cnt}
        tot = dict(val)
        for e in ENGS:
            for it in tr[e]:
                if it[0] == "op" and it[1] is not None:
                    tot[it[1]] = max(tot[it[1]], it[2])
        for e in ENGS:
            for it in tr[e]:
                if it[0] == "wait" and it[2] > tot[it[1]]:
                    raise AssertionError(f"wait {it} on {e} exceeds total {tot[it[1]]}")
        progressed = True
        while progressed:
            progressed = False
            for e in ENGS:
                while pos[e] < len(tr[e]):
                    it = tr[e][pos[e]]
                    if it[0] == "wait" and val[it[1]] < it[2]:
                        break
                    if it[0] == "op" and it[1] is not None:
                        val[it[1]] = max(val[it[1]], it[2])
                    pos[e] += 1
                    progressed = True
        stuck = {e: (pos[e], len(tr[e]), tr[e][pos[e]] if pos[e] < len(tr[e]) else None)
                 for e in ENGS if pos[e] < len(tr[e])}
        if stuck:
            raise AssertionError(f"DEADLOCK: {stuck} sem_vals={val}")
        self.done = val

    def flush(self):
        self.check()
        self._vc_init()
        bar = [max(c[i] for c in self._vc) for i in range(8)]
        for e in range(5):
            self._vc[e] = [max(a, b) for a, b in zip(self._vc[e], bar)]
        self.trace = {e: [] for e in ENGS}
        ops = self.ops
        self.ops = {e: [] for e in ENGS}
        with self.nc.Block() as blk:
            for eng in ENGS:
                lst = ops[eng]
                if lst:
                    def runner(e, lst=lst):
                        for f in lst:
                            f(e)
                    getattr(blk, eng)(runner)


def build_program():
    import os as _os
    nc = bass.Bass("TRN2", target_bir_lowering=False, debug=False)

    dI = {}
    def din(name, shape, dtype=F32):
        dI[name] = nc.dram_tensor(name, list(shape), dtype, kind="ExternalInput").ap()
        return dI[name]

    x_in = din("x", (T, D))
    y_out = nc.dram_tensor("y", [T, D], F32, kind="ExternalOutput").ap()
    bf_shapes = {"w1": (32, D, 128), "w2": (32, D, 128), "h1": (32, 4096, 128),
                 "h2": (32, 4096, 128), "w3": (8, 4096, 128)}
    bf_dram = {nm: nc.dram_tensor("bf_" + nm, list(sh), BF16, kind="Internal").ap()
               for nm, sh in bf_shapes.items()}
    dI_bf = {nm: [bf_dram[nm][oc].rearrange("(ic p) j -> p ic j", p=128)
                  for oc in range(bf_shapes[nm][0])] for nm in bf_shapes}
    dbg_out = nc.dram_tensor("dbg", [128, 4096], F32, kind="ExternalOutput").ap()
    _dbg = _os.environ.get("DEV_DBG", "")
    R = nc.dram_tensor("Rres", [T, D], F32, kind="Internal").ap()
    S2 = nc.dram_tensor("S2res", [T, D], F32, kind="Internal").ap()

    din("ident", (128, 128), BF16)
    din("mask_c", (128, 128), BF16)
    din("mask_a", (128, 128), BF16)
    din("qgc", (128, 512))
    din("kgc", (128, 512))
    din("ropeCg", (T, 64), BF16)
    din("ropeSg", (T, 64), BF16)
    din("ropeCl", (T, 64), BF16)
    din("ropeSl", (T, 64), BF16)
    din("l0_wq", (D, D))
    din("l0_wkv", (D, 128))
    din("l0_wk", (128, 512))
    din("l0_wv", (128, 512))
    din("l0_wo", (512, D))
    din("l0f_wT", (D, D))
    din("l0f_b", (1, D))
    din("l0f_ldT", (100, 4))
    din("l0f_luT", (4, 50))
    din("l0f_da", (100, D))
    din("l0f_uaT", (50, D))
    din("l1_n1", (128, 8))
    din("l1_n2", (128, 8))
    din("l1_gwq", (D, D))
    din("l1_gkvd", (D, 256))
    din("l1_gkup", (256, 256))
    din("l1_gvup", (256, 256))
    din("l1_gwo", (D, D))
    din("l1_lwq", (D, D))
    din("l1_lwk", (D, 256))
    din("l1_lwv", (D, 256))
    din("l1_lwo", (D, D))
    import os as _os
    if not _os.environ.get("DEV_NO_BIGW"):
        din("l1f_w1", (32, D, 128))
        din("l1f_w2", (32, D, 128))
        din("l1f_h1", (32, 4096, 128))
        din("l1f_h2", (32, 4096, 128))
        din("l1f_w3", (8, 4096, 128))
    din("l1f_s1", (128, 32))
    din("l1f_s2", (128, 32))
    din("l1f_sh1", (128, 32))
    din("l1f_sh2", (128, 32))
    din("l1f_s3", (128, 8))

    _ = _os
    st = ExitStack()
    with st:
        P = Prog(nc, st)

        def sb(name, shape, dtype=F32):
            return st.enter_context(nc.sbuf_tensor(name, list(shape), dtype))

        ident = sb("ident_s", (128, 128), BF16)
        mask_c = sb("mask_c_s", (128, 128), BF16)
        mask_a = sb("mask_a_s", (128, 128), BF16)
        ones_r = sb("ones_r", (1, 64))
        xt = [sb(f"xt{i}", (128, D)) for i in range(2)]
        x2t = [sb(f"x2t{i}", (128, D)) for i in range(2)]
        lnt = [sb(f"lnt{i}", (128, D), BF16) for i in range(2)]
        lnT = [sb(f"lnT{i}", (128, D), BF16) for i in range(2)]
        trash = sb("trash", (128, D))
        cols = sb("cols", (128, 64))
        colh = sb("colh", (128, 64))
        colh_ln = sb("colh_ln", (128, 16))
        colh_rs = sb("colh_rs", (128, 16))
        spacer = sb("spacer", (128, 16))
        P._spacer = (spacer[:, 0:8], spacer[:, 8:16])
        negC = sb("negC", (128, 1))
        zcol = sb("zcol", (128, 1))
        rsum = sb("rsum", (1, 512))
        bcast = sb("bcast", (64, 512))
        pbuf = [sb(f"pbuf{i}", (128, 1024), BF16) for i in range(2)]

        cvals = [EPS_LN, EPS_RMS, -0.5, 0.125]
        cstt = sb("cstt", (128, len(cvals)))
        if not _os.environ.get("DEV_NO_CST"):
            for i, v in enumerate(cvals):
                P.dve("memset", cstt[:, i:i + 1], float(v))
                nc.const_aps.register(F32, float(v), cstt[:, i:i + 1])

        PS_A = st.enter_context(nc.psum_tensor("PS_A", (128, 1024), F32))
        PS_B = st.enter_context(nc.psum_tensor("PS_B", (128, 1024), F32))
        PS_C = st.enter_context(nc.psum_tensor("PS_C", (128, 512), F32))
        PS_D = st.enter_context(nc.psum_tensor("PS_D", (128, 512), F32))
        PS_TR = st.enter_context(nc.psum_tensor("PS_TR", (128, 1024), BF16))
        PS_TR2 = st.enter_context(nc.psum_tensor("PS_TR2", (128, 1024), BF16))
        P.reg_psum(PS_A, 2); P.reg_psum(PS_B, 2); P.reg_psum(PS_C, 1)
        P.reg_psum(PS_D, 1); P.reg_psum(PS_TR, 1); P.reg_psum(PS_TR2, 1)

        dbgbuf = sb("dbgbuf", (128, 1024)) if _os.environ.get("DEV_DBG") else None

        def dbg(src_ap, width, part=128):
            for o in range(0, width, 1024):
                w = min(1024, width - o)
                cpy = P.dve("tensor_copy", dbgbuf[0:part, 0:w], src_ap[:, o:o + w], inc=True)
                P.wait("sync", "dve", cpy)
                cs_ = P.dma("sync", "st", dbg_out[0:part, o:o + w], dbgbuf[0:part, 0:w])
                P.wait("vector", "st", cs_)

        def ln_stats(src_ap, with_mean, eps, F, psum=False):
            sq = cols[:, 4:5]
            ca = P.act(trash[:, :F], src_ap, AF.Square, accum=sq)
            if with_mean:
                if psum:
                    ca = P.act(trash[:, :F], src_ap, AF.Copy, accum=cols[:, 0:1])
                else:
                    P.dve("reduce_sum", cols[:, 0:1], src_ap, axis=AX.X)
                P.wait("vector", "act", ca)
                P.dve("tensor_scalar_mul", cols[:, 0:1], cols[:, 0:1], 1.0 / F)
                P.dve("tensor_scalar_mul", cols[:, 1:2], sq, 1.0 / F)
                P.dve("tensor_mul", cols[:, 5:6], cols[:, 0:1], cols[:, 0:1])
                P.dve("tensor_sub", cols[:, 1:2], cols[:, 1:2], cols[:, 5:6])
                cv = P.dve("tensor_scalar_max", cols[:, 1:2], cols[:, 1:2], 0.0, inc=True)
            else:
                P.wait("vector", "act", ca)
                cv = P.dve("tensor_scalar_mul", cols[:, 1:2], sq, 1.0 / F, inc=True)
            P.wait("scalar", "dve", cv)
            P.act(cols[:, 6:7], cols[:, 1:2], AF.Ln, bias=float(eps), inc=False)
            return P.act(cols[:, 2:3], cols[:, 6:7], AF.Exp, scale=-0.5)

        def headnorm(src_ap, nh, hd, eps=EPS_RMS, tap=False):
            ca = P.act(trash[:, 0:nh * hd], src_ap, AF.Square)
            P.wait("vector", "act", ca)
            if tap:
                dbg(trash[:, 0:nh * hd], nh * hd)
            P.dve("reduce_sum", colh[:, 0:nh],
                  trash[:, 0:nh * hd].rearrange("p (h d) -> p h d", h=nh), axis=AX.X)
            cv = P.dve("tensor_scalar_mul", colh[:, 0:nh], colh[:, 0:nh], 1.0 / hd, inc=True)
            P.wait("scalar", "dve", cv)
            P.act(colh_ln[:, 0:nh], colh[:, 0:nh], AF.Ln, bias=float(eps), inc=False)
            return P.act(colh_rs[:, 0:nh], colh_ln[:, 0:nh], AF.Exp, scale=-0.5)

        with ExitStack() as stA:
            def sbA(name, shape, dtype=BF16):
                return stA.enter_context(nc.sbuf_tensor(name, list(shape), dtype))

            wq = sbA("wq", (128, 8 * 1024))
            wkv = sbA("wkv", (128, 8 * 128))
            wk = sbA("wk", (128, 512))
            wv = sbA("wv", (128, 512))
            wo8 = [sbA(f"wo8_{p}", (64, 1024)) for p in range(8)]
            qgc_s = sbA("qgc_s", (128, 512), F32)
            kgc_s = sbA("kgc_s", (128, 512), F32)
            qn = sbA("qn", (128, 512))
            kn = sbA("kn", (128, 512))
            qT = [sbA(f"qT{p}", (128, T)) for p in range(4)]
            kT = [sbA(f"kT{p}", (128, T)) for p in range(4)]
            VA = sbA("VA", (128, NT * 528))
            OT = [sbA(f"OT{p}", (64, 512)) for p in range(8)]
            kvb = sbA("kvb", (128, 128))
            kvT = sbA("kvT", (128, 128))

            _pre = int(_os.environ.get("DEV_PRE", "63"))
            cdc = 0
            if _pre & 1:
                P.dma("sync", "dS", ident[:], dI["ident"][:])
                P.dma("sync", "dS", mask_c[:], dI["mask_c"][:])
                P.dma("sync", "dS", mask_a[:], dI["mask_a"][:])
                P.dma("sync", "dS", qgc_s[:], dI["qgc"][:])
                cdc = P.dma("sync", "dS", kgc_s[:], dI["kgc"][:])
            if _pre & 16:
                P.dve("memset", ones_r[:], 1.0)
            if _pre & 2:
                for ic in range(8):
                    P.dma("gpsimd", "dG", wq[:, ic * 1024:(ic + 1) * 1024], dI["l0_wq"][ts(ic), :])
                    P.dma("gpsimd", "dG", wkv[:, ts(ic)], dI["l0_wkv"][ts(ic), :])
                P.dma("gpsimd", "dG", wk[:], dI["l0_wk"][:])
                P.dma("gpsimd", "dG", wv[:], dI["l0_wv"][:])
                for p in range(8):
                    P.dma("gpsimd", "dG", wo8[p][:], dI["l0_wo"][p * 64:(p + 1) * 64, :])
            dG_w = P.cnt["dG"]
            if "D" in PHASES:
                for nm in ("w1", "w2"):
                    for c in range(8):
                        P.dma("gpsimd", "dG", bf_dram[nm][c * 4:(c + 1) * 4],
                              dI["l1f_" + nm][c * 4:(c + 1) * 4])
                for c in range(8):
                    P.dma("gpsimd", "dG", bf_dram["w3"][c:c + 1],
                          dI["l1f_w3"][c:c + 1])
                for nm in ("h1", "h2"):
                    for c in range(16):
                        P.dma("gpsimd", "dG", bf_dram[nm][c * 2:(c + 1) * 2],
                              dI["l1f_" + nm][c * 2:(c + 1) * 2])

            if _pre & 4:
                P.wait("vector", "dS", cdc)
                P.dve("reduce_max", cols[:, 8:9], qgc_s[:], axis=AX.X, apply_absolute_value=True)
                P.dve("reduce_max", cols[:, 9:10], kgc_s[:], axis=AX.X, apply_absolute_value=True)
                P.dve("tensor_mul", cols[:, 8:9], cols[:, 8:9], cols[:, 9:10])
                P.dve("tensor_scalar_mul", negC[:], cols[:, 8:9], -5.6569)
            if _pre & 8:
                P.dve("memset", bass.AP(VA, 32, [[NT * 528, 128], [528, NT], [33, 16], [1, 1]]), 1.0)
            if _pre & 32:
                P.dve("memset", zcol[:], 0.0, inc=True)
            P.flush()

            prev = {"cln": 0, "dve_end": 0, "act_end": 0}
            cln_hist = []
            for t in (range(NT) if "A1" in PHASES else ()):
                pb = t % 2
                if t >= 2:
                    P.wait("sync", "dve", cln_hist[t - 2])
                cD = P.dma("sync", "dS", xt[pb][:], x_in[ts(t), :])
                P.wait("scalar", "dve", prev["dve_end"])
                P.wait("scalar", "dS", cD)
                P.wait("vector", "dS", cD)
                ca = ln_stats(xt[pb][:], True, EPS_LN, D)
                P.wait("vector", "act", ca)
                cln = P.dve("tensor_scalar", lnt[pb][:], xt[pb][:], cols[:, 0:1],
                            cols[:, 2:3], ALU.subtract, ALU.mult, inc=True)
                if t == 0 and _dbg == "ln1":
                    P.wait("vector", "dve", cln)
                    dbg(lnt[pb][:], D)
                P.wait("tensor", "dve", cln)
                P.wait("tensor", "dG", dG_w)
                for c in range(8):
                    P.tr(PS_TR[:, ts(c)], lnt[pb][:, ts(c)], ident[:], inc=(c == 7))
                cpe = P.cnt["pe"]
                P.wait("vector", "pe", cpe)
                clnT = P.dve("tensor_copy", lnT[pb][:], PS_TR[:], inc=True)
                P.wait("tensor", "dve", clnT)
                for nb in range(2):
                    for ic in range(8):
                        P.mm(PS_A[:, nb * 512:(nb + 1) * 512], lnT[pb][:, ts(ic)],
                             wq[:, ic * 1024 + nb * 512: ic * 1024 + (nb + 1) * 512],
                             start=(ic == 0), stop=(ic == 7),
                             inc=(nb == 1 and ic == 7))
                cQ = P.cnt["pe"]
                for ic in range(8):
                    P.mm(PS_B[:, 0:128], lnT[pb][:, ts(ic)], wkv[:, ts(ic)],
                         start=(ic == 0), stop=(ic == 7), inc=(ic == 7))
                cKV = P.cnt["pe"]
                if t == 0 and _dbg == "q0":
                    P.wait("vector", "pe", cQ)
                    dbg(PS_A[:], D)
                    P.wait("scalar", "dve", P.cnt["dve"])
                P.wait("scalar", "pe", cQ)
                cqs = headnorm(PS_A[:], 16, 64, tap=(t == 0 and _dbg == "qsq2"))
                if t == 0 and _dbg in ("qsq", "qrs"):
                    P.wait("vector", "act", cqs)
                    if _dbg == "qsq":
                        dbg(trash[:], D)
                    else:
                        dbg(colh[:, 0:16], 16)
                        P.dve("tensor_copy", dbgbuf[:, 16:32], colh[:, 16:32])
                        P.dve("tensor_copy", dbgbuf[:, 32:48], colh_rs[:, 0:16], inc=True)
                        P.dma("sync", "st", dbg_out[:, 16:48], dbgbuf[:, 16:48])
                    P.wait("scalar", "dve", P.cnt["dve"])
                P.wait("vector", "act", cqs)
                P.dve("tensor_tensor", trash[:, 0:512].rearrange("p (h d) -> p h d", h=16),
                      bass.AP(PS_A, 0, [[1024, 128], [64, 16], [1, 32]]),
                      colh_rs[:, 0:16].unsqueeze(2).broadcast_to([128, 16, 32]), ALU.mult)
                cqn = P.dve("tensor_mul", qn[:], trash[:, 0:512], qgc_s[:], inc=True)
                if t == 0 and _dbg == "qn":
                    dbg(qn[:], 512)
                P.wait("tensor", "dve", cqn)
                for c in range(4):
                    P.tr(PS_TR2[:, ts(c)], qn[:, ts(c)], ident[:], inc=(c == 3))
                cpq = P.cnt["pe"]
                P.wait("vector", "pe", cpq)
                for c in range(4):
                    P.dve("tensor_copy", qT[c][:, ts(t)], PS_TR2[:, ts(c)], inc=(c == 3))
                cqT = P.cnt["dve"]
                P.wait("vector", "pe", cKV)
                ckvb = P.dve("tensor_copy", kvb[:], PS_B[:, 0:128], inc=True)
                P.wait("tensor", "dve", ckvb)
                ckvTp = P.tr(PS_TR[:, 0:128], kvb[:], ident[:], inc=True)
                P.wait("vector", "pe", ckvTp)
                ckvT = P.dve("tensor_copy", kvT[:], PS_TR[:, 0:128], inc=True)
                P.wait("tensor", "dve", ckvT)
                cK = P.mm(PS_B[:, 512:1024], kvT[:], wk[:], start=True, stop=True, inc=True)
                P.wait("scalar", "pe", cK)
                cks = headnorm(PS_B[:, 512:1024], 16, 32)
                P.wait("vector", "act", cks)
                P.dve("tensor_tensor", trash[:, 0:512].rearrange("p (h d) -> p h d", h=16),
                      PS_B[:, 512:1024].rearrange("p (h d) -> p h d", h=16),
                      colh_rs[:, 0:16].unsqueeze(2).broadcast_to([128, 16, 32]), ALU.mult)
                ckn = P.dve("tensor_mul", kn[:], trash[:, 0:512], kgc_s[:], inc=True)
                if t == 0 and _dbg == "kn":
                    dbg(kn[:], 512)
                P.wait("tensor", "dve", ckn)
                for c in range(4):
                    P.tr(PS_TR2[:, ts(c)], kn[:, ts(c)], ident[:], inc=(c == 3))
                cpk = P.cnt["pe"]
                P.wait("vector", "pe", cpk)
                for c in range(4):
                    P.dve("tensor_copy", kT[c][:, ts(t)], PS_TR2[:, ts(c)], inc=(c == 3))
                P.wait("tensor", "dve", cqn)
                cV = P.mm(PS_A[:, 0:512], kvT[:], wv[:], start=True, stop=True, inc=True)
                P.wait("vector", "pe", cV)
                cVe = P.dve("tensor_copy",
                            bass.AP(VA, t * 528, [[NT * 528, 128], [33, 16], [1, 32]]),
                            PS_A[:, 0:512].rearrange("p (h d) -> p h d", h=16), inc=True)
                if t == 0 and _dbg == "va":
                    dbg(VA[:, 0:528], 528)
                if t == 0 and _dbg == "qT0":
                    dbg(qT[0][:, 0:512], 512)
                if t == 0 and _dbg == "kT0":
                    dbg(kT[0][:, 0:512], 512)
                cln_hist.append(cln)
                prev = {"cln": cln, "dve_end": P.cnt["dve"], "act_end": P.cnt["act"]}
            P.flush()

            A2L = int(_os.environ.get("DEV_A2", "99"))
            st_hist = []
            last_exp_set = {0: 0, 1: 0}
            for s in (range(8) if "A2" in PHASES else ()):
                q0 = s * 512
                for h in range(16):
                    pT = h // 4
                    off = 32 * (h % 4)
                    tp = (96, 0) if off == 96 else None
                    kts = list(range(0, 4 * s + 4))
                    klast = kts[-1]
                    pairs = [kts[i:i + 2] for i in range(0, len(kts), 2)]

                    def a2_scores(gi):
                        pair = pairs[gi]
                        psb = PS_A if gi % 2 == 0 else PS_B
                        if last_exp_set[gi % 2]:
                            P.wait("tensor", "act", last_exp_set[gi % 2])
                        for j, kt in enumerate(pair):
                            qlo = max(0, kt * 128 - q0)
                            P.mm(psb[:, j * 512 + qlo:(j + 1) * 512],
                                 kT[pT][off:off + 32, ts(kt)],
                                 qT[pT][off:off + 32, q0 + qlo:q0 + 512],
                                 start=True, stop=True, inc=(j == len(pair) - 1), tp=tp)
                        return P.cnt["pe"]

                    cs_pend = a2_scores(0)
                    for gi, pair in enumerate(pairs):
                        psb = PS_A if gi % 2 == 0 else PS_B
                        pbv = pbuf[gi % 2]
                        P.wait("scalar", "pe", cs_pend)
                        ce = P.act(pbv[:, 0:1024], psb[:, 0:1024], AF.Exp,
                                   bias=negC[:], scale=0.125)
                        last_exp_set[gi % 2] = ce
                        if gi + 1 < len(pairs):
                            cs_pend = a2_scores(gi + 1)
                        need_mask = [kt for kt in pair if q0 <= kt * 128 < q0 + 512]
                        cm = 0
                        if need_mask:
                            P.wait("vector", "act", ce)
                            for kt in need_mask:
                                j = pair.index(kt)
                                qrel = kt * 128 - q0
                                sl = slice(j * 512 + qrel, j * 512 + qrel + 128)
                                P.dve("tensor_mul", pbv[:, sl], pbv[:, sl],
                                      mask_c[:], inc=False)
                            cm = P.dve_sp("tensor_copy", P._spacer[0], P._spacer[1])
                            P.wait("tensor", "dve", cm)
                        else:
                            P.wait("tensor", "act", ce)
                        for j, kt in enumerate(pair):
                            qlo = max(0, kt * 128 - q0)
                            P.mm(PS_C[0:33, qlo:512],
                                 bass.AP(VA, kt * 528 + h * 33, [[NT * 528, 128], [1, 33]]),
                                 pbv[:, j * 512 + qlo:(j + 1) * 512],
                                 start=(kt == 0), stop=(kt == klast),
                                 inc=(j == len(pair) - 1))
                        cpv = P.cnt["pe"]
                    oOT, oOff = OT[h // 2], 32 * (h % 2)
                    P.wait("vector", "pe", cpv)
                    crs = P.dve("reciprocal", rsum[:, 0:512], PS_C[32:33, 0:512], inc=True)
                    P.wait("tensor", "dve", crs)
                    cbc = P.mm(PS_D[0:32, 0:512], ones_r[:, 0:32], rsum[:, 0:512],
                               start=True, stop=True, inc=True)
                    P.wait("vector", "pe", cbc)
                    P.dve("tensor_copy", bcast[0:32, 0:512], PS_D[0:32, 0:512])
                    cot = P.dve_sp("tensor_mul", oOT[oOff:oOff + 32, 0:512],
                                   PS_C[0:32, 0:512], bcast[0:32, 0:512])
                    P.wait("tensor", "dve", cot)
                for qt in range(4 if A2L >= 5 else 0):
                    for nb in range(2):
                        for p in range(8):
                            P.mm(PS_A[:, nb * 512:(nb + 1) * 512], OT[p][:, ts(qt)],
                                 wo8[p][:, nb * 512:(nb + 1) * 512],
                                 start=(p == 0), stop=(p == 7),
                                 inc=(p == 7 and nb == 1))
                    if s == 0 and qt == 0 and _dbg == "wo":
                        P.wait("vector", "pe", P.cnt["pe"])
                        dbg(PS_A[:], D)
                    cwo = P.cnt["pe"]
                    t_glob = s * 4 + qt
                    pb = t_glob % 2
                    if len(st_hist) >= 2:
                        P.wait("sync", "st", st_hist[-2])
                    cD = P.dma("sync", "dS", xt[pb][:], x_in[ts(t_glob), :])
                    if A2L >= 6:
                        P.wait("scalar", "pe", cwo)
                        P.wait("vector", "pe", cwo)
                        ca = ln_stats(PS_A[:], True, EPS_LN, D, psum=True)
                        P.wait("vector", "act", ca)
                        P.wait("vector", "dS", cD)
                        if len(st_hist) >= 2:
                            P.wait("vector", "st", st_hist[-2])
                        P.dve("tensor_scalar", trash[:], PS_A[:], cols[:, 0:1],
                              cols[:, 2:3], ALU.subtract, ALU.mult)
                        cx1 = P.dve("tensor_add", xt[pb][:], xt[pb][:], trash[:], inc=True)
                    else:
                        P.wait("vector", "pe", cwo)
                        P.wait("vector", "dS", cD)
                        if len(st_hist) >= 2:
                            P.wait("vector", "st", st_hist[-2])
                        P.dve("tensor_copy", trash[:], PS_A[:])
                        cx1 = P.dve("tensor_copy", xt[pb][:], xt[pb][:], inc=True)
                    P.wait("tensor", "dve", cx1)
                    P.wait("sync", "dve", cx1)
                    cst = P.dma("sync", "st", R[ts(t_glob), :], xt[pb][:])
                    st_hist.append(cst)
            if "A2" in PHASES and A2L < 5:
                for t_glob in range(NT):
                    pb = t_glob % 2
                    if len(st_hist) >= 2:
                        P.wait("sync", "st", st_hist[-2])
                    cD = P.dma("sync", "dS", xt[pb][:], x_in[ts(t_glob), :])
                    P.wait("sync", "dS", cD)
                    cst = P.dma("sync", "st", R[ts(t_glob), :], xt[pb][:])
                    st_hist.append(cst)
            P.flush()


        if "B" in PHASES:
          with ExitStack() as stB:
            def sbB(name, shape, dtype=BF16):
                return stB.enter_context(nc.sbuf_tensor(name, list(shape), dtype))
            wc = sbB("wc", (128, 8 * 1024))
            ldT_b = sbB("ldT_b", (100, 4))
            luT_b = sbB("luT_b", (4, 50))
            da_b = sbB("da_b", (100, 1024))
            uaT_b = sbB("uaT_b", (50, 1024))
            A_sb = sbB("A_sb", (4, 1024))
            B_sb = sbB("B_sb", (50, 1024))
            b_sb = sbB("b_sb", (1, 1024), F32)
            ones1 = sbB("ones1", (1, 128), F32)
            wstg = [sbB(f"bstg{i}", (128, 1024), F32) for i in range(2)]

            P.dve("memset", ones1[:], 1.0)
            small = [(ldT_b, "l0f_ldT"), (luT_b, "l0f_luT"),
                     (da_b, "l0f_da"), (uaT_b, "l0f_uaT")]
            for dst, nm in small:
                cL = P.dma("sync", "dS", wstg[0][0:dst.shape[0], 0:dst.shape[1]], dI[nm][:])
                P.wait("vector", "dS", cL)
                P.dve("tensor_copy", dst[:], wstg[0][0:dst.shape[0], 0:dst.shape[1]], inc=True)
                P.wait("sync", "dve", P.cnt["dve"])
            cB1 = P.dma("sync", "dS", b_sb[:], dI["l0f_b"][:])
            lastc = [0, 0]
            for ic in range(8):
                if lastc[ic % 2]:
                    P.wait("sync", "dve", lastc[ic % 2])
                cL = P.dma("sync", "dS", wstg[ic % 2][:], dI["l0f_wT"][ts(ic), :])
                P.wait("vector", "dS", cL)
                lastc[ic % 2] = P.dve("tensor_copy", wc[:, ic * 1024:(ic + 1) * 1024],
                                      wstg[ic % 2][:], inc=True)
            cwcast = P.cnt["dve"]
            P.wait("tensor", "dve", cwcast)
            P.mm(PS_A[0:4, 0:512], ldT_b[:], da_b[:, 0:512], start=True, stop=True)
            cA = P.mm(PS_A[0:4, 512:1024], ldT_b[:], da_b[:, 512:1024], start=True,
                      stop=True, inc=True)
            P.wait("vector", "pe", cA)
            cAe = P.dve("tensor_copy", A_sb[:], PS_A[0:4, 0:1024], inc=True)
            P.wait("tensor", "dve", cAe)
            P.mm(PS_B[0:50, 0:512], luT_b[:], A_sb[:, 0:512], start=True, stop=True)
            cBm = P.mm(PS_B[0:50, 512:1024], luT_b[:], A_sb[:, 512:1024], start=True,
                       stop=True, inc=True)
            P.wait("vector", "pe", cBm)
            cBe = P.dve("tensor_copy", B_sb[:], PS_B[0:50, 0:1024], inc=True)
            P.wait("tensor", "dve", cBe)
            for rc in range(8):
                P.mm(PS_A[:, 0:512], B_sb[:, ts(rc)], uaT_b[:, 0:512],
                     start=True, stop=True)
                cL2 = P.mm(PS_A[:, 512:1024], B_sb[:, ts(rc)], uaT_b[:, 512:1024],
                           start=True, stop=True, inc=True)
                P.wait("vector", "pe", cL2)
                cadd = P.dve("tensor_add", wc[:, rc * 1024:(rc + 1) * 1024],
                             PS_A[:, 0:1024], wc[:, rc * 1024:(rc + 1) * 1024], inc=True)
                P.wait("tensor", "dve", cadd)
            P.flush()

            prevB = {"dve_end": 0}
            st_hist_b = []
            cln_hist_b = []
            for t in range(NT):
                pb = t % 2
                if t >= 2:
                    P.wait("sync", "dve", cln_hist_b[t - 2])
                    P.wait("sync", "st", st_hist_b[t - 2])
                cD = P.dma("sync", "dS", xt[pb][:], R[ts(t), :])
                P.wait("scalar", "dve", prevB["dve_end"])
                P.wait("scalar", "dS", cD)
                P.wait("vector", "dS", cD)
                ca = ln_stats(xt[pb][:], True, EPS_LN, D)
                P.wait("vector", "act", ca)
                cln = P.dve("tensor_scalar", lnt[pb][:], xt[pb][:], cols[:, 0:1],
                            cols[:, 2:3], ALU.subtract, ALU.mult, inc=True)
                P.wait("tensor", "dve", cln)
                for c in range(8):
                    P.tr(PS_TR[:, ts(c)], lnt[pb][:, ts(c)], ident[:], inc=(c == 7))
                cpe = P.cnt["pe"]
                P.wait("vector", "pe", cpe)
                clnT = P.dve("tensor_copy", lnT[pb][:], PS_TR[:], inc=True)
                P.wait("tensor", "dve", clnT)
                P.wait("tensor", "dS", cB1)
                for nb in range(2):
                    for ic in range(8):
                        P.mm(PS_A[:, nb * 512:(nb + 1) * 512], lnT[pb][:, ts(ic)],
                             wc[:, ic * 1024 + nb * 512: ic * 1024 + (nb + 1) * 512],
                             start=(ic == 0), stop=False)
                    P.mm(PS_A[:, nb * 512:(nb + 1) * 512], ones1[:],
                         b_sb[:, nb * 512:(nb + 1) * 512],
                         start=False, stop=True, inc=(nb == 1))
                cff = P.cnt["pe"]
                P.wait("scalar", "pe", cff)
                P.wait("vector", "pe", cff)
                ca2 = ln_stats(PS_A[:], True, EPS_LN, D, psum=True)
                P.wait("vector", "act", ca2)
                P.dve("tensor_scalar", trash[:], PS_A[:], cols[:, 0:1],
                      cols[:, 2:3], ALU.subtract, ALU.mult)
                cx2 = P.dve("tensor_add", xt[pb][:], xt[pb][:], trash[:], inc=True)
                P.wait("tensor", "dve", cx2)
                P.wait("sync", "dve", cx2)
                cst = P.dma("sync", "st", R[ts(t), :], xt[pb][:])
                st_hist_b.append(cst)
                cln_hist_b.append(cln)
                prevB = {"dve_end": P.cnt["dve"]}
            P.flush()

        def phase_C(glob):
          with ExitStack() as stC:
            pre = "g" if glob else "l"
            def sbC(name, shape, dtype=BF16):
                return stC.enter_context(nc.sbuf_tensor(pre + name, list(shape), dtype))
            wq_s = sbC("wq", (128, 8 * 1024))
            wo_s = sbC("wo", (128, 8 * 1024))
            if glob:
                kvd_s = sbC("kvd", (128, 8 * 256))
                kup_s = sbC("kup", (128, 2 * 256))
                vup_s = sbC("vup", (128, 2 * 256))
            else:
                wk_s = sbC("wk", (128, 8 * 256))
                wv_s = sbC("wv", (128, 8 * 256))
            rCt = sbC("rC", (128, NT * 64))
            rSt = sbC("rS", (128, NT * 64))
            qrbuf = sbC("qrbuf", (128, 1024), F32)
            qn1 = sbC("qn1", (128, 1024))
            kn1 = sbC("kn1", (128, 256))
            latb = sbC("latb", (128, 256))
            latT = sbC("latT", (128, 256))
            qTl = [sbC(f"qT{p}", (128, T)) for p in range(8)]
            kTl = [sbC(f"kT{p}", (128, T)) for p in range(2)]
            VAl = sbC("VA", (128, NT * 260))
            OTl = [sbC(f"OT{p}", (128, 512)) for p in range(8)]
            ncol = sbC("ncol", (128, 8), F32)
            stW = ExitStack()
            wstg = [stW.enter_context(nc.sbuf_tensor(pre + "stg0", [128, 1024], F32))] * 2

            P.dve("memset", bass.AP(VAl, 64, [[NT * 260, 128], [260, NT], [65, 4], [1, 1]]), 1.0)
            cnc = P.dma("sync", "dS", ncol[:], dI["l1_n1" if glob else "l1_n2"][:])
            tCd = dI["ropeCg" if glob else "ropeCl"]
            tSd = dI["ropeSg" if glob else "ropeSl"]
            P.dma("sync", "dS", rCt[:].rearrange("p (t j) -> p t j", j=64),
                  tCd.rearrange("(t p) j -> p t j", p=128))
            crt = P.dma("sync", "dS", rSt[:].rearrange("p (t j) -> p t j", j=64),
                        tSd.rearrange("(t p) j -> p t j", p=128))
            wqsrc = dI["l1_gwq" if glob else "l1_lwq"]
            wosrc = dI["l1_gwo" if glob else "l1_lwo"]
            loads = []
            for ic in range(8):
                loads.append((wq_s[:, ic * 1024:(ic + 1) * 1024], wqsrc[ts(ic), :], ic, 1024))
            if glob:
                for ic in range(8):
                    loads.append((kvd_s[:, ic * 256:(ic + 1) * 256],
                                  dI["l1_gkvd"][ts(ic), :], ic, 256))
                for c in range(2):
                    loads.append((kup_s[:, c * 256:(c + 1) * 256],
                                  dI["l1_gkup"][ts(c), :], None, 256))
                    loads.append((vup_s[:, c * 256:(c + 1) * 256],
                                  dI["l1_gvup"][ts(c), :], None, 256))
            else:
                for ic in range(8):
                    loads.append((wk_s[:, ic * 256:(ic + 1) * 256],
                                  dI["l1_lwk"][ts(ic), :], ic, 256))
                    loads.append((wv_s[:, ic * 256:(ic + 1) * 256],
                                  dI["l1_lwv"][ts(ic), :], ic, 256))
            for ic in range(8):
                loads.append((wo_s[:, ic * 1024:(ic + 1) * 1024], wosrc[ts(ic), :], None, 1024))
            P.wait("vector", "dS", cnc)
            lastc = 0
            for i, (dst, srcap, sc, w) in enumerate(loads):
                stg = wstg[0]
                if lastc:
                    P.wait("sync", "dve", lastc)
                cL = P.dma("sync", "dS", stg[:, 0:w], srcap)
                P.wait("vector", "dS", cL)
                if sc is not None:
                    lastc = P.dve("tensor_scalar_mul", dst, stg[:, 0:w],
                                  ncol[:, sc:sc + 1], inc=True)
                else:
                    lastc = P.dve("tensor_copy", dst, stg[:, 0:w], inc=True)
            P.flush()
            stW.close()

            resid_src = R if glob else S2
            qt_of = lambda h: (h % 4) + 4 * (h // 8)
            off_of = lambda h: 64 * ((h // 4) % 2)
            prevC = {"dve_end": 0}
            ld_hist = []
            s2_hist = []
            for t in range(NT):
                pb = t % 2
                if t >= 2:
                    P.wait("sync", "dve", ld_hist[t - 2])
                    P.wait("sync", "st", s2_hist[t - 2])
                    P.wait("vector", "st", s2_hist[t - 2])
                cD = P.dma("sync", "dS", xt[pb][:], resid_src[ts(t), :])
                P.wait("scalar", "dve", prevC["dve_end"])
                P.wait("scalar", "dS", cD)
                P.wait("vector", "dS", cD)
                if glob:
                    ca = ln_stats(xt[pb][:], True, EPS_LN, D)
                    P.wait("vector", "act", ca)
                    cT = P.dve("tensor_scalar", x2t[pb][:], xt[pb][:], cols[:, 0:1],
                               cols[:, 2:3], ALU.subtract, ALU.mult, inc=True)
                    P.wait("sync", "dve", cT)
                    cs2 = P.dma("sync", "st", S2[ts(t), :], x2t[pb][:])
                    base_ap = x2t[pb][:]
                else:
                    cs2 = P.cnt["st"]
                    base_ap = xt[pb][:]
                P.wait("scalar", "dve", P.cnt["dve"])
                ca2 = ln_stats(base_ap, False, EPS_RMS, D)
                P.wait("vector", "act", ca2)
                cln = P.dve("tensor_scalar_mul", lnt[pb][:], base_ap, cols[:, 2:3], inc=True)
                P.wait("tensor", "dve", cln)
                for c in range(8):
                    P.tr(PS_TR[:, ts(c)], lnt[pb][:, ts(c)], ident[:], inc=(c == 7))
                cpe = P.cnt["pe"]
                P.wait("vector", "pe", cpe)
                clnT = P.dve("tensor_copy", lnT[pb][:], PS_TR[:], inc=True)
                P.wait("tensor", "dve", clnT)
                for nb in range(2):
                    for ic in range(8):
                        P.mm(PS_A[:, nb * 512:(nb + 1) * 512], lnT[pb][:, ts(ic)],
                             wq_s[:, ic * 1024 + nb * 512: ic * 1024 + (nb + 1) * 512],
                             start=(ic == 0), stop=(ic == 7),
                             inc=(nb == 1 and ic == 7))
                cQ = P.cnt["pe"]
                if t == 0 and _dbg == "cq0" + pre:
                    P.wait("vector", "pe", cQ)
                    dbg(PS_A[:], 1024)
                    P.wait("tensor", "dve", P.cnt["dve"])
                if t == 0 and _dbg == "ctb" + pre:
                    P.wait("vector", "dS", crt)
                    P.dve("tensor_copy", dbgbuf[:, 0:64], rCt[:, 0:64])
                    cdd = P.dve("tensor_copy", dbgbuf[:, 64:128], rSt[:, 0:64], inc=True)
                    P.wait("sync", "dve", cdd)
                    P.dma("sync", "st", dbg_out[:, 0:128], dbgbuf[:, 0:128])
                Cb = rCt[:, t * 64:(t + 1) * 64].rearrange("p (k two) -> p k two", two=2)                     .unsqueeze(1).broadcast_to([128, 16, 32, 2])
                Sb = rSt[:, t * 64:(t + 1) * 64].rearrange("p (k two) -> p k two", two=2)                     .unsqueeze(1).broadcast_to([128, 16, 32, 2])
                P.wait("vector", "pe", cQ)
                P.wait("vector", "dS", crt)
                qv4 = bass.AP(PS_A, 0, [[1024, 128], [64, 16], [2, 32], [1, 2]])
                qsw = bass.AP(PS_A, 1, [[1024, 128], [64, 16], [2, 32], [-1, 2]])
                P.dve("tensor_tensor",
                      qrbuf[:].rearrange("p (h k two) -> p h k two", h=16, two=2),
                      qv4, Cb, ALU.mult)
                P.dve("tensor_tensor",
                      trash[:].rearrange("p (h k two) -> p h k two", h=16, two=2),
                      qsw, Sb, ALU.mult)
                cqr = P.dve("tensor_add", qrbuf[:], qrbuf[:], trash[:], inc=True)
                P.wait("scalar", "dve", cqr)
                cqs = headnorm(qrbuf[:], 16, 64)
                P.wait("vector", "act", cqs)
                capp = P.dve("tensor_tensor",
                             qn1[:].rearrange("p (h d) -> p h d", h=16),
                             qrbuf[:].rearrange("p (h d) -> p h d", h=16),
                             colh_rs[:, 0:16].unsqueeze(2).broadcast_to([128, 16, 64]),
                             ALU.mult, inc=True)
                if t == 0 and _dbg == "cqn" + pre:
                    dbg(qn1[:], 1024)
                P.wait("tensor", "dve", capp)
                for c in range(8):
                    P.tr(PS_TR2[:, ts(c)], qn1[:, ts(c)], ident[:], inc=(c == 7))
                cpq = P.cnt["pe"]
                P.wait("vector", "pe", cpq)
                for c in range(8):
                    h0, h1 = 2 * c, 2 * c + 1
                    P.dve("tensor_copy",
                          qTl[qt_of(h0)][off_of(h0):off_of(h0) + 64, ts(t)],
                          PS_TR2[0:64, ts(c)])
                    P.dve("tensor_copy",
                          qTl[qt_of(h1)][off_of(h1):off_of(h1) + 64, ts(t)],
                          PS_TR2[64:128, ts(c)], inc=(c == 7))
                cqT = P.cnt["dve"]
                if glob:
                    for ic in range(8):
                        P.mm(PS_B[:, 0:256], lnT[pb][:, ts(ic)],
                             kvd_s[:, ic * 256:(ic + 1) * 256],
                             start=(ic == 0), stop=(ic == 7), inc=(ic == 7))
                    cKV = P.cnt["pe"]
                    P.wait("vector", "pe", cKV)
                    clat = P.dve("tensor_copy", latb[:], PS_B[:, 0:256], inc=True)
                    P.wait("tensor", "dve", clat)
                    for c in range(2):
                        P.tr(PS_TR[:, c * 128:(c + 1) * 128], latb[:, ts(c)], ident[:],
                             inc=(c == 1))
                    cplat = P.cnt["pe"]
                    P.wait("vector", "pe", cplat)
                    clatT = P.dve("tensor_copy", latT[:], PS_TR[:, 0:256], inc=True)
                    P.wait("tensor", "dve", clatT)
                    for c in range(2):
                        P.mm(PS_B[:, 512:768], latT[:, ts(c)],
                             kup_s[:, c * 256:(c + 1) * 256],
                             start=(c == 0), stop=(c == 1), inc=(c == 1))
                    cK = P.cnt["pe"]
                else:
                    for ic in range(8):
                        P.mm(PS_B[:, 512:768], lnT[pb][:, ts(ic)],
                             wk_s[:, ic * 256:(ic + 1) * 256],
                             start=(ic == 0), stop=(ic == 7), inc=(ic == 7))
                    cK = P.cnt["pe"]
                Cb4 = rCt[:, t * 64:(t + 1) * 64].rearrange("p (k two) -> p k two", two=2)                     .unsqueeze(1).broadcast_to([128, 4, 32, 2])
                Sb4 = rSt[:, t * 64:(t + 1) * 64].rearrange("p (k two) -> p k two", two=2)                     .unsqueeze(1).broadcast_to([128, 4, 32, 2])
                P.wait("vector", "pe", cK)
                kv4 = bass.AP(PS_B, 512, [[1024, 128], [64, 4], [2, 32], [1, 2]])
                ksw = bass.AP(PS_B, 513, [[1024, 128], [64, 4], [2, 32], [-1, 2]])
                P.dve("tensor_tensor",
                      qrbuf[:, 0:256].rearrange("p (h k two) -> p h k two", h=4, two=2),
                      kv4, Cb4, ALU.mult)
                P.dve("tensor_tensor",
                      trash[:, 0:256].rearrange("p (h k two) -> p h k two", h=4, two=2),
                      ksw, Sb4, ALU.mult)
                ckr = P.dve("tensor_add", qrbuf[:, 0:256], qrbuf[:, 0:256],
                            trash[:, 0:256], inc=True)
                P.wait("scalar", "dve", ckr)
                cks = headnorm(qrbuf[:, 0:256], 4, 64)
                P.wait("vector", "act", cks)
                ckn = P.dve("tensor_tensor",
                            kn1[:].rearrange("p (h d) -> p h d", h=4),
                            qrbuf[:, 0:256].rearrange("p (h d) -> p h d", h=4),
                            colh_rs[:, 0:4].unsqueeze(2).broadcast_to([128, 4, 64]),
                            ALU.mult, inc=True)
                if t == 0 and _dbg == "ckn" + pre:
                    dbg(kn1[:], 256)
                P.wait("tensor", "dve", ckn)
                for c in range(2):
                    P.tr(PS_TR2[:, c * 128:(c + 1) * 128], kn1[:, ts(c)], ident[:],
                         inc=(c == 1))
                cpk = P.cnt["pe"]
                P.wait("vector", "pe", cpk)
                for c in range(2):
                    P.dve("tensor_copy", kTl[c][:, ts(t)], PS_TR2[:, c * 128:(c + 1) * 128],
                          inc=(c == 1))
                P.wait("tensor", "dve", cqr)
                if glob:
                    for c in range(2):
                        P.mm(PS_A[:, 0:256], latT[:, ts(c)],
                             vup_s[:, c * 256:(c + 1) * 256],
                             start=(c == 0), stop=(c == 1), inc=(c == 1))
                else:
                    for ic in range(8):
                        P.mm(PS_A[:, 0:256], lnT[pb][:, ts(ic)],
                             wv_s[:, ic * 256:(ic + 1) * 256],
                             start=(ic == 0), stop=(ic == 7), inc=(ic == 7))
                cV = P.cnt["pe"]
                P.wait("vector", "pe", cV)
                cVe = P.dve("tensor_copy",
                            bass.AP(VAl, t * 260, [[NT * 260, 128], [65, 4], [1, 64]]),
                            PS_A[:, 0:256].rearrange("p (h d) -> p h d", h=4), inc=True)
                if t == 0 and _dbg == "cva" + pre:
                    dbg(VAl[:, 0:260], 260)
                ld_hist.append(cln)
                s2_hist.append(cs2)
                prevC = {"dve_end": P.cnt["dve"]}
            P.flush()

            st_hist_c = []
            last_exp_set = {0: 0, 1: 0}
            P.wait("sync", "st", P.cnt["st"])
            for s in range(8):
                q0 = s * 512
                for h in range(16):
                    kv = h // 4
                    kb = 64 * (kv % 2)
                    kten = kTl[kv // 2]
                    qten = qTl[qt_of(h)]
                    qb = off_of(h)
                    if glob:
                        kts = list(range(0, 4 * s + 4))
                    else:
                        kts = list(range(max(0, 4 * s - 2), 4 * s + 4))
                    klast = kts[-1]
                    kfirst = kts[0]
                    pairs = [kts[i:i + 2] for i in range(0, len(kts), 2)]

                    def c_scores(gi):
                        pair = pairs[gi]
                        psb = PS_A if gi % 2 == 0 else PS_B
                        if last_exp_set[gi % 2]:
                            P.wait("tensor", "act", last_exp_set[gi % 2])
                        for j, kt in enumerate(pair):
                            qlo = max(0, kt * 128 - q0)
                            qhi = 512 if glob else min(512, kt * 128 + 384 - q0)
                            P.mm(psb[:, j * 512 + qlo: j * 512 + qhi],
                                 kten[kb:kb + 64, ts(kt)],
                                 qten[qb:qb + 64, q0 + qlo:q0 + qhi],
                                 start=True, stop=True, inc=(j == len(pair) - 1))
                        return P.cnt["pe"]

                    cs_pend = c_scores(0)
                    for gi, pair in enumerate(pairs):
                        psb = PS_A if gi % 2 == 0 else PS_B
                        pbv = pbuf[gi % 2]
                        P.wait("scalar", "pe", cs_pend)
                        ce = P.act(pbv[:, 0:1024], psb[:, 0:1024], AF.Exp, scale=0.125)
                        last_exp_set[gi % 2] = ce
                        if gi + 1 < len(pairs):
                            cs_pend = c_scores(gi + 1)
                        cm = 0
                        masks = []
                        for kt in pair:
                            j = pair.index(kt)
                            dqrel = kt * 128 - q0
                            if 0 <= dqrel < 512:
                                masks.append((j, dqrel, mask_c))
                            if not glob:
                                aqrel = kt * 128 + 256 - q0
                                if 0 <= aqrel < 512 and aqrel < (kt * 128 + 384 - q0):
                                    masks.append((j, aqrel, mask_a))
                        if masks:
                            P.wait("vector", "act", ce)
                            for j, qrel, mk in masks:
                                sl = slice(j * 512 + qrel, j * 512 + qrel + 128)
                                P.dve("tensor_mul", pbv[:, sl], pbv[:, sl],
                                      mk[:], inc=False)
                            cm = P.dve_sp("tensor_copy", P._spacer[0], P._spacer[1])
                            P.wait("tensor", "dve", cm)
                        else:
                            P.wait("tensor", "act", ce)
                        for j, kt in enumerate(pair):
                            qlo = max(0, kt * 128 - q0)
                            qhi = 512 if glob else min(512, kt * 128 + 384 - q0)
                            P.mm(PS_C[0:65, qlo:qhi],
                                 bass.AP(VAl, kt * 260 + kv * 65, [[NT * 260, 128], [1, 65]]),
                                 pbv[:, j * 512 + qlo: j * 512 + qhi],
                                 start=(kt == kfirst), stop=(kt == klast),
                                 inc=(j == len(pair) - 1))
                        cpv = P.cnt["pe"]
                    P.wait("vector", "pe", cpv)
                    crs = P.dve("reciprocal", rsum[:, 0:512], PS_C[64:65, 0:512], inc=True)
                    P.wait("tensor", "dve", crs)
                    cbc = P.mm(PS_D[0:64, 0:512], ones_r[:, 0:64], rsum[:, 0:512],
                               start=True, stop=True, inc=True)
                    P.wait("vector", "pe", cbc)
                    P.dve("tensor_copy", bcast[0:64, 0:512], PS_D[0:64, 0:512])
                    cot = P.dve_sp("tensor_mul", OTl[h // 2][64 * (h % 2):64 * (h % 2) + 64, 0:512],
                                   PS_C[0:64, 0:512], bcast[0:64, 0:512])
                    if s == 0 and h == 0 and _dbg == "cot" + pre:
                        dbg(OTl[0][0:64, 0:512], 512, part=64)
                    P.wait("tensor", "dve", cot)
                for qt in range(4):
                    for nb in range(2):
                        for p in range(8):
                            P.mm(PS_A[:, nb * 512:(nb + 1) * 512],
                                 OTl[p][:, ts(qt)],
                                 wo_s[:, p * 1024 + nb * 512: p * 1024 + (nb + 1) * 512],
                                 start=(p == 0), stop=(p == 7),
                                 inc=(p == 7 and nb == 1))
                    cwo = P.cnt["pe"]
                    t_glob = s * 4 + qt
                    pb = t_glob % 2
                    if len(st_hist_c) >= 2:
                        P.wait("sync", "st", st_hist_c[-2])
                    cDu = P.dma("sync", "dS", x2t[pb][:], S2[ts(t_glob), :])
                    if glob:
                        P.wait("vector", "pe", cwo)
                        P.wait("vector", "dS", cDu)
                        if len(st_hist_c) >= 2:
                            P.wait("vector", "st", st_hist_c[-2])
                        cu1 = P.dve("tensor_add", x2t[pb][:], x2t[pb][:], PS_A[:], inc=True)
                        P.wait("tensor", "dve", cu1)
                        P.wait("sync", "dve", cu1)
                        cst = P.dma("sync", "st", S2[ts(t_glob), :], x2t[pb][:])
                    else:
                        cDx = P.dma("sync", "dS", xt[pb][:], R[ts(t_glob), :])
                        P.wait("vector", "pe", cwo)
                        P.wait("vector", "dS", cDu)
                        if len(st_hist_c) >= 2:
                            P.wait("vector", "st", st_hist_c[-2])
                        cu2 = P.dve("tensor_add", x2t[pb][:], x2t[pb][:], PS_A[:], inc=True)
                        P.wait("scalar", "dve", cu2)
                        ca3 = ln_stats(x2t[pb][:], True, EPS_LN, D)
                        P.wait("vector", "act", ca3)
                        P.wait("vector", "dS", cDx)
                        P.dve("tensor_scalar", trash[:], x2t[pb][:], cols[:, 0:1],
                              cols[:, 2:3], ALU.subtract, ALU.mult)
                        cx3 = P.dve("tensor_add", xt[pb][:], xt[pb][:], trash[:], inc=True)
                        P.wait("tensor", "dve", cu2)
                        P.wait("sync", "dve", cx3)
                        cst = P.dma("sync", "st", R[ts(t_glob), :], xt[pb][:])
                    st_hist_c.append(cst)
            P.flush()

        if "C1" in PHASES:
            phase_C(True)
        if "C2" in PHASES:
            phase_C(False)

        if "D" in PHASES:
          with ExitStack() as stD:
            def sbD(name, shape, dtype=BF16):
                return stD.enter_context(nc.sbuf_tensor(name, list(shape), dtype))
            xnT = sbD("xnT", (128, 8 * 512))
            uT = sbD("uT", (128, 32 * 512))
            vT = sbD("vT", (128, 32 * 512))
            fT = sbD("fT", (128, 8 * 512))
            silb = sbD("silb", (128, 1024), F32)
            w1b = [sbD(f"w1b{i}", (128, 8 * 128)) for i in range(2)]
            w2b = [sbD(f"w2b{i}", (128, 8 * 128)) for i in range(2)]
            hb = [sbD(f"hb{i}", (128, 32 * 128)) for i in range(2)]
            w3b = [sbD(f"w3b{i}", (128, 32 * 128)) for i in range(2)]
            scl = sbD("scl", (128, 32 * 4 + 8), F32)
            s1c = scl[:, 0:32]; s2c = scl[:, 32:64]
            sh1c = scl[:, 64:96]; sh2c = scl[:, 96:128]; s3c = scl[:, 128:136]
            P.dma("sync", "dS", s1c, dI["l1f_s1"][:])
            P.dma("sync", "dS", s2c, dI["l1f_s2"][:])
            P.dma("sync", "dS", sh1c, dI["l1f_sh1"][:])
            P.dma("sync", "dS", sh2c, dI["l1f_sh2"][:])
            cscl = P.dma("sync", "dS", s3c, dI["l1f_s3"][:])
            P.wait("scalar", "dS", cscl)
            P.wait("scalar", "dG", P.cnt["dG"])
            NG = 8
            prevD = {"dve_end": 0, "uT": 0, "vT": 0}
            st_hist_d = []
            xtg = []
            for g in range(NG):
                for ti in range(4):
                    t = g * 4 + ti
                    pb = t % 2
                    if len(xtg) >= 2:
                        P.wait("sync", "dve", xtg[-2])
                    cD = P.dma("sync", "dS", xt[pb][:], R[ts(t), :])
                    P.wait("scalar", "dve", prevD["dve_end"])
                    P.wait("scalar", "dS", cD)
                    P.wait("vector", "dS", cD)
                    ca = ln_stats(xt[pb][:], True, EPS_LN, D)
                    P.wait("vector", "act", ca)
                    cln = P.dve("tensor_scalar", lnt[pb][:], xt[pb][:], cols[:, 0:1],
                                cols[:, 2:3], ALU.subtract, ALU.mult, inc=True)
                    xtg.append(cln)
                    P.wait("tensor", "dve", cln)
                    for c in range(8):
                        P.tr(PS_TR[:, ts(c)], lnt[pb][:, ts(c)], ident[:], inc=(c == 7))
                    cpe = P.cnt["pe"]
                    P.wait("vector", "pe", cpe)
                    P.wait("vector", "pe", prevD.get("xnT_pe", 0))
                    cxT = P.dve("tensor_copy",
                                bass.AP(xnT, ti * 128, [[4096, 128], [512, 8], [1, 128]]),
                                PS_TR[:].rearrange("p (c k) -> p c k", k=128), inc=True)
                    prevD["dve_end"] = P.cnt["dve"]
                cxnT = P.cnt["dve"]
                if g == 0 and _dbg == "dxnT":
                    dbg(xnT[:], 4096)
                last_mm = {0: 0, 1: 0}
                last_u = 0
                dA_pend = {}

                def w12_load(oc):
                    bi = oc % 2
                    if last_mm[bi]:
                        P.wait("scalar", "pe", last_mm[bi])
                    P.dma("scalar", "dA",
                          w1b[bi][:].rearrange("p (ic j) -> p ic j", j=128),
                          dI_bf["w1"][oc])
                    dA_pend[oc] = P.dma("scalar", "dA",
                                        w2b[bi][:].rearrange("p (ic j) -> p ic j", j=128),
                                        dI_bf["w2"][oc])

                w12_load(0)
                last_stt = {0: 0, 1: 0}
                for oc in range(32):
                    bi = oc % 2
                    zp = PS_A if bi == 0 else PS_B
                    if oc == 0:
                        P.wait("vector", "pe", prevD.get("w3_pe", 0))
                    if oc + 1 < 32:
                        w12_load(oc + 1)
                    P.wait("tensor", "dA", dA_pend[oc])
                    if oc == 0:
                        P.wait("tensor", "dve", cxnT)
                    if last_stt[bi]:
                        P.wait("tensor", "dve", last_stt[bi])
                    for ic in range(8):
                        P.mm(zp[:, 0:512], w1b[bi][:, ts(ic)],
                             xnT[:, ic * 512:(ic + 1) * 512],
                             start=(ic == 0), stop=(ic == 7))
                    for ic in range(8):
                        P.mm(zp[:, 512:1024], w2b[bi][:, ts(ic)],
                             xnT[:, ic * 512:(ic + 1) * 512],
                             start=(ic == 0), stop=(ic == 7), inc=(ic == 7))
                    cz = P.cnt["pe"]
                    last_mm[bi] = cz
                    P.wait("scalar", "pe", cz)
                    sb_h = silb[:, bi * 512: bi * 512 + 512]
                    csil = P.act(sb_h, zp[:, 0:512], AF.Silu, scale=s1c[:, oc:oc + 1])
                    P.wait("vector", "act", csil)
                    last_stt[bi] = P.dve_sp("scalar_tensor_tensor",
                                            uT[:, oc * 512:(oc + 1) * 512],
                                            zp[:, 512:1024], s2c[:, oc:oc + 1], sb_h,
                                            ALU.mult, ALU.mult)
                    last_u = last_stt[bi]
                cuT = last_u
                if g == 0 and _dbg == "duT":
                    dbg(uT[:, 0:2048], 2048)
                prevD["xnT_pe"] = last_mm[0] if last_mm[0] > last_mm[1] else last_mm[1]
                last_mm = {0: 0, 1: 0}
                lact = {0: 0, 1: 0}
                last_act = 0
                dA_pend = {}

                def h1_load(oc):
                    bi = oc % 2
                    if last_mm[bi]:
                        P.wait("scalar", "pe", last_mm[bi])
                    dA_pend[oc] = P.dma("scalar", "dA",
                                        hb[bi][:].rearrange("p (ic j) -> p ic j", j=128),
                                        dI_bf["h1"][oc])

                h1_load(0)
                for oc in range(32):
                    bi = oc % 2
                    if oc + 1 < 32:
                        h1_load(oc + 1)
                    P.wait("tensor", "dA", dA_pend[oc])
                    if oc == 0:
                        P.wait("tensor", "dve", cuT)
                    if lact[bi]:
                        P.wait("tensor", "act", lact[bi])
                    bank = PS_B[:, (oc % 2) * 512:(oc % 2) * 512 + 512]
                    for ic in range(32):
                        P.mm(bank, hb[bi][:, ts(ic)], uT[:, ic * 512:(ic + 1) * 512],
                             start=(ic == 0), stop=(ic == 31), inc=(ic == 31))
                    ch = P.cnt["pe"]
                    last_mm[bi] = ch
                    P.wait("scalar", "pe", ch)
                    last_act = P.act(vT[:, oc * 512:(oc + 1) * 512], bank, AF.Gelu,
                                     scale=sh1c[:, oc:oc + 1])
                    lact[bi] = last_act
                cvT = last_act
                if g == 0 and _dbg == "dvT":
                    P.wait("vector", "act", cvT)
                    dbg(vT[:, 0:2048], 2048)
                last_mm = {0: 0, 1: 0}
                lact = {0: 0, 1: 0}
                dA_pend = {}

                def h2_load(oc):
                    bi = oc % 2
                    if last_mm[bi]:
                        P.wait("scalar", "pe", last_mm[bi])
                    dA_pend[oc] = P.dma("scalar", "dA",
                                        hb[bi][:].rearrange("p (ic j) -> p ic j", j=128),
                                        dI_bf["h2"][oc])

                h2_load(0)
                for oc in range(32):
                    bi = oc % 2
                    if oc + 1 < 32:
                        h2_load(oc + 1)
                    P.wait("tensor", "dA", dA_pend[oc])
                    if oc == 0:
                        P.wait("tensor", "act", cvT)
                    if lact[bi]:
                        P.wait("tensor", "act", lact[bi])
                    bank = PS_B[:, (oc % 2) * 512:(oc % 2) * 512 + 512]
                    for ic in range(32):
                        P.mm(bank, hb[bi][:, ts(ic)], vT[:, ic * 512:(ic + 1) * 512],
                             start=(ic == 0), stop=(ic == 31), inc=(ic == 31))
                    ch = P.cnt["pe"]
                    last_mm[bi] = ch
                    P.wait("scalar", "pe", ch)
                    last_act = P.act(uT[:, oc * 512:(oc + 1) * 512], bank, AF.Gelu,
                                     scale=sh2c[:, oc:oc + 1])
                    lact[bi] = last_act
                cuT2 = last_act
                last_mm = {0: 0, 1: 0}
                lact = {0: 0, 1: 0}
                dA_pend = {}

                def w3_load(oc):
                    bi = oc % 2
                    if last_mm[bi]:
                        P.wait("scalar", "pe", last_mm[bi])
                    dA_pend[oc] = P.dma("scalar", "dA",
                                        w3b[bi][:].rearrange("p (ic j) -> p ic j", j=128),
                                        dI_bf["w3"][oc])

                w3_load(0)
                for oc in range(8):
                    bi = oc % 2
                    if oc + 1 < 8:
                        w3_load(oc + 1)
                    P.wait("tensor", "dA", dA_pend[oc])
                    if oc == 0:
                        P.wait("tensor", "act", cuT2)
                    if lact[bi]:
                        P.wait("tensor", "act", lact[bi])
                    bank = PS_B[:, (oc % 2) * 512:(oc % 2) * 512 + 512]
                    for ic in range(32):
                        P.mm(bank, w3b[bi][:, ts(ic)], uT[:, ic * 512:(ic + 1) * 512],
                             start=(ic == 0), stop=(ic == 31), inc=(ic == 31))
                    cf = P.cnt["pe"]
                    last_mm[bi] = cf
                    P.wait("scalar", "pe", cf)
                    last_act = P.act(fT[:, oc * 512:(oc + 1) * 512], bank, AF.Copy,
                                     scale=s3c[:, oc:oc + 1])
                    lact[bi] = last_act
                cfT = last_act
                prevD["w3_pe"] = cf
                if g == 0 and _dbg == "dfT":
                    P.wait("vector", "act", cfT)
                    dbg(fT[:], 4096)
                for ti in range(4):
                    t = g * 4 + ti
                    pb = t % 2
                    if len(st_hist_d) >= 2:
                        P.wait("sync", "st", st_hist_d[-2])
                    if len(xtg) >= 2:
                        P.wait("sync", "dve", xtg[-2])
                    cD = P.dma("sync", "dS", xt[pb][:], R[ts(t), :])
                    P.wait("tensor", "act", cfT)
                    P.wait("tensor", "dve", prevD["dve_end"])
                    for c in range(8):
                        P.tr(PS_TR[:, ts(c)], fT[:, c * 512 + ti * 128: c * 512 + ti * 128 + 128],
                             ident[:], inc=(c == 7))
                    cpf = P.cnt["pe"]
                    P.wait("vector", "pe", cpf)
                    P.wait("scalar", "dve", prevD["dve_end"])
                    cfm = P.dve("tensor_copy", lnt[pb][:], PS_TR[:], inc=True)
                    P.wait("scalar", "dve", cfm)
                    ca = ln_stats(lnt[pb][:], True, EPS_LN, D)
                    P.wait("vector", "act", ca)
                    P.wait("vector", "dS", cD)
                    if len(st_hist_d) >= 2:
                        P.wait("vector", "st", st_hist_d[-2])
                    P.dve("tensor_scalar", trash[:], lnt[pb][:], cols[:, 0:1],
                          cols[:, 2:3], ALU.subtract, ALU.mult)
                    cx4 = P.dve("tensor_add", xt[pb][:], xt[pb][:], trash[:], inc=True)
                    xtg.append(cx4)
                    P.wait("sync", "dve", cx4)
                    cst = P.dma("sync", "st", y_out[ts(t), :], xt[pb][:])
                    st_hist_d.append(cst)
                    prevD["dve_end"] = P.cnt["dve"]
            P.flush()
        if "D" not in PHASES:
            P.wait("sync", "st", P.cnt["st"])
            zsrc = R if ("A2" in PHASES or "B" in PHASES) else x_in
            if "C1" in PHASES and "C2" not in PHASES:
                zsrc = S2
            for t in range(NT):
                pb = t % 2
                cD = P.dma("sync", "dS", xt[pb][:], zsrc[ts(t), :])
                P.wait("sync", "dS", cD)
                P.dma("sync", "st", y_out[ts(t), :], xt[pb][:])
        P.wait("sync", "st", P.cnt["st"])
        P.flush()
        P.audit_done()

    return nc



def _rope_tables(base):
    inv = 1.0 / (float(base) ** (np.arange(0, 64, 2, dtype=np.float64) / 64.0))
    pos = np.arange(T, dtype=np.float64)
    ang = pos[:, None] * inv[None, :]
    cos = np.cos(ang); sin = np.sin(ang)
    C = np.empty((T, 64), np.float64); S = np.empty((T, 64), np.float64)
    C[:, 0::2] = cos; C[:, 1::2] = cos
    S[:, 0::2] = -sin; S[:, 1::2] = sin
    return C.astype(np.float32), S.astype(np.float32)


def kernel(**inp):
    i32 = np.ascontiguousarray
    base = {}
    base["ident"] = np.eye(128, dtype=BF)
    idx = np.arange(128)
    base["mask_c"] = (idx[None, :] >= idx[:, None]).astype(BF)
    base["mask_a"] = (idx[None, :] < idx[:, None]).astype(BF)
    base["qgc"] = i32(np.broadcast_to(np.tile(np.asarray(inp["L0_attn_q_g"], np.float32)[:32], 16), (128, 512)))
    base["kgc"] = i32(np.broadcast_to(np.tile(np.asarray(inp["L0_attn_k_g"], np.float32), 16), (128, 512)))
    Cg_, Sg_ = _rope_tables(1e6)
    Cl_, Sl_ = _rope_tables(1e4)
    base["ropeCg"], base["ropeSg"] = Cg_.astype(BF), Sg_.astype(BF)
    base["ropeCl"], base["ropeSl"] = Cl_.astype(BF), Sl_.astype(BF)
    base["l0_wq"] = i32(np.asarray(inp["L0_attn_Wq"], np.float32))
    base["l0_wkv"] = i32(np.asarray(inp["L0_attn_Wkv"], np.float32))
    base["l0_wk"] = i32(np.asarray(inp["L0_attn_Wk"], np.float32))
    base["l0_wv"] = i32(np.asarray(inp["L0_attn_Wv"], np.float32))
    wo = np.asarray(inp["L0_attn_Wo"], np.float32).reshape(16, 64, 1024)
    base["l0_wo"] = i32(wo[:, :32, :].reshape(512, 1024))
    base["l0f_wT"] = i32(np.asarray(inp["L0_ffn_W"], np.float32).T)
    base["l0f_b"] = i32(np.asarray(inp["L0_ffn_b"], np.float32).reshape(1, D))
    base["l0f_ldT"] = i32(np.asarray(inp["L0_ffn_lora_down"], np.float32).T)
    base["l0f_luT"] = i32(np.asarray(inp["L0_ffn_lora_up"], np.float32).T)
    base["l0f_da"] = i32(np.asarray(inp["L0_ffn_down_aux"], np.float32))
    base["l0f_uaT"] = i32(np.asarray(inp["L0_ffn_up_aux"], np.float32).T)
    base["l1_n1"] = i32(np.asarray(inp["L1_attn_norm1_w"], np.float32).reshape(8, 128).T)
    base["l1_n2"] = i32(np.asarray(inp["L1_attn_norm2_w"], np.float32).reshape(8, 128).T)
    for k, nm in (("l1_gwq", "L1_attn_g_wq"), ("l1_gkvd", "L1_attn_g_kv_down"),
                  ("l1_gkup", "L1_attn_g_k_up"), ("l1_gvup", "L1_attn_g_v_up"),
                  ("l1_gwo", "L1_attn_g_wo"), ("l1_lwq", "L1_attn_l_wq"),
                  ("l1_lwk", "L1_attn_l_wk"), ("l1_lwv", "L1_attn_l_wv"),
                  ("l1_lwo", "L1_attn_l_wo")):
        base[k] = i32(np.asarray(inp[nm], np.float32))

    def tile_oc(wT, ocs):
        K = wT.shape[0]
        return i32(wT.reshape(K, ocs, 128).transpose(1, 0, 2))

    base["l1f_w1"] = tile_oc(np.asarray(inp["L1_ffn_w1_w"], np.float32).T, 32)
    base["l1f_w2"] = tile_oc(np.asarray(inp["L1_ffn_w2_w"], np.float32).T, 32)
    base["l1f_h1"] = tile_oc(np.asarray(inp["L1_ffn_h1_w"], np.float32).T, 32)
    base["l1f_h2"] = tile_oc(np.asarray(inp["L1_ffn_h2_w"], np.float32).T, 32)
    base["l1f_w3"] = tile_oc(np.asarray(inp["L1_ffn_w3_w"], np.float32).T, 8)
    base["l1f_s1"] = i32(np.asarray(inp["L1_ffn_w1_s"], np.float32).reshape(32, 128).T)
    base["l1f_s2"] = i32(np.asarray(inp["L1_ffn_w2_s"], np.float32).reshape(32, 128).T)
    base["l1f_sh1"] = i32(np.asarray(inp["L1_ffn_h1_s"], np.float32).reshape(32, 128).T)
    base["l1f_sh2"] = i32(np.asarray(inp["L1_ffn_h2_s"], np.float32).reshape(32, 128).T)
    base["l1f_s3"] = i32(np.asarray(inp["L1_ffn_w3_s"], np.float32).reshape(8, 128).T)

    import os as _os
    if _os.environ.get("DEV_NO_BIGW"):
        for k in ("l1f_w1", "l1f_w2", "l1f_h1", "l1f_h2", "l1f_w3"):
            base.pop(k)
    x = np.asarray(inp["L0_attn_x"], np.float32)
    in_maps = []
    for c in range(8):
        m = dict(base)
        m["x"] = i32(x[c])
        in_maps.append(m)

    nc = build_program()
    ncores = int(_os.environ.get("DEV_CORES", "8"))
    res = run_bass_kernel_spmd(nc, in_maps[:ncores], core_ids=list(range(ncores)))
    out = [np.asarray(res.results[c]["y"], np.float32) for c in range(ncores)]
    if _os.environ.get("DEV_DBG"):
        np.save("/tmp/dbg.npy", np.asarray(res.results[0]["dbg"], np.float32))
    while len(out) < 8:
        out.append(out[-1])
    return np.stack(out)


# version 03
# speedup vs baseline: 1.5806x; 1.0484x over previous
import sys
import numpy as np
import ml_dtypes
import concourse.bass as bass
import concourse.mybir as mybir
from concourse.bass_utils import run_bass_kernel_spmd
from contextlib import ExitStack

F32 = mybir.dt.float32
BF16 = mybir.dt.bfloat16
AF = mybir.ActivationFunctionType
ALU = mybir.AluOpType
AX = mybir.AxisListType
BF = ml_dtypes.bfloat16

T = 4096
D = 1024
NT = T // 128
EPS_LN = 1e-5
EPS_RMS = 1e-6

ENGS = ("sync", "scalar", "vector", "tensor", "gpsimd")
PHASES = ("A1", "A2", "B", "C1", "C2", "D")


def ts(i, n=128):
    return slice(i * n, (i + 1) * n)


class Prog:

    def __init__(self, nc, stack):
        self.nc = nc
        self.ops = {e: [] for e in ENGS}
        self.sems = {}
        self.cnt = {}
        self.trace = {e: [] for e in ENGS}
        self.done = {}
        self.blocks = 0
        self.gtrace = []
        self.psums = {}
        self.bankof_cache = {}
        for s in ("dS", "dG", "dA", "pe", "act", "dve", "st"):
            self.sems[s] = stack.enter_context(nc.semaphore(s))
            self.cnt[s] = 0

    def reg_psum(self, handle, nbanks):
        base = sum(nb for _, nb in self.psums.values())
        self.psums[handle.name] = (base, nbanks)

    def banks(self, ap):
        try:
            nm = ap.tensor.name
        except Exception:
            return ()
        if nm not in self.psums:
            return ()
        base, nbanks = self.psums[nm]
        esz = 2 if "bfloat" in str(ap.dtype) else 4
        free = int(ap.tensor.shape[-1])
        off = int(ap.offset)
        col0 = off % free
        ext = 1
        for step, count in list(ap.ap):
            if int(step) < free and int(count) > 1:
                ext += int(step) * (int(count) - 1)
        b0 = (col0 * esz) // 2048
        b1 = ((col0 + ext - 1) * esz) // 2048
        return tuple(range(base + b0, base + min(b1, nbanks - 1) + 1))

    def raw(self, eng, fn):
        self.ops[eng].append(fn)

    _TL = {"sync": 0, "scalar": 1, "vector": 2, "tensor": 3, "gpsimd": 4}
    _QTL = {"sync": 5, "gpsimd": 6, "scalar": 7}

    def _vc_init(self):
        if not hasattr(self, "_vc"):
            self._vc = [ [0]*8 for _ in range(5) ]
            self._qv = [ [0]*8 for _ in range(3) ]
            self._tlpos = [0]*8
            self._semvc = {s: {} for s in self.cnt}
            self._eff = []
            self._nhaz = 0

    def note(self, eng, kind, sem=None, val=None, rd=(), wr=(), tag=""):
        self._vc_init()
        import traceback
        E = self._TL[eng]
        if kind == "wait":
            w = self._semvc[sem].get(val)
            if w is None:
                raise AssertionError(f"wait {sem}>={val} on {eng}: count never produced")
            c = self._vc[E]
            self._vc[E] = [max(a, b) for a, b in zip(c, w)]
            return
        vec = list(self._vc[E])
        self._tlpos[E] += 1
        vec[E] = self._tlpos[E]
        self._vc[E] = vec
        if kind in ("dma", "dma0"):
            qi = self._QTL[eng]
            q = qi - 5
            cvec = [max(a, b) for a, b in zip(vec, self._qv[q])]
            self._tlpos[qi] += 1
            cvec[qi] = self._tlpos[qi]
            self._qv[q] = cvec
            if sem is not None:
                self._semvc[sem][val] = cvec
            end_tl, end_idx, begin = qi, self._tlpos[qi], vec
        else:
            if sem is not None:
                self._semvc[sem][val] = vec
            end_tl, end_idx, begin = E, vec[E], vec
        if rd or wr:
            fr = sys._getframe(2)
            line = fr.f_lineno
            me = (end_tl, end_idx, begin, frozenset(rd) | frozenset(wr),
                  frozenset(wr), eng, tag, line)
            W = 800
            for prev in self._eff[-W:]:
                ptl, pidx, pbegin, pbanks, pwr, peng, ptag, pline = prev
                if not (me[3] & pbanks):
                    continue
                if not (pwr or me[4]):
                    if not ({peng, eng} == {"scalar", "vector"}):
                        continue
                if peng == eng and kind not in ("dma", "dma0"):
                    continue
                if ptl == end_tl:
                    continue
                if begin[ptl] >= pidx:
                    continue
                self._nhaz += 1
                if self._nhaz < 30:
                    print(f"PSUM-HAZARD: {ptag}@line{pline} ({peng}) banks{sorted(pbanks)} "
                          f"not ordered before {tag}@line{line} ({eng}) banks{sorted(me[3])}")
            self._eff.append(me)

    def audit_done(self):
        if getattr(self, "_nhaz", 0):
            raise AssertionError(f"{self._nhaz} PSUM hazards detected")

    def _inc(self, eng, sem, build, n):
        self.cnt[sem] += 1
        s = self.sems[sem]
        self.ops[eng].append(lambda e: build(e).then_inc(s, n))
        self.trace[eng].append(("op", sem, self.cnt[sem]))
        self.note(eng, "dma" if n == 16 else "op", sem, self.cnt[sem],
                  self._rd, self._wr, self._tag)
        return self.cnt[sem]

    def wait(self, eng, sem, val):
        if val <= 0:
            return
        s = self.sems[sem]
        mult = 16 if sem in ("dS", "dG", "dA", "st") else 1
        self.ops[eng].append(lambda e: e.wait_ge(s, mult * val))
        self.trace[eng].append(("wait", sem, val))
        self.note(eng, "wait", sem, val)

    def dma(self, queue, sem, out_ap, in_ap, inc=True):
        self._rd, self._wr, self._tag = self.banks(in_ap), self.banks(out_ap), "dma"
        if inc:
            return self._inc(queue, sem, lambda e: e.dma_start(out_ap, in_ap), 16)
        self.ops[queue].append(lambda e: e.dma_start(out_ap, in_ap))
        self.note(queue, "dma0", None, None, self._rd, self._wr, "dma0")
        return None

    def mm(self, out_ap, l_ap, r_ap, start, stop, inc=False, tp=None):
        if tp is None:
            b = lambda e: e.matmul(out_ap, l_ap, r_ap, start=start, stop=stop)
        else:
            b = lambda e: e.matmul(out_ap, l_ap, r_ap, start=start, stop=stop,
                                   tile_position=tp)
        self._rd, self._wr, self._tag = (), self.banks(out_ap), "mm"
        if inc:
            return self._inc("tensor", "pe", b, 1)
        self.ops["tensor"].append(b)
        self.note("tensor", "op", None, None, self._rd, self._wr, "mm")

    def tr(self, out_ap, in_ap, ident_ap, inc=False):
        b = lambda e: e.transpose(out_ap, in_ap, ident_ap)
        self._rd, self._wr, self._tag = (), self.banks(out_ap), "tr"
        if inc:
            return self._inc("tensor", "pe", b, 1)
        self.ops["tensor"].append(b)
        self.note("tensor", "op", None, None, self._rd, self._wr, "tr")

    def act(self, out_ap, in_ap, func, bias=0.0, scale=1.0, accum=None, inc=True):
        def b(e):
            kw = {}
            if accum is not None:
                kw["accum_out"] = accum
            return e.activation(out_ap, in_ap, func, bias=bias, scale=scale, **kw)
        self._rd, self._wr, self._tag = self.banks(in_ap), self.banks(out_ap), "act"
        if inc:
            return self._inc("scalar", "act", b, 1)
        self.ops["scalar"].append(b)
        self.note("scalar", "op", None, None, self._rd, self._wr, "act")

    def dve(self, name, *aps, inc=False, **kw):
        b = lambda e: getattr(e, name)(*aps, **kw)
        rd = []
        wr = list(self.banks(aps[0])) if aps and hasattr(aps[0], "tensor") else []
        for a in aps[1:]:
            if hasattr(a, "tensor"):
                rd += list(self.banks(a))
        self._rd, self._wr, self._tag = tuple(rd), tuple(wr), "dve:" + name
        if name.startswith("reduce") or name == "tensor_reduce":
            self.ops["vector"].append(b)
            self.note("vector", "op", None, None, self._rd, self._wr, "dve:" + name)
            so, si = self._spacer
            sb_ = lambda e: e.tensor_copy(so, si)
            self._rd, self._wr = (), ()
            if inc:
                return self._inc("vector", "dve", sb_, 1)
            self.ops["vector"].append(sb_)
            self.note("vector", "op", None, None, (), (), "dve:spacer")
            return None
        if inc:
            return self._inc("vector", "dve", b, 1)
        self.ops["vector"].append(b)
        self.note("vector", "op", None, None, self._rd, self._wr, "dve:" + name)

    def dve_sp(self, name, *aps, **kw):
        self.dve(name, *aps, inc=False, **kw)
        so, si = self._spacer
        return self.dve("tensor_copy", so, si, inc=True)

    def gpc(self, name, *aps, inc=False, **kw):
        b = lambda e: getattr(e, name)(*aps, **kw)
        if inc:
            return self._inc("gpsimd", "dG", b, 16)
        self.ops["gpsimd"].append(b)

    def check(self):
        tr = {e: list(self.trace[e]) for e in ENGS}
        pos = {e: 0 for e in ENGS}
        val = {s: self.done.get(s, 0) for s in self.cnt}
        tot = dict(val)
        for e in ENGS:
            for it in tr[e]:
                if it[0] == "op" and it[1] is not None:
                    tot[it[1]] = max(tot[it[1]], it[2])
        for e in ENGS:
            for it in tr[e]:
                if it[0] == "wait" and it[2] > tot[it[1]]:
                    raise AssertionError(f"wait {it} on {e} exceeds total {tot[it[1]]}")
        progressed = True
        while progressed:
            progressed = False
            for e in ENGS:
                while pos[e] < len(tr[e]):
                    it = tr[e][pos[e]]
                    if it[0] == "wait" and val[it[1]] < it[2]:
                        break
                    if it[0] == "op" and it[1] is not None:
                        val[it[1]] = max(val[it[1]], it[2])
                    pos[e] += 1
                    progressed = True
        stuck = {e: (pos[e], len(tr[e]), tr[e][pos[e]] if pos[e] < len(tr[e]) else None)
                 for e in ENGS if pos[e] < len(tr[e])}
        if stuck:
            raise AssertionError(f"DEADLOCK: {stuck} sem_vals={val}")
        self.done = val

    def flush(self):
        self.check()
        self._vc_init()
        bar = [max(c[i] for c in self._vc) for i in range(8)]
        for e in range(5):
            self._vc[e] = [max(a, b) for a, b in zip(self._vc[e], bar)]
        self.trace = {e: [] for e in ENGS}
        ops = self.ops
        self.ops = {e: [] for e in ENGS}
        with self.nc.Block() as blk:
            for eng in ENGS:
                lst = ops[eng]
                if lst:
                    def runner(e, lst=lst):
                        for f in lst:
                            f(e)
                    getattr(blk, eng)(runner)


def build_program():
    import os as _os
    nc = bass.Bass("TRN2", target_bir_lowering=False, debug=False)

    dI = {}
    def din(name, shape, dtype=F32):
        dI[name] = nc.dram_tensor(name, list(shape), dtype, kind="ExternalInput").ap()
        return dI[name]

    x_in = din("x", (T, D))
    y_out = nc.dram_tensor("y", [T, D], F32, kind="ExternalOutput").ap()
    bf_shapes = {"w1": (32, D, 128), "w2": (32, D, 128), "h1": (32, 4096, 128),
                 "h2": (32, 4096, 128), "w3": (8, 4096, 128)}
    bf_dram = {nm: nc.dram_tensor("bf_" + nm, list(sh), BF16, kind="Internal").ap()
               for nm, sh in bf_shapes.items()}
    dI_bf = {nm: [bf_dram[nm][oc].rearrange("(ic p) j -> p ic j", p=128)
                  for oc in range(bf_shapes[nm][0])] for nm in bf_shapes}
    dbg_out = nc.dram_tensor("dbg", [128, 4096], F32, kind="ExternalOutput").ap()
    _dbg = _os.environ.get("DEV_DBG", "")
    R = nc.dram_tensor("Rres", [T, D], F32, kind="Internal").ap()
    S2 = nc.dram_tensor("S2res", [T, D], F32, kind="Internal").ap()

    din("ident", (128, 128), BF16)
    din("mask_c", (128, 128), BF16)
    din("mask_a", (128, 128), BF16)
    din("qgc", (128, 512))
    din("kgc", (128, 512))
    din("ropeCg", (T, 64), BF16)
    din("ropeSg", (T, 64), BF16)
    din("ropeCl", (T, 64), BF16)
    din("ropeSl", (T, 64), BF16)
    din("l0_wq", (D, D))
    din("l0_wkv", (D, 128))
    din("l0_wk", (128, 512))
    din("l0_wv", (128, 512))
    din("l0_wo", (512, D))
    din("l0f_wT", (D, D))
    din("l0f_b", (1, D))
    din("l0f_ldT", (100, 4))
    din("l0f_luT", (4, 50))
    din("l0f_da", (100, D))
    din("l0f_uaT", (50, D))
    din("l1_n1", (128, 8))
    din("l1_n2", (128, 8))
    din("l1_gwq", (D, D))
    din("l1_gkvd", (D, 256))
    din("l1_gkup", (256, 256))
    din("l1_gvup", (256, 256))
    din("l1_gwo", (D, D))
    din("l1_lwq", (D, D))
    din("l1_lwk", (D, 256))
    din("l1_lwv", (D, 256))
    din("l1_lwo", (D, D))
    import os as _os
    if not _os.environ.get("DEV_NO_BIGW"):
        din("l1f_w1", (32, D, 128))
        din("l1f_w2", (32, D, 128))
        din("l1f_h1", (32, 4096, 128))
        din("l1f_h2", (32, 4096, 128))
        din("l1f_w3", (8, 4096, 128))
    din("l1f_s1", (128, 32))
    din("l1f_s2", (128, 32))
    din("l1f_sh1", (128, 32))
    din("l1f_sh2", (128, 32))
    din("l1f_s3", (128, 8))

    _ = _os
    st = ExitStack()
    with st:
        P = Prog(nc, st)

        def sb(name, shape, dtype=F32):
            return st.enter_context(nc.sbuf_tensor(name, list(shape), dtype))

        ident = sb("ident_s", (128, 128), BF16)
        mask_c = sb("mask_c_s", (128, 128), BF16)
        mask_a = sb("mask_a_s", (128, 128), BF16)
        ones_r = sb("ones_r", (1, 64))
        xt = [sb(f"xt{i}", (128, D)) for i in range(2)]
        x2t = [sb(f"x2t{i}", (128, D)) for i in range(2)]
        lnt = [sb(f"lnt{i}", (128, D), BF16) for i in range(2)]
        lnT = [sb(f"lnT{i}", (128, D), BF16) for i in range(2)]
        trash = sb("trash", (128, D))
        cols = sb("cols", (128, 64))
        colh = sb("colh", (128, 64))
        colh_ln = sb("colh_ln", (128, 16))
        colh_rs = sb("colh_rs", (128, 16))
        spacer = sb("spacer", (128, 16))
        P._spacer = (spacer[:, 0:8], spacer[:, 8:16])
        negC = sb("negC", (128, 1))
        zcol = sb("zcol", (128, 1))
        rsum = sb("rsum", (1, 512))
        bcast = sb("bcast", (64, 512))
        pbuf = [sb(f"pbuf{i}", (128, 1024), BF16) for i in range(2)]

        cvals = [EPS_LN, EPS_RMS, -0.5, 0.125]
        cstt = sb("cstt", (128, len(cvals)))
        if not _os.environ.get("DEV_NO_CST"):
            for i, v in enumerate(cvals):
                P.dve("memset", cstt[:, i:i + 1], float(v))
                nc.const_aps.register(F32, float(v), cstt[:, i:i + 1])

        PS_A = st.enter_context(nc.psum_tensor("PS_A", (128, 1024), F32))
        PS_B = st.enter_context(nc.psum_tensor("PS_B", (128, 1024), F32))
        PS_C = st.enter_context(nc.psum_tensor("PS_C", (128, 512), F32))
        PS_D = st.enter_context(nc.psum_tensor("PS_D", (128, 512), F32))
        PS_TR = st.enter_context(nc.psum_tensor("PS_TR", (128, 1024), BF16))
        PS_TR2 = st.enter_context(nc.psum_tensor("PS_TR2", (128, 1024), BF16))
        P.reg_psum(PS_A, 2); P.reg_psum(PS_B, 2); P.reg_psum(PS_C, 1)
        P.reg_psum(PS_D, 1); P.reg_psum(PS_TR, 1); P.reg_psum(PS_TR2, 1)

        dbgbuf = sb("dbgbuf", (128, 1024)) if _os.environ.get("DEV_DBG") else None

        def dbg(src_ap, width, part=128):
            for o in range(0, width, 1024):
                w = min(1024, width - o)
                cpy = P.dve("tensor_copy", dbgbuf[0:part, 0:w], src_ap[:, o:o + w], inc=True)
                P.wait("sync", "dve", cpy)
                cs_ = P.dma("sync", "st", dbg_out[0:part, o:o + w], dbgbuf[0:part, 0:w])
                P.wait("vector", "st", cs_)

        def ln_stats(src_ap, with_mean, eps, F, psum=False):
            sq = cols[:, 4:5]
            ca = P.act(trash[:, :F], src_ap, AF.Square, accum=sq)
            if with_mean:
                if psum:
                    ca = P.act(trash[:, :F], src_ap, AF.Copy, accum=cols[:, 0:1])
                else:
                    P.dve("reduce_sum", cols[:, 0:1], src_ap, axis=AX.X)
                P.wait("vector", "act", ca)
                P.dve("tensor_scalar_mul", cols[:, 0:1], cols[:, 0:1], 1.0 / F)
                P.dve("tensor_scalar_mul", cols[:, 1:2], sq, 1.0 / F)
                P.dve("tensor_mul", cols[:, 5:6], cols[:, 0:1], cols[:, 0:1])
                P.dve("tensor_sub", cols[:, 1:2], cols[:, 1:2], cols[:, 5:6])
                cv = P.dve("tensor_scalar_max", cols[:, 1:2], cols[:, 1:2], 0.0, inc=True)
            else:
                P.wait("vector", "act", ca)
                cv = P.dve("tensor_scalar_mul", cols[:, 1:2], sq, 1.0 / F, inc=True)
            P.wait("scalar", "dve", cv)
            P.act(cols[:, 6:7], cols[:, 1:2], AF.Ln, bias=float(eps), inc=False)
            return P.act(cols[:, 2:3], cols[:, 6:7], AF.Exp, scale=-0.5)

        def headnorm(src_ap, nh, hd, eps=EPS_RMS, tap=False):
            ca = P.act(trash[:, 0:nh * hd], src_ap, AF.Square)
            P.wait("vector", "act", ca)
            if tap:
                dbg(trash[:, 0:nh * hd], nh * hd)
            P.dve("reduce_sum", colh[:, 0:nh],
                  trash[:, 0:nh * hd].rearrange("p (h d) -> p h d", h=nh), axis=AX.X)
            cv = P.dve("tensor_scalar_mul", colh[:, 0:nh], colh[:, 0:nh], 1.0 / hd, inc=True)
            P.wait("scalar", "dve", cv)
            P.act(colh_ln[:, 0:nh], colh[:, 0:nh], AF.Ln, bias=float(eps), inc=False)
            return P.act(colh_rs[:, 0:nh], colh_ln[:, 0:nh], AF.Exp, scale=-0.5)

        with ExitStack() as stA:
            def sbA(name, shape, dtype=BF16):
                return stA.enter_context(nc.sbuf_tensor(name, list(shape), dtype))

            wq = sbA("wq", (128, 8 * 1024))
            wkv = sbA("wkv", (128, 8 * 128))
            wk = sbA("wk", (128, 512))
            wv = sbA("wv", (128, 512))
            wo8 = [sbA(f"wo8_{p}", (64, 1024)) for p in range(8)]
            qgc_s = sbA("qgc_s", (128, 512), F32)
            kgc_s = sbA("kgc_s", (128, 512), F32)
            qn = sbA("qn", (128, 512))
            kn = sbA("kn", (128, 512))
            qT = [sbA(f"qT{p}", (128, T)) for p in range(4)]
            kT = [sbA(f"kT{p}", (128, T)) for p in range(4)]
            VA = sbA("VA", (128, NT * 528))
            OT = [sbA(f"OT{p}", (64, 512)) for p in range(8)]
            kvb = sbA("kvb", (128, 128))
            kvT = sbA("kvT", (128, 128))

            _pre = int(_os.environ.get("DEV_PRE", "63"))
            cdc = 0
            if _pre & 1:
                P.dma("sync", "dS", ident[:], dI["ident"][:])
                P.dma("sync", "dS", mask_c[:], dI["mask_c"][:])
                P.dma("sync", "dS", mask_a[:], dI["mask_a"][:])
                P.dma("sync", "dS", qgc_s[:], dI["qgc"][:])
                cdc = P.dma("sync", "dS", kgc_s[:], dI["kgc"][:])
            if _pre & 16:
                P.dve("memset", ones_r[:], 1.0)
            if _pre & 2:
                for ic in range(8):
                    P.dma("gpsimd", "dG", wq[:, ic * 1024:(ic + 1) * 1024], dI["l0_wq"][ts(ic), :])
                    P.dma("gpsimd", "dG", wkv[:, ts(ic)], dI["l0_wkv"][ts(ic), :])
                P.dma("gpsimd", "dG", wk[:], dI["l0_wk"][:])
                P.dma("gpsimd", "dG", wv[:], dI["l0_wv"][:])
                for p in range(8):
                    P.dma("gpsimd", "dG", wo8[p][:], dI["l0_wo"][p * 64:(p + 1) * 64, :])
            dG_w = P.cnt["dG"]
            if "D" in PHASES:
                for nm in ("w1", "w2"):
                    for c in range(8):
                        P.dma("gpsimd", "dG", bf_dram[nm][c * 4:(c + 1) * 4],
                              dI["l1f_" + nm][c * 4:(c + 1) * 4])
                for c in range(8):
                    P.dma("gpsimd", "dG", bf_dram["w3"][c:c + 1],
                          dI["l1f_w3"][c:c + 1])
                for nm in ("h1", "h2"):
                    for c in range(16):
                        P.dma("gpsimd", "dG", bf_dram[nm][c * 2:(c + 1) * 2],
                              dI["l1f_" + nm][c * 2:(c + 1) * 2])

            if _pre & 4:
                P.wait("vector", "dS", cdc)
                P.dve("reduce_max", cols[:, 8:9], qgc_s[:], axis=AX.X, apply_absolute_value=True)
                P.dve("reduce_max", cols[:, 9:10], kgc_s[:], axis=AX.X, apply_absolute_value=True)
                P.dve("tensor_mul", cols[:, 8:9], cols[:, 8:9], cols[:, 9:10])
                P.dve("tensor_scalar_mul", negC[:], cols[:, 8:9], -5.6569)
            if _pre & 8:
                P.dve("memset", bass.AP(VA, 32, [[NT * 528, 128], [528, NT], [33, 16], [1, 1]]), 1.0)
            if _pre & 32:
                P.dve("memset", zcol[:], 0.0, inc=True)
            P.flush()

            prev = {"cln": 0, "dve_end": 0, "act_end": 0}
            cln_hist = []
            for t in (range(NT) if "A1" in PHASES else ()):
                pb = t % 2
                if t >= 2:
                    P.wait("sync", "dve", cln_hist[t - 2])
                cD = P.dma("sync", "dS", xt[pb][:], x_in[ts(t), :])
                P.wait("scalar", "dve", prev["dve_end"])
                P.wait("scalar", "dS", cD)
                P.wait("vector", "dS", cD)
                ca = ln_stats(xt[pb][:], True, EPS_LN, D)
                P.wait("vector", "act", ca)
                cln = P.dve("tensor_scalar", lnt[pb][:], xt[pb][:], cols[:, 0:1],
                            cols[:, 2:3], ALU.subtract, ALU.mult, inc=True)
                if t == 0 and _dbg == "ln1":
                    P.wait("vector", "dve", cln)
                    dbg(lnt[pb][:], D)
                P.wait("tensor", "dve", cln)
                P.wait("tensor", "dG", dG_w)
                for c in range(8):
                    P.tr(PS_TR[:, ts(c)], lnt[pb][:, ts(c)], ident[:], inc=(c == 7))
                cpe = P.cnt["pe"]
                P.wait("vector", "pe", cpe)
                clnT = P.dve("tensor_copy", lnT[pb][:], PS_TR[:], inc=True)
                P.wait("tensor", "dve", clnT)
                for nb in range(2):
                    for ic in range(8):
                        P.mm(PS_A[:, nb * 512:(nb + 1) * 512], lnT[pb][:, ts(ic)],
                             wq[:, ic * 1024 + nb * 512: ic * 1024 + (nb + 1) * 512],
                             start=(ic == 0), stop=(ic == 7),
                             inc=(nb == 1 and ic == 7))
                cQ = P.cnt["pe"]
                for ic in range(8):
                    P.mm(PS_B[:, 0:128], lnT[pb][:, ts(ic)], wkv[:, ts(ic)],
                         start=(ic == 0), stop=(ic == 7), inc=(ic == 7))
                cKV = P.cnt["pe"]
                if t == 0 and _dbg == "q0":
                    P.wait("vector", "pe", cQ)
                    dbg(PS_A[:], D)
                    P.wait("scalar", "dve", P.cnt["dve"])
                P.wait("scalar", "pe", cQ)
                cqs = headnorm(PS_A[:], 16, 64, tap=(t == 0 and _dbg == "qsq2"))
                if t == 0 and _dbg in ("qsq", "qrs"):
                    P.wait("vector", "act", cqs)
                    if _dbg == "qsq":
                        dbg(trash[:], D)
                    else:
                        dbg(colh[:, 0:16], 16)
                        P.dve("tensor_copy", dbgbuf[:, 16:32], colh[:, 16:32])
                        P.dve("tensor_copy", dbgbuf[:, 32:48], colh_rs[:, 0:16], inc=True)
                        P.dma("sync", "st", dbg_out[:, 16:48], dbgbuf[:, 16:48])
                    P.wait("scalar", "dve", P.cnt["dve"])
                P.wait("vector", "act", cqs)
                P.dve("tensor_tensor", trash[:, 0:512].rearrange("p (h d) -> p h d", h=16),
                      bass.AP(PS_A, 0, [[1024, 128], [64, 16], [1, 32]]),
                      colh_rs[:, 0:16].unsqueeze(2).broadcast_to([128, 16, 32]), ALU.mult)
                cqn = P.dve("tensor_mul", qn[:], trash[:, 0:512], qgc_s[:], inc=True)
                if t == 0 and _dbg == "qn":
                    dbg(qn[:], 512)
                P.wait("tensor", "dve", cqn)
                for c in range(4):
                    P.tr(PS_TR2[:, ts(c)], qn[:, ts(c)], ident[:], inc=(c == 3))
                cpq = P.cnt["pe"]
                P.wait("vector", "pe", cpq)
                for c in range(4):
                    P.dve("tensor_copy", qT[c][:, ts(t)], PS_TR2[:, ts(c)], inc=(c == 3))
                cqT = P.cnt["dve"]
                P.wait("vector", "pe", cKV)
                ckvb = P.dve("tensor_copy", kvb[:], PS_B[:, 0:128], inc=True)
                P.wait("tensor", "dve", ckvb)
                ckvTp = P.tr(PS_TR[:, 0:128], kvb[:], ident[:], inc=True)
                P.wait("vector", "pe", ckvTp)
                ckvT = P.dve("tensor_copy", kvT[:], PS_TR[:, 0:128], inc=True)
                P.wait("tensor", "dve", ckvT)
                cK = P.mm(PS_B[:, 512:1024], kvT[:], wk[:], start=True, stop=True, inc=True)
                P.wait("scalar", "pe", cK)
                cks = headnorm(PS_B[:, 512:1024], 16, 32)
                P.wait("vector", "act", cks)
                P.dve("tensor_tensor", trash[:, 0:512].rearrange("p (h d) -> p h d", h=16),
                      PS_B[:, 512:1024].rearrange("p (h d) -> p h d", h=16),
                      colh_rs[:, 0:16].unsqueeze(2).broadcast_to([128, 16, 32]), ALU.mult)
                ckn = P.dve("tensor_mul", kn[:], trash[:, 0:512], kgc_s[:], inc=True)
                if t == 0 and _dbg == "kn":
                    dbg(kn[:], 512)
                P.wait("tensor", "dve", ckn)
                for c in range(4):
                    P.tr(PS_TR2[:, ts(c)], kn[:, ts(c)], ident[:], inc=(c == 3))
                cpk = P.cnt["pe"]
                P.wait("vector", "pe", cpk)
                for c in range(4):
                    P.dve("tensor_copy", kT[c][:, ts(t)], PS_TR2[:, ts(c)], inc=(c == 3))
                P.wait("tensor", "dve", cqn)
                cV = P.mm(PS_A[:, 0:512], kvT[:], wv[:], start=True, stop=True, inc=True)
                P.wait("vector", "pe", cV)
                cVe = P.dve("tensor_copy",
                            bass.AP(VA, t * 528, [[NT * 528, 128], [33, 16], [1, 32]]),
                            PS_A[:, 0:512].rearrange("p (h d) -> p h d", h=16), inc=True)
                if t == 0 and _dbg == "va":
                    dbg(VA[:, 0:528], 528)
                if t == 0 and _dbg == "qT0":
                    dbg(qT[0][:, 0:512], 512)
                if t == 0 and _dbg == "kT0":
                    dbg(kT[0][:, 0:512], 512)
                cln_hist.append(cln)
                prev = {"cln": cln, "dve_end": P.cnt["dve"], "act_end": P.cnt["act"]}
            P.flush()

            A2L = int(_os.environ.get("DEV_A2", "99"))
            st_hist = []
            last_exp_set = {0: 0, 1: 0}
            for s in (range(8) if "A2" in PHASES else ()):
                q0 = s * 512
                def a2_denom(h, cpv):
                    oOT, oOff = OT[h // 2], 32 * (h % 2)
                    P.wait("vector", "pe", cpv)
                    crs = P.dve("reciprocal", rsum[:, 0:512], PS_C[32:33, 0:512], inc=True)
                    P.wait("tensor", "dve", crs)
                    cbc = P.mm(PS_D[0:32, 0:512], ones_r[:, 0:32], rsum[:, 0:512],
                               start=True, stop=True, inc=True)
                    P.wait("vector", "pe", cbc)
                    P.dve("tensor_copy", bcast[0:32, 0:512], PS_D[0:32, 0:512])
                    return P.dve_sp("tensor_mul", oOT[oOff:oOff + 32, 0:512],
                                    PS_C[0:32, 0:512], bcast[0:32, 0:512])

                pend_denom = None
                for h in range(16):
                    pT = h // 4
                    off = 32 * (h % 4)
                    tp = (96, 0) if off == 96 else None
                    kts = list(range(0, 4 * s + 4))
                    klast = kts[-1]
                    pairs = [kts[i:i + 2] for i in range(0, len(kts), 2)]

                    def a2_scores(gi):
                        pair = pairs[gi]
                        psb = PS_A if gi % 2 == 0 else PS_B
                        if last_exp_set[gi % 2]:
                            P.wait("tensor", "act", last_exp_set[gi % 2])
                        for j, kt in enumerate(pair):
                            qlo = max(0, kt * 128 - q0)
                            P.mm(psb[:, j * 512 + qlo:(j + 1) * 512],
                                 kT[pT][off:off + 32, ts(kt)],
                                 qT[pT][off:off + 32, q0 + qlo:q0 + 512],
                                 start=True, stop=True, inc=(j == len(pair) - 1), tp=tp)
                        return P.cnt["pe"]

                    cs_pend = a2_scores(0)
                    if pend_denom is not None:
                        cot_prev = a2_denom(*pend_denom)
                    for gi, pair in enumerate(pairs):
                        psb = PS_A if gi % 2 == 0 else PS_B
                        pbv = pbuf[gi % 2]
                        P.wait("scalar", "pe", cs_pend)
                        ce = P.act(pbv[:, 0:1024], psb[:, 0:1024], AF.Exp,
                                   bias=negC[:], scale=0.125)
                        last_exp_set[gi % 2] = ce
                        if gi + 1 < len(pairs):
                            cs_pend = a2_scores(gi + 1)
                        need_mask = [kt for kt in pair if q0 <= kt * 128 < q0 + 512]
                        cm = 0
                        if need_mask:
                            P.wait("vector", "act", ce)
                            for kt in need_mask:
                                j = pair.index(kt)
                                qrel = kt * 128 - q0
                                sl = slice(j * 512 + qrel, j * 512 + qrel + 128)
                                P.dve("tensor_mul", pbv[:, sl], pbv[:, sl],
                                      mask_c[:], inc=False)
                            cm = P.dve_sp("tensor_copy", P._spacer[0], P._spacer[1])
                            P.wait("tensor", "dve", cm)
                        else:
                            P.wait("tensor", "act", ce)
                        if gi == 0:
                            P.wait("tensor", "dve", cot_prev if pend_denom is not None
                                   else P.cnt["dve"] * 0)
                        for j, kt in enumerate(pair):
                            qlo = max(0, kt * 128 - q0)
                            P.mm(PS_C[0:33, qlo:512],
                                 bass.AP(VA, kt * 528 + h * 33, [[NT * 528, 128], [1, 33]]),
                                 pbv[:, j * 512 + qlo:(j + 1) * 512],
                                 start=(kt == 0), stop=(kt == klast),
                                 inc=(j == len(pair) - 1))
                        cpv = P.cnt["pe"]
                    pend_denom = (h, cpv)
                cot_last = a2_denom(*pend_denom)
                pend_denom = None
                P.wait("tensor", "dve", cot_last)
                for qt in range(4 if A2L >= 5 else 0):
                    for nb in range(2):
                        for p in range(8):
                            P.mm(PS_A[:, nb * 512:(nb + 1) * 512], OT[p][:, ts(qt)],
                                 wo8[p][:, nb * 512:(nb + 1) * 512],
                                 start=(p == 0), stop=(p == 7),
                                 inc=(p == 7 and nb == 1))
                    if s == 0 and qt == 0 and _dbg == "wo":
                        P.wait("vector", "pe", P.cnt["pe"])
                        dbg(PS_A[:], D)
                    cwo = P.cnt["pe"]
                    t_glob = s * 4 + qt
                    pb = t_glob % 2
                    if len(st_hist) >= 2:
                        P.wait("sync", "st", st_hist[-2])
                    cD = P.dma("sync", "dS", xt[pb][:], x_in[ts(t_glob), :])
                    if A2L >= 6:
                        P.wait("scalar", "pe", cwo)
                        P.wait("vector", "pe", cwo)
                        ca = ln_stats(PS_A[:], True, EPS_LN, D, psum=True)
                        P.wait("vector", "act", ca)
                        P.wait("vector", "dS", cD)
                        if len(st_hist) >= 2:
                            P.wait("vector", "st", st_hist[-2])
                        P.dve("tensor_scalar", trash[:], PS_A[:], cols[:, 0:1],
                              cols[:, 2:3], ALU.subtract, ALU.mult)
                        cx1 = P.dve("tensor_add", xt[pb][:], xt[pb][:], trash[:], inc=True)
                    else:
                        P.wait("vector", "pe", cwo)
                        P.wait("vector", "dS", cD)
                        if len(st_hist) >= 2:
                            P.wait("vector", "st", st_hist[-2])
                        P.dve("tensor_copy", trash[:], PS_A[:])
                        cx1 = P.dve("tensor_copy", xt[pb][:], xt[pb][:], inc=True)
                    P.wait("tensor", "dve", cx1)
                    P.wait("sync", "dve", cx1)
                    cst = P.dma("sync", "st", R[ts(t_glob), :], xt[pb][:])
                    st_hist.append(cst)
            if "A2" in PHASES and A2L < 5:
                for t_glob in range(NT):
                    pb = t_glob % 2
                    if len(st_hist) >= 2:
                        P.wait("sync", "st", st_hist[-2])
                    cD = P.dma("sync", "dS", xt[pb][:], x_in[ts(t_glob), :])
                    P.wait("sync", "dS", cD)
                    cst = P.dma("sync", "st", R[ts(t_glob), :], xt[pb][:])
                    st_hist.append(cst)
            P.flush()


        if "B" in PHASES:
          with ExitStack() as stB:
            def sbB(name, shape, dtype=BF16):
                return stB.enter_context(nc.sbuf_tensor(name, list(shape), dtype))
            wc = sbB("wc", (128, 8 * 1024))
            ldT_b = sbB("ldT_b", (100, 4))
            luT_b = sbB("luT_b", (4, 50))
            da_b = sbB("da_b", (100, 1024))
            uaT_b = sbB("uaT_b", (50, 1024))
            A_sb = sbB("A_sb", (4, 1024))
            B_sb = sbB("B_sb", (50, 1024))
            b_sb = sbB("b_sb", (1, 1024), F32)
            ones1 = sbB("ones1", (1, 128), F32)
            wstg = [sbB(f"bstg{i}", (128, 1024), F32) for i in range(2)]

            P.dve("memset", ones1[:], 1.0)
            small = [(ldT_b, "l0f_ldT"), (luT_b, "l0f_luT"),
                     (da_b, "l0f_da"), (uaT_b, "l0f_uaT")]
            for dst, nm in small:
                cL = P.dma("sync", "dS", wstg[0][0:dst.shape[0], 0:dst.shape[1]], dI[nm][:])
                P.wait("vector", "dS", cL)
                P.dve("tensor_copy", dst[:], wstg[0][0:dst.shape[0], 0:dst.shape[1]], inc=True)
                P.wait("sync", "dve", P.cnt["dve"])
            cB1 = P.dma("sync", "dS", b_sb[:], dI["l0f_b"][:])
            lastc = [0, 0]
            for ic in range(8):
                if lastc[ic % 2]:
                    P.wait("sync", "dve", lastc[ic % 2])
                cL = P.dma("sync", "dS", wstg[ic % 2][:], dI["l0f_wT"][ts(ic), :])
                P.wait("vector", "dS", cL)
                lastc[ic % 2] = P.dve("tensor_copy", wc[:, ic * 1024:(ic + 1) * 1024],
                                      wstg[ic % 2][:], inc=True)
            cwcast = P.cnt["dve"]
            P.wait("tensor", "dve", cwcast)
            P.mm(PS_A[0:4, 0:512], ldT_b[:], da_b[:, 0:512], start=True, stop=True)
            cA = P.mm(PS_A[0:4, 512:1024], ldT_b[:], da_b[:, 512:1024], start=True,
                      stop=True, inc=True)
            P.wait("vector", "pe", cA)
            cAe = P.dve("tensor_copy", A_sb[:], PS_A[0:4, 0:1024], inc=True)
            P.wait("tensor", "dve", cAe)
            P.mm(PS_B[0:50, 0:512], luT_b[:], A_sb[:, 0:512], start=True, stop=True)
            cBm = P.mm(PS_B[0:50, 512:1024], luT_b[:], A_sb[:, 512:1024], start=True,
                       stop=True, inc=True)
            P.wait("vector", "pe", cBm)
            cBe = P.dve("tensor_copy", B_sb[:], PS_B[0:50, 0:1024], inc=True)
            P.wait("tensor", "dve", cBe)
            for rc in range(8):
                P.mm(PS_A[:, 0:512], B_sb[:, ts(rc)], uaT_b[:, 0:512],
                     start=True, stop=True)
                cL2 = P.mm(PS_A[:, 512:1024], B_sb[:, ts(rc)], uaT_b[:, 512:1024],
                           start=True, stop=True, inc=True)
                P.wait("vector", "pe", cL2)
                cadd = P.dve("tensor_add", wc[:, rc * 1024:(rc + 1) * 1024],
                             PS_A[:, 0:1024], wc[:, rc * 1024:(rc + 1) * 1024], inc=True)
                P.wait("tensor", "dve", cadd)
            P.flush()

            prevB = {"dve_end": 0}
            st_hist_b = []
            cln_hist_b = []
            for t in range(NT):
                pb = t % 2
                if t >= 2:
                    P.wait("sync", "dve", cln_hist_b[t - 2])
                    P.wait("sync", "st", st_hist_b[t - 2])
                cD = P.dma("sync", "dS", xt[pb][:], R[ts(t), :])
                P.wait("scalar", "dve", prevB["dve_end"])
                P.wait("scalar", "dS", cD)
                P.wait("vector", "dS", cD)
                ca = ln_stats(xt[pb][:], True, EPS_LN, D)
                P.wait("vector", "act", ca)
                cln = P.dve("tensor_scalar", lnt[pb][:], xt[pb][:], cols[:, 0:1],
                            cols[:, 2:3], ALU.subtract, ALU.mult, inc=True)
                P.wait("tensor", "dve", cln)
                for c in range(8):
                    P.tr(PS_TR[:, ts(c)], lnt[pb][:, ts(c)], ident[:], inc=(c == 7))
                cpe = P.cnt["pe"]
                P.wait("vector", "pe", cpe)
                clnT = P.dve("tensor_copy", lnT[pb][:], PS_TR[:], inc=True)
                P.wait("tensor", "dve", clnT)
                P.wait("tensor", "dS", cB1)
                for nb in range(2):
                    for ic in range(8):
                        P.mm(PS_A[:, nb * 512:(nb + 1) * 512], lnT[pb][:, ts(ic)],
                             wc[:, ic * 1024 + nb * 512: ic * 1024 + (nb + 1) * 512],
                             start=(ic == 0), stop=False)
                    P.mm(PS_A[:, nb * 512:(nb + 1) * 512], ones1[:],
                         b_sb[:, nb * 512:(nb + 1) * 512],
                         start=False, stop=True, inc=(nb == 1))
                cff = P.cnt["pe"]
                P.wait("scalar", "pe", cff)
                P.wait("vector", "pe", cff)
                ca2 = ln_stats(PS_A[:], True, EPS_LN, D, psum=True)
                P.wait("vector", "act", ca2)
                P.dve("tensor_scalar", trash[:], PS_A[:], cols[:, 0:1],
                      cols[:, 2:3], ALU.subtract, ALU.mult)
                cx2 = P.dve("tensor_add", xt[pb][:], xt[pb][:], trash[:], inc=True)
                P.wait("tensor", "dve", cx2)
                P.wait("sync", "dve", cx2)
                cst = P.dma("sync", "st", R[ts(t), :], xt[pb][:])
                st_hist_b.append(cst)
                cln_hist_b.append(cln)
                prevB = {"dve_end": P.cnt["dve"]}
            P.flush()

        def phase_C(glob):
          with ExitStack() as stC:
            pre = "g" if glob else "l"
            def sbC(name, shape, dtype=BF16):
                return stC.enter_context(nc.sbuf_tensor(pre + name, list(shape), dtype))
            wq_s = sbC("wq", (128, 8 * 1024))
            wo_s = sbC("wo", (128, 8 * 1024))
            if glob:
                kvd_s = sbC("kvd", (128, 8 * 256))
                kup_s = sbC("kup", (128, 2 * 256))
                vup_s = sbC("vup", (128, 2 * 256))
            else:
                wk_s = sbC("wk", (128, 8 * 256))
                wv_s = sbC("wv", (128, 8 * 256))
            rCt = sbC("rC", (128, NT * 64))
            rSt = sbC("rS", (128, NT * 64))
            qrbuf = sbC("qrbuf", (128, 1024), F32)
            qn1 = sbC("qn1", (128, 1024))
            kn1 = sbC("kn1", (128, 256))
            latb = sbC("latb", (128, 256))
            latT = sbC("latT", (128, 256))
            qTl = [sbC(f"qT{p}", (128, T)) for p in range(8)]
            kTl = [sbC(f"kT{p}", (128, T)) for p in range(2)]
            VAl = sbC("VA", (128, NT * 260))
            OTl = [sbC(f"OT{p}", (128, 512)) for p in range(8)]
            ncol = sbC("ncol", (128, 8), F32)
            stW = ExitStack()
            wstg = [stW.enter_context(nc.sbuf_tensor(pre + "stg0", [128, 1024], F32))] * 2

            P.dve("memset", bass.AP(VAl, 64, [[NT * 260, 128], [260, NT], [65, 4], [1, 1]]), 1.0)
            cnc = P.dma("sync", "dS", ncol[:], dI["l1_n1" if glob else "l1_n2"][:])
            tCd = dI["ropeCg" if glob else "ropeCl"]
            tSd = dI["ropeSg" if glob else "ropeSl"]
            P.dma("sync", "dS", rCt[:].rearrange("p (t j) -> p t j", j=64),
                  tCd.rearrange("(t p) j -> p t j", p=128))
            crt = P.dma("sync", "dS", rSt[:].rearrange("p (t j) -> p t j", j=64),
                        tSd.rearrange("(t p) j -> p t j", p=128))
            wqsrc = dI["l1_gwq" if glob else "l1_lwq"]
            wosrc = dI["l1_gwo" if glob else "l1_lwo"]
            loads = []
            for ic in range(8):
                loads.append((wq_s[:, ic * 1024:(ic + 1) * 1024], wqsrc[ts(ic), :], ic, 1024))
            if glob:
                for ic in range(8):
                    loads.append((kvd_s[:, ic * 256:(ic + 1) * 256],
                                  dI["l1_gkvd"][ts(ic), :], ic, 256))
                for c in range(2):
                    loads.append((kup_s[:, c * 256:(c + 1) * 256],
                                  dI["l1_gkup"][ts(c), :], None, 256))
                    loads.append((vup_s[:, c * 256:(c + 1) * 256],
                                  dI["l1_gvup"][ts(c), :], None, 256))
            else:
                for ic in range(8):
                    loads.append((wk_s[:, ic * 256:(ic + 1) * 256],
                                  dI["l1_lwk"][ts(ic), :], ic, 256))
                    loads.append((wv_s[:, ic * 256:(ic + 1) * 256],
                                  dI["l1_lwv"][ts(ic), :], ic, 256))
            for ic in range(8):
                loads.append((wo_s[:, ic * 1024:(ic + 1) * 1024], wosrc[ts(ic), :], None, 1024))
            P.wait("vector", "dS", cnc)
            lastc = 0
            for i, (dst, srcap, sc, w) in enumerate(loads):
                stg = wstg[0]
                if lastc:
                    P.wait("sync", "dve", lastc)
                cL = P.dma("sync", "dS", stg[:, 0:w], srcap)
                P.wait("vector", "dS", cL)
                if sc is not None:
                    lastc = P.dve("tensor_scalar_mul", dst, stg[:, 0:w],
                                  ncol[:, sc:sc + 1], inc=True)
                else:
                    lastc = P.dve("tensor_copy", dst, stg[:, 0:w], inc=True)
            P.flush()
            stW.close()

            resid_src = R if glob else S2
            qt_of = lambda h: (h % 4) + 4 * (h // 8)
            off_of = lambda h: 64 * ((h // 4) % 2)
            prevC = {"dve_end": 0}
            ld_hist = []
            s2_hist = []
            for t in range(NT):
                pb = t % 2
                if t >= 2:
                    P.wait("sync", "dve", ld_hist[t - 2])
                    P.wait("sync", "st", s2_hist[t - 2])
                    P.wait("vector", "st", s2_hist[t - 2])
                cD = P.dma("sync", "dS", xt[pb][:], resid_src[ts(t), :])
                P.wait("scalar", "dve", prevC["dve_end"])
                P.wait("scalar", "dS", cD)
                P.wait("vector", "dS", cD)
                if glob:
                    ca = ln_stats(xt[pb][:], True, EPS_LN, D)
                    P.wait("vector", "act", ca)
                    cT = P.dve("tensor_scalar", x2t[pb][:], xt[pb][:], cols[:, 0:1],
                               cols[:, 2:3], ALU.subtract, ALU.mult, inc=True)
                    P.wait("sync", "dve", cT)
                    cs2 = P.dma("sync", "st", S2[ts(t), :], x2t[pb][:])
                    base_ap = x2t[pb][:]
                else:
                    cs2 = P.cnt["st"]
                    base_ap = xt[pb][:]
                P.wait("scalar", "dve", P.cnt["dve"])
                ca2 = ln_stats(base_ap, False, EPS_RMS, D)
                P.wait("vector", "act", ca2)
                cln = P.dve("tensor_scalar_mul", lnt[pb][:], base_ap, cols[:, 2:3], inc=True)
                P.wait("tensor", "dve", cln)
                for c in range(8):
                    P.tr(PS_TR[:, ts(c)], lnt[pb][:, ts(c)], ident[:], inc=(c == 7))
                cpe = P.cnt["pe"]
                P.wait("vector", "pe", cpe)
                clnT = P.dve("tensor_copy", lnT[pb][:], PS_TR[:], inc=True)
                P.wait("tensor", "dve", clnT)
                for nb in range(2):
                    for ic in range(8):
                        P.mm(PS_A[:, nb * 512:(nb + 1) * 512], lnT[pb][:, ts(ic)],
                             wq_s[:, ic * 1024 + nb * 512: ic * 1024 + (nb + 1) * 512],
                             start=(ic == 0), stop=(ic == 7),
                             inc=(nb == 1 and ic == 7))
                cQ = P.cnt["pe"]
                if t == 0 and _dbg == "cq0" + pre:
                    P.wait("vector", "pe", cQ)
                    dbg(PS_A[:], 1024)
                    P.wait("tensor", "dve", P.cnt["dve"])
                if t == 0 and _dbg == "ctb" + pre:
                    P.wait("vector", "dS", crt)
                    P.dve("tensor_copy", dbgbuf[:, 0:64], rCt[:, 0:64])
                    cdd = P.dve("tensor_copy", dbgbuf[:, 64:128], rSt[:, 0:64], inc=True)
                    P.wait("sync", "dve", cdd)
                    P.dma("sync", "st", dbg_out[:, 0:128], dbgbuf[:, 0:128])
                Cb = rCt[:, t * 64:(t + 1) * 64].rearrange("p (k two) -> p k two", two=2)                     .unsqueeze(1).broadcast_to([128, 16, 32, 2])
                Sb = rSt[:, t * 64:(t + 1) * 64].rearrange("p (k two) -> p k two", two=2)                     .unsqueeze(1).broadcast_to([128, 16, 32, 2])
                P.wait("vector", "pe", cQ)
                P.wait("vector", "dS", crt)
                qv4 = bass.AP(PS_A, 0, [[1024, 128], [64, 16], [2, 32], [1, 2]])
                qsw = bass.AP(PS_A, 1, [[1024, 128], [64, 16], [2, 32], [-1, 2]])
                P.dve("tensor_tensor",
                      qrbuf[:].rearrange("p (h k two) -> p h k two", h=16, two=2),
                      qv4, Cb, ALU.mult)
                P.dve("tensor_tensor",
                      trash[:].rearrange("p (h k two) -> p h k two", h=16, two=2),
                      qsw, Sb, ALU.mult)
                cqr = P.dve("tensor_add", qrbuf[:], qrbuf[:], trash[:], inc=True)
                P.wait("scalar", "dve", cqr)
                cqs = headnorm(qrbuf[:], 16, 64)
                P.wait("vector", "act", cqs)
                capp = P.dve("tensor_tensor",
                             qn1[:].rearrange("p (h d) -> p h d", h=16),
                             qrbuf[:].rearrange("p (h d) -> p h d", h=16),
                             colh_rs[:, 0:16].unsqueeze(2).broadcast_to([128, 16, 64]),
                             ALU.mult, inc=True)
                if t == 0 and _dbg == "cqn" + pre:
                    dbg(qn1[:], 1024)
                P.wait("tensor", "dve", capp)
                for c in range(8):
                    P.tr(PS_TR2[:, ts(c)], qn1[:, ts(c)], ident[:], inc=(c == 7))
                cpq = P.cnt["pe"]
                P.wait("vector", "pe", cpq)
                for c in range(8):
                    h0, h1 = 2 * c, 2 * c + 1
                    P.dve("tensor_copy",
                          qTl[qt_of(h0)][off_of(h0):off_of(h0) + 64, ts(t)],
                          PS_TR2[0:64, ts(c)])
                    P.dve("tensor_copy",
                          qTl[qt_of(h1)][off_of(h1):off_of(h1) + 64, ts(t)],
                          PS_TR2[64:128, ts(c)], inc=(c == 7))
                cqT = P.cnt["dve"]
                if glob:
                    for ic in range(8):
                        P.mm(PS_B[:, 0:256], lnT[pb][:, ts(ic)],
                             kvd_s[:, ic * 256:(ic + 1) * 256],
                             start=(ic == 0), stop=(ic == 7), inc=(ic == 7))
                    cKV = P.cnt["pe"]
                    P.wait("vector", "pe", cKV)
                    clat = P.dve("tensor_copy", latb[:], PS_B[:, 0:256], inc=True)
                    P.wait("tensor", "dve", clat)
                    for c in range(2):
                        P.tr(PS_TR[:, c * 128:(c + 1) * 128], latb[:, ts(c)], ident[:],
                             inc=(c == 1))
                    cplat = P.cnt["pe"]
                    P.wait("vector", "pe", cplat)
                    clatT = P.dve("tensor_copy", latT[:], PS_TR[:, 0:256], inc=True)
                    P.wait("tensor", "dve", clatT)
                    for c in range(2):
                        P.mm(PS_B[:, 512:768], latT[:, ts(c)],
                             kup_s[:, c * 256:(c + 1) * 256],
                             start=(c == 0), stop=(c == 1), inc=(c == 1))
                    cK = P.cnt["pe"]
                else:
                    for ic in range(8):
                        P.mm(PS_B[:, 512:768], lnT[pb][:, ts(ic)],
                             wk_s[:, ic * 256:(ic + 1) * 256],
                             start=(ic == 0), stop=(ic == 7), inc=(ic == 7))
                    cK = P.cnt["pe"]
                Cb4 = rCt[:, t * 64:(t + 1) * 64].rearrange("p (k two) -> p k two", two=2)                     .unsqueeze(1).broadcast_to([128, 4, 32, 2])
                Sb4 = rSt[:, t * 64:(t + 1) * 64].rearrange("p (k two) -> p k two", two=2)                     .unsqueeze(1).broadcast_to([128, 4, 32, 2])
                P.wait("vector", "pe", cK)
                kv4 = bass.AP(PS_B, 512, [[1024, 128], [64, 4], [2, 32], [1, 2]])
                ksw = bass.AP(PS_B, 513, [[1024, 128], [64, 4], [2, 32], [-1, 2]])
                P.dve("tensor_tensor",
                      qrbuf[:, 0:256].rearrange("p (h k two) -> p h k two", h=4, two=2),
                      kv4, Cb4, ALU.mult)
                P.dve("tensor_tensor",
                      trash[:, 0:256].rearrange("p (h k two) -> p h k two", h=4, two=2),
                      ksw, Sb4, ALU.mult)
                ckr = P.dve("tensor_add", qrbuf[:, 0:256], qrbuf[:, 0:256],
                            trash[:, 0:256], inc=True)
                P.wait("scalar", "dve", ckr)
                cks = headnorm(qrbuf[:, 0:256], 4, 64)
                P.wait("vector", "act", cks)
                ckn = P.dve("tensor_tensor",
                            kn1[:].rearrange("p (h d) -> p h d", h=4),
                            qrbuf[:, 0:256].rearrange("p (h d) -> p h d", h=4),
                            colh_rs[:, 0:4].unsqueeze(2).broadcast_to([128, 4, 64]),
                            ALU.mult, inc=True)
                if t == 0 and _dbg == "ckn" + pre:
                    dbg(kn1[:], 256)
                P.wait("tensor", "dve", ckn)
                for c in range(2):
                    P.tr(PS_TR2[:, c * 128:(c + 1) * 128], kn1[:, ts(c)], ident[:],
                         inc=(c == 1))
                cpk = P.cnt["pe"]
                P.wait("vector", "pe", cpk)
                for c in range(2):
                    P.dve("tensor_copy", kTl[c][:, ts(t)], PS_TR2[:, c * 128:(c + 1) * 128],
                          inc=(c == 1))
                P.wait("tensor", "dve", cqr)
                if glob:
                    for c in range(2):
                        P.mm(PS_A[:, 0:256], latT[:, ts(c)],
                             vup_s[:, c * 256:(c + 1) * 256],
                             start=(c == 0), stop=(c == 1), inc=(c == 1))
                else:
                    for ic in range(8):
                        P.mm(PS_A[:, 0:256], lnT[pb][:, ts(ic)],
                             wv_s[:, ic * 256:(ic + 1) * 256],
                             start=(ic == 0), stop=(ic == 7), inc=(ic == 7))
                cV = P.cnt["pe"]
                P.wait("vector", "pe", cV)
                cVe = P.dve("tensor_copy",
                            bass.AP(VAl, t * 260, [[NT * 260, 128], [65, 4], [1, 64]]),
                            PS_A[:, 0:256].rearrange("p (h d) -> p h d", h=4), inc=True)
                if t == 0 and _dbg == "cva" + pre:
                    dbg(VAl[:, 0:260], 260)
                ld_hist.append(cln)
                s2_hist.append(cs2)
                prevC = {"dve_end": P.cnt["dve"]}
            P.flush()

            st_hist_c = []
            last_exp_set = {0: 0, 1: 0}
            P.wait("sync", "st", P.cnt["st"])
            for s in range(8):
                q0 = s * 512
                def c_denom(h, cpv):
                    P.wait("vector", "pe", cpv)
                    crs = P.dve("reciprocal", rsum[:, 0:512], PS_C[64:65, 0:512], inc=True)
                    P.wait("tensor", "dve", crs)
                    cbc = P.mm(PS_D[0:64, 0:512], ones_r[:, 0:64], rsum[:, 0:512],
                               start=True, stop=True, inc=True)
                    P.wait("vector", "pe", cbc)
                    P.dve("tensor_copy", bcast[0:64, 0:512], PS_D[0:64, 0:512])
                    return P.dve_sp("tensor_mul",
                                    OTl[h // 2][64 * (h % 2):64 * (h % 2) + 64, 0:512],
                                    PS_C[0:64, 0:512], bcast[0:64, 0:512])

                pend_denom = None
                for h in range(16):
                    kv = h // 4
                    kb = 64 * (kv % 2)
                    kten = kTl[kv // 2]
                    qten = qTl[qt_of(h)]
                    qb = off_of(h)
                    if glob:
                        kts = list(range(0, 4 * s + 4))
                    else:
                        kts = list(range(max(0, 4 * s - 2), 4 * s + 4))
                    klast = kts[-1]
                    kfirst = kts[0]
                    pairs = [kts[i:i + 2] for i in range(0, len(kts), 2)]

                    def c_scores(gi):
                        pair = pairs[gi]
                        psb = PS_A if gi % 2 == 0 else PS_B
                        if last_exp_set[gi % 2]:
                            P.wait("tensor", "act", last_exp_set[gi % 2])
                        for j, kt in enumerate(pair):
                            qlo = max(0, kt * 128 - q0)
                            qhi = 512 if glob else min(512, kt * 128 + 384 - q0)
                            P.mm(psb[:, j * 512 + qlo: j * 512 + qhi],
                                 kten[kb:kb + 64, ts(kt)],
                                 qten[qb:qb + 64, q0 + qlo:q0 + qhi],
                                 start=True, stop=True, inc=(j == len(pair) - 1))
                        return P.cnt["pe"]

                    cs_pend = c_scores(0)
                    if pend_denom is not None:
                        cot_prev = c_denom(*pend_denom)
                    for gi, pair in enumerate(pairs):
                        psb = PS_A if gi % 2 == 0 else PS_B
                        pbv = pbuf[gi % 2]
                        P.wait("scalar", "pe", cs_pend)
                        ce = P.act(pbv[:, 0:1024], psb[:, 0:1024], AF.Exp, scale=0.125)
                        last_exp_set[gi % 2] = ce
                        if gi + 1 < len(pairs):
                            cs_pend = c_scores(gi + 1)
                        cm = 0
                        masks = []
                        for kt in pair:
                            j = pair.index(kt)
                            dqrel = kt * 128 - q0
                            if 0 <= dqrel < 512:
                                masks.append((j, dqrel, mask_c))
                            if not glob:
                                aqrel = kt * 128 + 256 - q0
                                if 0 <= aqrel < 512 and aqrel < (kt * 128 + 384 - q0):
                                    masks.append((j, aqrel, mask_a))
                        if masks:
                            P.wait("vector", "act", ce)
                            for j, qrel, mk in masks:
                                sl = slice(j * 512 + qrel, j * 512 + qrel + 128)
                                P.dve("tensor_mul", pbv[:, sl], pbv[:, sl],
                                      mk[:], inc=False)
                            cm = P.dve_sp("tensor_copy", P._spacer[0], P._spacer[1])
                            P.wait("tensor", "dve", cm)
                        else:
                            P.wait("tensor", "act", ce)
                        if gi == 0 and pend_denom is not None:
                            P.wait("tensor", "dve", cot_prev)
                        for j, kt in enumerate(pair):
                            qlo = max(0, kt * 128 - q0)
                            qhi = 512 if glob else min(512, kt * 128 + 384 - q0)
                            P.mm(PS_C[0:65, qlo:qhi],
                                 bass.AP(VAl, kt * 260 + kv * 65, [[NT * 260, 128], [1, 65]]),
                                 pbv[:, j * 512 + qlo: j * 512 + qhi],
                                 start=(kt == kfirst), stop=(kt == klast),
                                 inc=(j == len(pair) - 1))
                        cpv = P.cnt["pe"]
                    pend_denom = (h, cpv)
                cot_last = c_denom(*pend_denom)
                pend_denom = None
                P.wait("tensor", "dve", cot_last)
                for qt in range(4):
                    for nb in range(2):
                        for p in range(8):
                            P.mm(PS_A[:, nb * 512:(nb + 1) * 512],
                                 OTl[p][:, ts(qt)],
                                 wo_s[:, p * 1024 + nb * 512: p * 1024 + (nb + 1) * 512],
                                 start=(p == 0), stop=(p == 7),
                                 inc=(p == 7 and nb == 1))
                    cwo = P.cnt["pe"]
                    t_glob = s * 4 + qt
                    pb = t_glob % 2
                    if len(st_hist_c) >= 2:
                        P.wait("sync", "st", st_hist_c[-2])
                    cDu = P.dma("sync", "dS", x2t[pb][:], S2[ts(t_glob), :])
                    if glob:
                        P.wait("vector", "pe", cwo)
                        P.wait("vector", "dS", cDu)
                        if len(st_hist_c) >= 2:
                            P.wait("vector", "st", st_hist_c[-2])
                        cu1 = P.dve("tensor_add", x2t[pb][:], x2t[pb][:], PS_A[:], inc=True)
                        P.wait("tensor", "dve", cu1)
                        P.wait("sync", "dve", cu1)
                        cst = P.dma("sync", "st", S2[ts(t_glob), :], x2t[pb][:])
                    else:
                        cDx = P.dma("sync", "dS", xt[pb][:], R[ts(t_glob), :])
                        P.wait("vector", "pe", cwo)
                        P.wait("vector", "dS", cDu)
                        if len(st_hist_c) >= 2:
                            P.wait("vector", "st", st_hist_c[-2])
                        cu2 = P.dve("tensor_add", x2t[pb][:], x2t[pb][:], PS_A[:], inc=True)
                        P.wait("scalar", "dve", cu2)
                        ca3 = ln_stats(x2t[pb][:], True, EPS_LN, D)
                        P.wait("vector", "act", ca3)
                        P.wait("vector", "dS", cDx)
                        P.dve("tensor_scalar", trash[:], x2t[pb][:], cols[:, 0:1],
                              cols[:, 2:3], ALU.subtract, ALU.mult)
                        cx3 = P.dve("tensor_add", xt[pb][:], xt[pb][:], trash[:], inc=True)
                        P.wait("tensor", "dve", cu2)
                        P.wait("sync", "dve", cx3)
                        cst = P.dma("sync", "st", R[ts(t_glob), :], xt[pb][:])
                    st_hist_c.append(cst)
            P.flush()

        if "C1" in PHASES:
            phase_C(True)
        if "C2" in PHASES:
            phase_C(False)

        if "D" in PHASES:
          with ExitStack() as stD:
            def sbD(name, shape, dtype=BF16):
                return stD.enter_context(nc.sbuf_tensor(name, list(shape), dtype))
            xnT = sbD("xnT", (128, 8 * 512))
            uT = sbD("uT", (128, 32 * 512))
            vT = sbD("vT", (128, 32 * 512))
            fT = sbD("fT", (128, 8 * 512))
            silb = sbD("silb", (128, 1024), F32)
            w1b = [sbD(f"w1b{i}", (128, 8 * 128)) for i in range(2)]
            w2b = [sbD(f"w2b{i}", (128, 8 * 128)) for i in range(2)]
            hb = [sbD(f"hb{i}", (128, 32 * 128)) for i in range(3)]
            w3b = [sbD(f"w3b{i}", (128, 32 * 128)) for i in range(2)]
            scl = sbD("scl", (128, 32 * 4 + 8), F32)
            s1c = scl[:, 0:32]; s2c = scl[:, 32:64]
            sh1c = scl[:, 64:96]; sh2c = scl[:, 96:128]; s3c = scl[:, 128:136]
            P.dma("sync", "dS", s1c, dI["l1f_s1"][:])
            P.dma("sync", "dS", s2c, dI["l1f_s2"][:])
            P.dma("sync", "dS", sh1c, dI["l1f_sh1"][:])
            P.dma("sync", "dS", sh2c, dI["l1f_sh2"][:])
            cscl = P.dma("sync", "dS", s3c, dI["l1f_s3"][:])
            P.wait("scalar", "dS", cscl)
            P.wait("scalar", "dG", P.cnt["dG"])
            NG = 8
            prevD = {"dve_end": 0, "uT": 0, "vT": 0}
            st_hist_d = []
            xtg = []
            for g in range(NG):
                for ti in range(4):
                    t = g * 4 + ti
                    pb = t % 2
                    if len(xtg) >= 2:
                        P.wait("sync", "dve", xtg[-2])
                    cD = P.dma("sync", "dS", xt[pb][:], R[ts(t), :])
                    P.wait("scalar", "dve", prevD["dve_end"])
                    P.wait("scalar", "dS", cD)
                    P.wait("vector", "dS", cD)
                    ca = ln_stats(xt[pb][:], True, EPS_LN, D)
                    P.wait("vector", "act", ca)
                    cln = P.dve("tensor_scalar", lnt[pb][:], xt[pb][:], cols[:, 0:1],
                                cols[:, 2:3], ALU.subtract, ALU.mult, inc=True)
                    xtg.append(cln)
                    P.wait("tensor", "dve", cln)
                    for c in range(8):
                        P.tr(PS_TR[:, ts(c)], lnt[pb][:, ts(c)], ident[:], inc=(c == 7))
                    cpe = P.cnt["pe"]
                    P.wait("vector", "pe", cpe)
                    P.wait("vector", "pe", prevD.get("xnT_pe", 0))
                    cxT = P.dve("tensor_copy",
                                bass.AP(xnT, ti * 128, [[4096, 128], [512, 8], [1, 128]]),
                                PS_TR[:].rearrange("p (c k) -> p c k", k=128), inc=True)
                    prevD["dve_end"] = P.cnt["dve"]
                cxnT = P.cnt["dve"]
                if g == 0 and _dbg == "dxnT":
                    dbg(xnT[:], 4096)
                last_mm = {0: 0, 1: 0}
                last_u = 0
                dA_pend = {}

                def w12_load(oc):
                    bi = oc % 2
                    if last_mm[bi]:
                        P.wait("scalar", "pe", last_mm[bi])
                    P.dma("scalar", "dA",
                          w1b[bi][:].rearrange("p (ic j) -> p ic j", j=128),
                          dI_bf["w1"][oc])
                    dA_pend[oc] = P.dma("scalar", "dA",
                                        w2b[bi][:].rearrange("p (ic j) -> p ic j", j=128),
                                        dI_bf["w2"][oc])

                w12_load(0)
                last_stt = {0: 0, 1: 0}
                for oc in range(32):
                    bi = oc % 2
                    zp = PS_A if bi == 0 else PS_B
                    if oc == 0:
                        P.wait("vector", "pe", prevD.get("w3_pe", 0))
                    if oc + 1 < 32:
                        w12_load(oc + 1)
                    P.wait("tensor", "dA", dA_pend[oc])
                    if oc == 0:
                        P.wait("tensor", "dve", cxnT)
                    if last_stt[bi]:
                        P.wait("tensor", "dve", last_stt[bi])
                    for ic in range(8):
                        P.mm(zp[:, 0:512], w1b[bi][:, ts(ic)],
                             xnT[:, ic * 512:(ic + 1) * 512],
                             start=(ic == 0), stop=(ic == 7))
                    for ic in range(8):
                        P.mm(zp[:, 512:1024], w2b[bi][:, ts(ic)],
                             xnT[:, ic * 512:(ic + 1) * 512],
                             start=(ic == 0), stop=(ic == 7), inc=(ic == 7))
                    cz = P.cnt["pe"]
                    last_mm[bi] = cz
                    P.wait("scalar", "pe", cz)
                    sb_h = silb[:, bi * 512: bi * 512 + 512]
                    csil = P.act(sb_h, zp[:, 0:512], AF.Silu, scale=s1c[:, oc:oc + 1])
                    P.wait("vector", "act", csil)
                    last_stt[bi] = P.dve_sp("scalar_tensor_tensor",
                                            uT[:, oc * 512:(oc + 1) * 512],
                                            zp[:, 512:1024], s2c[:, oc:oc + 1], sb_h,
                                            ALU.mult, ALU.mult)
                    last_u = last_stt[bi]
                cuT = last_u
                if g == 0 and _dbg == "duT":
                    dbg(uT[:, 0:2048], 2048)
                prevD["xnT_pe"] = last_mm[0] if last_mm[0] > last_mm[1] else last_mm[1]
                last_mm = {0: 0, 1: 0}
                lact = {0: 0, 1: 0}
                last_act = 0
                dA_pend = {}

                def h1_load(oc):
                    bi = oc % 3
                    if last_mm.get(bi):
                        P.wait("scalar", "pe", last_mm[bi])
                    dA_pend[oc] = P.dma("scalar", "dA",
                                        hb[bi][:].rearrange("p (ic j) -> p ic j", j=128),
                                        dI_bf["h1"][oc])

                h1_load(0)
                h1_load(1)
                for oc in range(32):
                    bi = oc % 3
                    if oc + 2 < 32:
                        h1_load(oc + 2)
                    P.wait("tensor", "dA", dA_pend[oc])
                    if oc == 0:
                        P.wait("tensor", "dve", cuT)
                    if lact.get(oc % 2):
                        P.wait("tensor", "act", lact[oc % 2])
                    bank = PS_B[:, (oc % 2) * 512:(oc % 2) * 512 + 512]
                    for ic in range(32):
                        P.mm(bank, hb[bi][:, ts(ic)], uT[:, ic * 512:(ic + 1) * 512],
                             start=(ic == 0), stop=(ic == 31), inc=(ic == 31))
                    ch = P.cnt["pe"]
                    last_mm[bi] = ch
                    P.wait("scalar", "pe", ch)
                    last_act = P.act(vT[:, oc * 512:(oc + 1) * 512], bank, AF.Gelu,
                                     scale=sh1c[:, oc:oc + 1])
                    lact[oc % 2] = last_act
                cvT = last_act
                if g == 0 and _dbg == "dvT":
                    P.wait("vector", "act", cvT)
                    dbg(vT[:, 0:2048], 2048)
                last_mm = {0: 0, 1: 0}
                lact = {0: 0, 1: 0}
                dA_pend = {}

                def h2_load(oc):
                    bi = oc % 3
                    if last_mm.get(bi):
                        P.wait("scalar", "pe", last_mm[bi])
                    dA_pend[oc] = P.dma("scalar", "dA",
                                        hb[bi][:].rearrange("p (ic j) -> p ic j", j=128),
                                        dI_bf["h2"][oc])

                h2_load(0)
                h2_load(1)
                for oc in range(32):
                    bi = oc % 3
                    if oc + 2 < 32:
                        h2_load(oc + 2)
                    P.wait("tensor", "dA", dA_pend[oc])
                    if oc == 0:
                        P.wait("tensor", "act", cvT)
                    if lact.get(oc % 2):
                        P.wait("tensor", "act", lact[oc % 2])
                    bank = PS_B[:, (oc % 2) * 512:(oc % 2) * 512 + 512]
                    for ic in range(32):
                        P.mm(bank, hb[bi][:, ts(ic)], vT[:, ic * 512:(ic + 1) * 512],
                             start=(ic == 0), stop=(ic == 31), inc=(ic == 31))
                    ch = P.cnt["pe"]
                    last_mm[bi] = ch
                    P.wait("scalar", "pe", ch)
                    last_act = P.act(uT[:, oc * 512:(oc + 1) * 512], bank, AF.Gelu,
                                     scale=sh2c[:, oc:oc + 1])
                    lact[oc % 2] = last_act
                cuT2 = last_act
                last_mm = {0: 0, 1: 0}
                lact = {0: 0, 1: 0}
                dA_pend = {}

                def w3_load(oc):
                    bi = oc % 2
                    if last_mm[bi]:
                        P.wait("scalar", "pe", last_mm[bi])
                    dA_pend[oc] = P.dma("scalar", "dA",
                                        w3b[bi][:].rearrange("p (ic j) -> p ic j", j=128),
                                        dI_bf["w3"][oc])

                w3_load(0)
                for oc in range(8):
                    bi = oc % 2
                    if oc + 1 < 8:
                        w3_load(oc + 1)
                    P.wait("tensor", "dA", dA_pend[oc])
                    if oc == 0:
                        P.wait("tensor", "act", cuT2)
                    if lact[bi]:
                        P.wait("tensor", "act", lact[bi])
                    bank = PS_B[:, (oc % 2) * 512:(oc % 2) * 512 + 512]
                    for ic in range(32):
                        P.mm(bank, w3b[bi][:, ts(ic)], uT[:, ic * 512:(ic + 1) * 512],
                             start=(ic == 0), stop=(ic == 31), inc=(ic == 31))
                    cf = P.cnt["pe"]
                    last_mm[bi] = cf
                    P.wait("scalar", "pe", cf)
                    last_act = P.act(fT[:, oc * 512:(oc + 1) * 512], bank, AF.Copy,
                                     scale=s3c[:, oc:oc + 1])
                    lact[bi] = last_act
                cfT = last_act
                prevD["w3_pe"] = cf
                if g == 0 and _dbg == "dfT":
                    P.wait("vector", "act", cfT)
                    dbg(fT[:], 4096)
                for ti in range(4):
                    t = g * 4 + ti
                    pb = t % 2
                    if len(st_hist_d) >= 2:
                        P.wait("sync", "st", st_hist_d[-2])
                    if len(xtg) >= 2:
                        P.wait("sync", "dve", xtg[-2])
                    cD = P.dma("sync", "dS", xt[pb][:], R[ts(t), :])
                    P.wait("tensor", "act", cfT)
                    P.wait("tensor", "dve", prevD["dve_end"])
                    for c in range(8):
                        P.tr(PS_TR[:, ts(c)], fT[:, c * 512 + ti * 128: c * 512 + ti * 128 + 128],
                             ident[:], inc=(c == 7))
                    cpf = P.cnt["pe"]
                    P.wait("vector", "pe", cpf)
                    P.wait("scalar", "dve", prevD["dve_end"])
                    cfm = P.dve("tensor_copy", lnt[pb][:], PS_TR[:], inc=True)
                    P.wait("scalar", "dve", cfm)
                    ca = ln_stats(lnt[pb][:], True, EPS_LN, D)
                    P.wait("vector", "act", ca)
                    P.wait("vector", "dS", cD)
                    if len(st_hist_d) >= 2:
                        P.wait("vector", "st", st_hist_d[-2])
                    P.dve("tensor_scalar", trash[:], lnt[pb][:], cols[:, 0:1],
                          cols[:, 2:3], ALU.subtract, ALU.mult)
                    cx4 = P.dve("tensor_add", xt[pb][:], xt[pb][:], trash[:], inc=True)
                    xtg.append(cx4)
                    P.wait("sync", "dve", cx4)
                    cst = P.dma("sync", "st", y_out[ts(t), :], xt[pb][:])
                    st_hist_d.append(cst)
                    prevD["dve_end"] = P.cnt["dve"]
            P.flush()
        if "D" not in PHASES:
            P.wait("sync", "st", P.cnt["st"])
            zsrc = R if ("A2" in PHASES or "B" in PHASES) else x_in
            if "C1" in PHASES and "C2" not in PHASES:
                zsrc = S2
            for t in range(NT):
                pb = t % 2
                cD = P.dma("sync", "dS", xt[pb][:], zsrc[ts(t), :])
                P.wait("sync", "dS", cD)
                P.dma("sync", "st", y_out[ts(t), :], xt[pb][:])
        P.wait("sync", "st", P.cnt["st"])
        P.flush()
        P.audit_done()

    return nc



def _rope_tables(base):
    inv = 1.0 / (float(base) ** (np.arange(0, 64, 2, dtype=np.float64) / 64.0))
    pos = np.arange(T, dtype=np.float64)
    ang = pos[:, None] * inv[None, :]
    cos = np.cos(ang); sin = np.sin(ang)
    C = np.empty((T, 64), np.float64); S = np.empty((T, 64), np.float64)
    C[:, 0::2] = cos; C[:, 1::2] = cos
    S[:, 0::2] = -sin; S[:, 1::2] = sin
    return C.astype(np.float32), S.astype(np.float32)


def kernel(**inp):
    i32 = np.ascontiguousarray
    base = {}
    base["ident"] = np.eye(128, dtype=BF)
    idx = np.arange(128)
    base["mask_c"] = (idx[None, :] >= idx[:, None]).astype(BF)
    base["mask_a"] = (idx[None, :] < idx[:, None]).astype(BF)
    base["qgc"] = i32(np.broadcast_to(np.tile(np.asarray(inp["L0_attn_q_g"], np.float32)[:32], 16), (128, 512)))
    base["kgc"] = i32(np.broadcast_to(np.tile(np.asarray(inp["L0_attn_k_g"], np.float32), 16), (128, 512)))
    Cg_, Sg_ = _rope_tables(1e6)
    Cl_, Sl_ = _rope_tables(1e4)
    base["ropeCg"], base["ropeSg"] = Cg_.astype(BF), Sg_.astype(BF)
    base["ropeCl"], base["ropeSl"] = Cl_.astype(BF), Sl_.astype(BF)
    base["l0_wq"] = i32(np.asarray(inp["L0_attn_Wq"], np.float32))
    base["l0_wkv"] = i32(np.asarray(inp["L0_attn_Wkv"], np.float32))
    base["l0_wk"] = i32(np.asarray(inp["L0_attn_Wk"], np.float32))
    base["l0_wv"] = i32(np.asarray(inp["L0_attn_Wv"], np.float32))
    wo = np.asarray(inp["L0_attn_Wo"], np.float32).reshape(16, 64, 1024)
    base["l0_wo"] = i32(wo[:, :32, :].reshape(512, 1024))
    base["l0f_wT"] = i32(np.asarray(inp["L0_ffn_W"], np.float32).T)
    base["l0f_b"] = i32(np.asarray(inp["L0_ffn_b"], np.float32).reshape(1, D))
    base["l0f_ldT"] = i32(np.asarray(inp["L0_ffn_lora_down"], np.float32).T)
    base["l0f_luT"] = i32(np.asarray(inp["L0_ffn_lora_up"], np.float32).T)
    base["l0f_da"] = i32(np.asarray(inp["L0_ffn_down_aux"], np.float32))
    base["l0f_uaT"] = i32(np.asarray(inp["L0_ffn_up_aux"], np.float32).T)
    base["l1_n1"] = i32(np.asarray(inp["L1_attn_norm1_w"], np.float32).reshape(8, 128).T)
    base["l1_n2"] = i32(np.asarray(inp["L1_attn_norm2_w"], np.float32).reshape(8, 128).T)
    for k, nm in (("l1_gwq", "L1_attn_g_wq"), ("l1_gkvd", "L1_attn_g_kv_down"),
                  ("l1_gkup", "L1_attn_g_k_up"), ("l1_gvup", "L1_attn_g_v_up"),
                  ("l1_gwo", "L1_attn_g_wo"), ("l1_lwq", "L1_attn_l_wq"),
                  ("l1_lwk", "L1_attn_l_wk"), ("l1_lwv", "L1_attn_l_wv"),
                  ("l1_lwo", "L1_attn_l_wo")):
        base[k] = i32(np.asarray(inp[nm], np.float32))

    def tile_oc(wT, ocs):
        K = wT.shape[0]
        return i32(wT.reshape(K, ocs, 128).transpose(1, 0, 2))

    base["l1f_w1"] = tile_oc(np.asarray(inp["L1_ffn_w1_w"], np.float32).T, 32)
    base["l1f_w2"] = tile_oc(np.asarray(inp["L1_ffn_w2_w"], np.float32).T, 32)
    base["l1f_h1"] = tile_oc(np.asarray(inp["L1_ffn_h1_w"], np.float32).T, 32)
    base["l1f_h2"] = tile_oc(np.asarray(inp["L1_ffn_h2_w"], np.float32).T, 32)
    base["l1f_w3"] = tile_oc(np.asarray(inp["L1_ffn_w3_w"], np.float32).T, 8)
    base["l1f_s1"] = i32(np.asarray(inp["L1_ffn_w1_s"], np.float32).reshape(32, 128).T)
    base["l1f_s2"] = i32(np.asarray(inp["L1_ffn_w2_s"], np.float32).reshape(32, 128).T)
    base["l1f_sh1"] = i32(np.asarray(inp["L1_ffn_h1_s"], np.float32).reshape(32, 128).T)
    base["l1f_sh2"] = i32(np.asarray(inp["L1_ffn_h2_s"], np.float32).reshape(32, 128).T)
    base["l1f_s3"] = i32(np.asarray(inp["L1_ffn_w3_s"], np.float32).reshape(8, 128).T)

    import os as _os
    if _os.environ.get("DEV_NO_BIGW"):
        for k in ("l1f_w1", "l1f_w2", "l1f_h1", "l1f_h2", "l1f_w3"):
            base.pop(k)
    x = np.asarray(inp["L0_attn_x"], np.float32)
    in_maps = []
    for c in range(8):
        m = dict(base)
        m["x"] = i32(x[c])
        in_maps.append(m)

    nc = build_program()
    ncores = int(_os.environ.get("DEV_CORES", "8"))
    res = run_bass_kernel_spmd(nc, in_maps[:ncores], core_ids=list(range(ncores)))
    out = [np.asarray(res.results[c]["y"], np.float32) for c in range(ncores)]
    if _os.environ.get("DEV_DBG"):
        np.save("/tmp/dbg.npy", np.asarray(res.results[0]["dbg"], np.float32))
    while len(out) < 8:
        out.append(out[-1])
    return np.stack(out)
